# version 05
# baseline (speedup 1.0000x reference)
import contextlib
import numpy as np
import concourse.bass as bass
import concourse.mybir as mybir
from concourse.bass_utils import run_bass_kernel_spmd

F32 = mybir.dt.float32
BF16 = mybir.dt.bfloat16
AF = mybir.ActivationFunctionType
ALU = mybir.AluOpType
AX = mybir.AxisListType

STREAMS = ("pe", "dve", "act", "pool", "sp")

NCORES = 8
T = 2048
D = 1024
DI = 2048
NST = 64
CK = 64
SEC = 256
NSEC = T // SEC
NCHS = SEC // CK
B3 = SEC + 3
NH = 16
HD = 64
TP = T // 2
E = 8
FF = 4096
LOG_THR = float(np.log(np.float32(0.3) / np.float32(0.7)))
GELU_S = float(2.0 * 0.7978845608028654)
GELU_C = 0.044715
DBG = []


class Buf:
    __slots__ = ("name", "t", "lw", "readers")

    def __init__(self, name, t):
        self.name, self.t = name, t
        self.lw, self.readers = None, []

    @property
    def root(self):
        return self

    def __getitem__(self, idx):
        return self.t[idx]

    @property
    def ap(self):
        return self.t[:]


class SubBuf:
    __slots__ = ("name", "parent", "off", "width", "dtype")

    def __init__(self, name, parent, off, width, dtype):
        self.name, self.parent, self.off, self.width, self.dtype = name, parent, off, width, dtype

    def __getitem__(self, idx):
        return self.ap[idx]

    @property
    def root(self):
        return self.parent

    @property
    def ap(self):
        a = self.parent.t[:].bitcast(self.dtype)
        return a[:, self.off:self.off + self.width]


class Op:
    __slots__ = ("stream", "sidx", "fn", "dma", "coll", "inc_marked",
                 "milestone", "waits", "drain_before", "tag")

    def __init__(self, stream, sidx, fn, dma, coll, tag):
        self.stream, self.sidx, self.fn = stream, sidx, fn
        self.dma, self.coll = dma, coll
        self.inc_marked = dma or coll
        self.milestone = None
        self.waits = {}
        self.drain_before = False
        self.tag = tag


class Prog:
    def __init__(self, nc):
        self.nc = nc
        self.es = contextlib.ExitStack()
        self.sops = {s: [] for s in STREAMS}
        self.nops = 0
        self._names = set()
        self.csem = {s: self.es.enter_context(nc.semaphore("c_" + s)) for s in STREAMS}
        self.NQL = 8
        self.qsem = {s: [self.es.enter_context(nc.semaphore(f"q_{s}{i}")) for i in range(self.NQL)]
                     for s in ("sp", "act", "pool")}
        self.ccsem = self.es.enter_context(nc.semaphore("ccq"))
        self._drain_cover = {s: -1 for s in STREAMS}
        self._qcnt = {s: 0 for s in ("sp", "act", "pool")}
        self._cccnt = 0

    def _nm(self, name):
        assert name not in self._names, name
        self._names.add(name)
        return name

    def sbuf(self, name, shape, dtype):
        return Buf(name, self.es.enter_context(self.nc.sbuf_tensor(self._nm(name), list(shape), dtype)))

    def psum(self, name, shape, dtype):
        return Buf(name, self.es.enter_context(self.nc.psum_tensor(self._nm(name), list(shape), dtype)))

    def dram_in(self, name, shape, dtype):
        return Buf(name, self.nc.dram_tensor(self._nm(name), list(shape), dtype, kind="ExternalInput").ap())

    def dram_out(self, name, shape, dtype):
        return Buf(name, self.nc.dram_tensor(self._nm(name), list(shape), dtype, kind="ExternalOutput").ap())

    def dram_tmp(self, name, shape, dtype):
        return Buf(name, self.nc.dram_tensor(self._nm(name), list(shape), dtype).ap())

    def op(self, stream, fn, reads=(), writes=(), dma=False, coll=False, tag=""):
        o = Op(stream, len(self.sops[stream]), fn, dma, coll, tag)
        self.nops += 1
        self.sops[stream].append(o)
        if dma:
            qn = self._qcnt[stream]
            self._qcnt[stream] = qn + 1
            o.milestone = (f"q_{stream}{qn % self.NQL}", 16 * (qn // self.NQL + 1))
        elif coll:
            self._cccnt += 1
            o.milestone = ("cc", self._cccnt)
        deps = []
        for b in reads:
            b = b.root
            if b.lw is not None:
                deps.append(b.lw)
        wroots = [b.root for b in writes]
        for b in wroots:
            if b.lw is not None:
                deps.append(b.lw)
            deps.extend(b.readers)
        best = {}
        for d in deps:
            if d is o:
                continue
            kk = d.milestone[0] if (d.dma or d.coll) else (d.stream,)
            prev = best.get(kk)
            if prev is None or d.sidx > prev.sidx:
                best[kk] = d
        need_drain = False
        for d in best.values():
            if d.stream == stream and not d.dma and not d.coll:
                if stream == "pe":
                    continue
                if d.sidx > self._drain_cover[stream]:
                    need_drain = True
                continue
            d.inc_marked = True
            o.waits[d] = True
        if need_drain:
            o.drain_before = True
            self._drain_cover[stream] = o.sidx - 1
        for b in reads:
            b.root.readers.append(o)
        for b in wroots:
            b.lw = o
            b.readers = []
        return o

    def dma(self, stream, fn, reads=(), writes=(), tag=""):
        assert stream in ("sp", "act", "pool")
        return self.op(stream, fn, reads, writes, dma=True, tag=tag)

    def coll(self, fn, reads=(), writes=(), tag=""):
        return self.op("pool", fn, reads, writes, coll=True, tag=tag)

    def tt(self, st, out, a, b, alu, reads, writes, tag=""):
        self.op(st, lambda e: e.tensor_tensor(out, a, b, alu), reads, writes, tag=tag)

    def ts(self, st, out, a, s1, s2, op0, op1, reads, writes, tag=""):
        self.op(st, lambda e: e.tensor_scalar(out, a, s1, s2, op0, op1), reads, writes, tag=tag)

    def stt(self, st, out, a, s, b, op0, op1, reads, writes, tag=""):
        self.op(st, lambda e: e.scalar_tensor_tensor(out, a, s, b, op0, op1), reads, writes, tag=tag)

    def act(self, out, a, fn, reads, writes, scale=None, bias=None, accum=None, tag=""):
        kw = {}
        if scale is not None:
            kw["scale"] = scale
        if bias is not None:
            kw["bias"] = bias
        if accum is not None:
            kw["accum_out"] = accum
        self.op("act", lambda e: e.activation(out, a, fn, **kw), reads, writes, tag=tag)

    def copy(self, st, out, a, reads, writes, tag=""):
        self.op(st, lambda e: e.tensor_copy(out, a), reads, writes, tag=tag)

    def mm(self, out, lhsT, rhs, start, stop, reads, writes, tag=""):
        self.op("pe", lambda e: e.matmul(out, lhsT, rhs, start=start, stop=stop), reads, writes, tag=tag)

    def tr(self, out, in_, ident, reads, writes, tag=""):
        self.op("pe", lambda e: e.transpose(out, in_, ident), reads, writes, tag=tag)

    def finalize(self, debug=False):
        totals = {}
        qlane_tot = {}
        for s in STREAMS:
            cc = 0
            for o in self.sops[s]:
                if o.dma or o.coll or not o.inc_marked:
                    continue
                cc += 1
                o.milestone = ("c_" + s, cc)
            totals[s] = (self._qcnt.get(s, 0) * 16, self._cccnt if s == "pool" else 0, cc)
        for s in ("sp", "act", "pool"):
            qn = self._qcnt[s]
            qlane_tot[s] = [16 * ((qn - 1 - i) // self.NQL + 1) if i < min(qn, self.NQL) else 0
                            for i in range(self.NQL)]
        sem_of = {("c_" + s): self.csem[s] for s in STREAMS}
        for s in ("sp", "act", "pool"):
            for i in range(self.NQL):
                sem_of[f"q_{s}{i}"] = self.qsem[s][i]
        sem_of["cc"] = self.ccsem
        nwaits = [0]

        def emit(eng, s):
            hw = {}
            for o in self.sops[s]:
                if o.drain_before:
                    eng.drain()
                byname = {}
                for d in o.waits:
                    nm, v = d.milestone
                    if byname.get(nm, 0) < v:
                        byname[nm] = v
                for nm in sorted(byname):
                    v = byname[nm]
                    if hw.get(nm, 0) >= v:
                        continue
                    eng.wait_ge(sem_of[nm], v)
                    hw[nm] = v
                    nwaits[0] += 1
                inst = o.fn(eng)
                if o.inc_marked:
                    nm, _ = o.milestone
                    inst.then_inc(sem_of[nm], 16 if o.dma else 1)

        block = self.es.enter_context(self.nc.Block())

        @block.tensor
        def _(e):
            emit(e, "pe")

        @block.vector
        def _(e):
            emit(e, "dve")

        @block.scalar
        def _(e):
            emit(e, "act")

        @block.gpsimd
        def _(e):
            emit(e, "pool")

        @block.sync
        def _(e):
            emit(e, "sp")
            for s in ("sp", "act", "pool"):
                for i in range(self.NQL):
                    if qlane_tot[s][i]:
                        e.wait_ge(self.qsem[s][i], qlane_tot[s][i])
            if totals["pool"][1]:
                e.wait_ge(self.ccsem, totals["pool"][1])

        if debug:
            print(f"Prog: {self.nops} ops, {nwaits[0]} waits; totals={totals}")
        return totals


def ceil_div(a, b):
    return -(-a // b)


LNG, LNB = 0, 32
ALOGC, WDTC, DTBC, DSKC = 64, 80, 96, 112
TCBC, MURC, MUKC = 128, 136, 144
SELR, SELP, SELN = 152, 160, 168
CONVC = 176
RBC, THC = 240, 241
MB1C, MB2C = 256, 512
WCVC = 576
COLS_W = 640
KVW = 1040


class Kern:
    def __init__(self, nc, tap=None):
        self.nc = nc
        self.p = Prog(nc)
        self.tap = tap
        p = self.p
        s = p.sbuf

        self.i_x = p.dram_in("x_pad", [T + 3, D], F32)
        self.i_xhalo = p.dram_in("x_halo_fm", [D, 4], F32)
        self.i_win = p.dram_in("m_win", [D, 2 * DI], F32)
        self.i_wbc = p.dram_in("m_wbc", [DI, 2 * NST], F32)
        self.i_wout = p.dram_in("m_wout", [DI, D], F32)
        self.i_tcw = p.dram_in("tc_w", [D, D], F32)
        self.i_wq = p.dram_in("w_q", [D, D], F32)
        self.i_wk = p.dram_in("w_k", [D, D], F32)
        self.i_wv = p.dram_in("w_v", [D, D], F32)
        self.i_wo = p.dram_in("w_o", [D, D], F32)
        self.i_wg = p.dram_in("moe_wg", [D, E], F32)
        self.i_w1 = p.dram_in("moe_w1", [E * D, FF], F32)
        self.i_w2 = p.dram_in("moe_w2", [E * FF, D], F32)
        self.i_wr = p.dram_in("ffn_wr", [D, D], F32)
        self.i_wkf = p.dram_in("ffn_wk", [D, D], F32)
        self.i_wvf = p.dram_in("ffn_wv", [D, D], F32)
        self.i_rw = p.dram_in("router_w", [D, 4], F32)
        self.i_b2r = p.dram_in("moe_b2r", [8, 1024], F32)
        self.i_ohot = p.dram_in("ohot", [8, 1024], F32)
        self.i_cols = p.dram_in("cols", [128, COLS_W], F32)
        self.i_ident = p.dram_in("identity", [128, 128], F32)
        self.i_tri = p.dram_in("tri", [CK, CK], F32)
        self.out = p.dram_out("y", [T, D], F32)

        self.g_pool_in = p.dram_tmp("cc_pool_in", [128, 8], F32)
        self.g_pool_out = p.dram_tmp("cc_pool_out", [128, 8], F32)
        self.g_mst_in = p.dram_tmp("cc_mst_in", [DI, NST], F32)
        self.g_mp_in = p.dram_tmp("cc_mp_in", [128, 16], F32)
        self.g_mp_out = p.dram_tmp("cc_mp_out", [NCORES * 128, 16], F32)
        self.g_mst_out = p.dram_tmp("cc_mst_out", [NCORES * DI, NST], F32)
        self.g_kv_in = p.dram_tmp("cc_kv_in", [2 * D + 16, KVW], BF16)
        self.g_kv_out = p.dram_tmp("cc_kv_out", [NCORES * (2 * D + 16), KVW], BF16)
        self.g_yw_in = p.dram_tmp("cc_yw_in", [D, 4], F32)
        self.g_yw_out = p.dram_tmp("cc_yw_out", [NCORES * D, 4], F32)
        self.g_xp_in = p.dram_tmp("cc_xp_in", [128, 8], F32)
        self.g_xp_out = p.dram_tmp("cc_xp_out", [NCORES * 128, 8], F32)

        self.h = [s(f"h{c}", [128, T], F32) for c in range(8)]
        self.consts = s("consts", [128, COLS_W], F32)
        self.ident = s("identf", [128, 128], F32)
        self.identb = s("identb", [128, 128], BF16)
        self.onesb = s("onesb", [128, 128], BF16)
        self.onesf = s("onesf", [128, 256], F32)
        self.trib = s("trib", [CK, CK], F32)
        self.masks = s("masksrep", [128, 4], F32)
        self.rows = s("rows", [1, 1536], F32)

        _arw = [4144, 4096, 4096, 4096, 4160, 4096]
        self.ar = [s(f"ar{i}", [128, _arw[i]], F32) for i in range(6)]

        self.psA = p.psum("psA", [128, 1024], F32)
        self.psB = p.psum("psB", [128, 1024], F32)
        self.psC = p.psum("psC", [128, 512], F32)
        self.psD = p.psum("psD", [128, 512], F32)
        self.psT = p.psum("psT", [128, 2048], BF16)
        self._psi = 0

        self.ln_xb = s("ln_xb", [128, 512], BF16)
        self.ln_xb2 = s("ln_xb2", [128, 512], BF16)
        self.ln_sq = s("ln_sq", [128, 512], BF16)
        self.ln_sq2 = s("ln_sq2", [128, 512], BF16)
        self.ln_t1 = s("ln_t1", [128, 512], F32)

        self.wst = [s(f"wst{i}", [128, 2048], BF16) for i in range(2)]

        self.xh = s("xh", [128, 32], F32)
        self.mlnb = [s(f"mlnb{c}", [128, B3], BF16) for c in range(8)]
        self.mBb = s("mBb", [64, SEC], BF16)
        self.mCb = s("mCb", [64, SEC], BF16)
        self.mwtm = s("mwtm", [64, 2048], BF16)
        self.mhtm = s("mhtm", [64, 2048], BF16)
        self.mbtm = s("mbtm", [64, 64], BF16)
        self.mhbf = s("mhbf", [128, 16 * 64], BF16)
        self.mhst = s("mhst", [128, 16 * 64], F32)
        self.mqcol = s("mqcol", [128, 16], F32)
        self.mdtot = s("mdtot", [128, 16], F32)
        self.mec = s("mec", [128, 16 * NCHS], F32)
        self.mpc = s("mpc", [128, 16 * NCHS], F32)
        self.mmc = s("mmc", [128, 16 * NCHS], F32)
        self.mdh = s("mdh", [128, 16 * NCHS], F32)
        self.mdf = s("mdf", [128, 16 * NCHS], F32)
        self.mgm = s("mgm", [64, CK], BF16)
        self.pooled = s("pooled", [128, 8], F32)
        self.poolg = s("poolg", [128, 8], F32)
        self.rwsb = s("rwsb", [128, 32], F32)
        self.lg = s("lg", [4, 1], F32)
        self.mrow = s("mrow", [1, 4], F32)

    def sbuf(self, name, shape, dtype):
        return self.p.sbuf(name, shape, dtype)

    def next_ps(self):
        b = (self.psA, self.psB)[self._psi % 2]
        self._psi += 1
        return b

    def sub(self, name, ai, off, width, dtype):
        return SubBuf(name, self.ar[ai], off, width, dtype)

    def col(self, idx, n=1):
        return self.consts[:, idx:idx + n]

    def cbc(self, idx, nblk, w):
        return self.consts[:, idx:idx + nblk].unsqueeze(2).broadcast_to([128, nblk, w])


def blkv(buf, nblk, off=0, w=None):
    a = buf.ap
    q = a.shape[1] // nblk
    a = a.rearrange("p (b q) -> p b q", b=nblk)
    if w is None:
        return a
    return a[:, :, off:off + w]


def emit_setup(k):
    p = k.p
    p.dma("sp", lambda e: e.dma_start(out=k.consts.ap, in_=k.i_cols.ap), writes=[k.consts], tag="ld_cols")
    p.dma("sp", lambda e: e.dma_start(out=k.ident.ap, in_=k.i_ident.ap), writes=[k.ident])
    p.dma("pool", lambda e: e.dma_start(out=k.identb.ap, in_=k.i_ident.ap), writes=[k.identb])
    p.dma("sp", lambda e: e.dma_start(out=k.trib.ap, in_=k.i_tri.ap), writes=[k.trib])
    p.op("dve", lambda e: e.memset(k.onesf.ap, 1.0), writes=[k.onesf])
    p.op("dve", lambda e: e.memset(k.onesb.ap, 1.0), writes=[k.onesb])
    acol = k.consts[:, ALOGC:ALOGC + 16]
    p.act(acol, acol, AF.Exp, reads=[k.consts], writes=[k.consts], tag="A_exp")
    p.ts("dve", acol, acol, -1.0, None, ALU.mult, ALU.bypass, reads=[k.consts], writes=[k.consts], tag="A_neg")


def emit_input(k):
    p = k.p
    xtm = [k.sub(f"xtm{i}", i, 0, 1024, F32) for i in range(2)]
    for i in range(16):
        b = xtm[i % 2]
        src = k.i_x[3 + 128 * i: 3 + 128 * (i + 1), :]
        p.dma("sp", lambda e, b=b, src=src: e.dma_start(out=b.ap, in_=src), writes=[b], tag=f"ldx{i}")
        ps = (k.psA, k.psB)[i % 2]
        for c in range(8):
            p.tr(ps[:, 128 * c:128 * (c + 1)], b.ap[:, 128 * c:128 * (c + 1)], k.ident.ap,
                 reads=[b, k.ident], writes=[ps], tag=f"trx{i}_{c}")
        for c in range(8):
            p.copy("dve", k.h[c][:, 128 * i:128 * (i + 1)], ps[:, 128 * c:128 * (c + 1)],
                   reads=[ps], writes=[k.h[c]], tag=f"evx{i}_{c}")
    for c in range(8):
        p.op("dve", lambda e, c=c: e.tensor_reduce(k.pooled[:, c:c + 1], k.h[c].ap, AX.X, ALU.add),
             reads=[k.h[c]], writes=[k.pooled], tag=f"pool{c}")
    gi, go = k.g_pool_in, k.g_pool_out
    p.dma("sp", lambda e: e.dma_start(out=gi.ap, in_=k.pooled.ap), reads=[k.pooled], writes=[gi])
    p.coll(lambda e: e.collective_compute("AllReduce", ALU.add, replica_groups=[list(range(NCORES))],
                                          ins=[gi.ap.opt()], outs=[go.ap.opt()]),
           reads=[gi], writes=[go], tag="AR_pool")
    p.dma("sp", lambda e: e.dma_start(out=k.poolg.ap, in_=go.ap), reads=[go], writes=[k.poolg])
    p.ts("dve", k.poolg.ap, k.poolg.ap, 1.0 / (NCORES * T), None, ALU.mult, ALU.bypass,
         reads=[k.poolg], writes=[k.poolg], tag="poolmean")
    p.dma("sp", lambda e: e.dma_start(out=k.rwsb.ap.rearrange("p (kk f) -> p kk f", kk=8),
                                      in_=k.i_rw.ap.rearrange("(kk p) f -> p kk f", p=128)),
          writes=[k.rwsb])
    psr = k.psC
    for c in range(8):
        p.mm(psr[0:4, 0:1], k.rwsb[:, 4 * c:4 * c + 4], k.poolg[:, c:c + 1], c == 0, c == 7,
             reads=[k.rwsb, k.poolg], writes=[psr], tag=f"rlog{c}")
    p.tt("dve", k.lg.ap, psr[0:4, 0:1], k.consts[0:4, RBC:RBC + 1], ALU.add,
         reads=[psr, k.consts], writes=[k.lg])
    p.ts("dve", k.lg.ap, k.lg.ap, LOG_THR, None, ALU.is_gt, ALU.bypass, reads=[k.lg], writes=[k.lg])
    if "rt" in DBG:
        dp = p.dram_out("dbg_pooled", [128, 8], F32)
        p.dma("sp", lambda e: e.dma_start(out=dp.ap, in_=k.pooled.ap), reads=[k.pooled], writes=[dp])
        d5 = p.dram_out("dbg_lg", [4, 1], F32)
        d6 = p.dram_out("dbg_poolg", [128, 8], F32)
        p.dma("sp", lambda e: e.dma_start(out=d5.ap, in_=k.lg.ap), reads=[k.lg], writes=[d5])
        p.dma("sp", lambda e: e.dma_start(out=d6.ap, in_=k.poolg.ap), reads=[k.poolg], writes=[d6])
    p.tr(psr[0:1, 4:8], k.lg.ap, k.ident[0:4, 0:4], reads=[k.lg, k.ident], writes=[psr], tag="masktr")
    p.copy("dve", k.mrow.ap, psr[0:1, 4:8], reads=[psr], writes=[k.mrow])
    p.mm(psr[:, 8:12], k.onesf[0:1, 0:128], k.mrow.ap, True, True,
         reads=[k.onesf, k.mrow], writes=[psr], tag="maskbc")
    p.copy("dve", k.masks.ap, psr[:, 8:12], reads=[psr], writes=[k.masks])


def emit_ln(k, nm, src_aps, src_bufs, outs, ln_i, eps, width):
    p = k.p
    nch = ceil_div(width, 512)
    for ci in range(nch):
        a = ci * 512
        w = min(512, width - a)
        psS, psQ = k.psC, k.psD
        for c in range(8):
            sap = src_aps(c, a, w)
            rb = src_bufs(c)
            xb = (k.ln_xb, k.ln_xb2)[c % 2]
            sq = (k.ln_sq, k.ln_sq2)[c % 2]
            p.copy("dve", xb[:, 0:w], sap, reads=[rb], writes=[xb], tag=f"{nm}cv{c}")
            p.act(sq[:, 0:w], sap, AF.Square, reads=[rb], writes=[sq], tag=f"{nm}sq{c}")
            p.mm(psS[0:1, 0:w], k.onesb[:, 0:1], xb[:, 0:w], c == 0, c == 7,
                 reads=[k.onesb, xb], writes=[psS], tag=f"{nm}mS{c}")
            p.mm(psQ[0:1, 0:w], k.onesb[:, 0:1], sq[:, 0:w], c == 0, c == 7,
                 reads=[k.onesb, sq], writes=[psQ], tag=f"{nm}mQ{c}")
        m_r, v_r, nm_r = k.rows[0:1, 0:w], k.rows[0:1, 512:512 + w], k.rows[0:1, 1024:1024 + w]
        p.ts("dve", m_r, psS[0:1, 0:w], 1.0 / D, None, ALU.mult, ALU.bypass, reads=[psS], writes=[k.rows])
        p.tt("dve", v_r, m_r, m_r, ALU.mult, reads=[k.rows], writes=[k.rows])
        p.stt("dve", v_r, psQ[0:1, 0:w], 1.0 / D, v_r, ALU.mult, ALU.subtract,
              reads=[psQ, k.rows], writes=[k.rows])
        p.ts("dve", v_r, v_r, float(eps), None, ALU.add, ALU.bypass, reads=[k.rows], writes=[k.rows])
        p.act(v_r, v_r, AF.Ln, reads=[k.rows], writes=[k.rows])
        p.act(v_r, v_r, AF.Exp, scale=-0.5, reads=[k.rows], writes=[k.rows])
        p.tt("dve", nm_r, m_r, v_r, ALU.mult, reads=[k.rows], writes=[k.rows])
        p.ts("dve", nm_r, nm_r, -1.0, None, ALU.mult, ALU.bypass, reads=[k.rows], writes=[k.rows])
        psR, psB_ = k.psA, k.psB
        p.mm(psR[:, 0:w], k.onesf[0:1, 0:128], v_r, True, True, reads=[k.onesf, k.rows], writes=[psR], tag=f"{nm}bR")
        p.mm(psB_[:, 0:w], k.onesf[0:1, 0:128], nm_r, True, True, reads=[k.onesf, k.rows], writes=[psB_], tag=f"{nm}bB")
        for c in range(8):
            sap = src_aps(c, a, w)
            rb = src_bufs(c)
            tb = k.ln_t1
            p.tt("dve", tb[:, 0:w], sap, psR[:, 0:w], ALU.mult, reads=[rb, psR], writes=[tb], tag=f"{nm}a1_{c}")
            d0ap, d0buf = outs[0][0](c, a, w), outs[0][1](c)
            if ln_i is not None:
                p.tt("dve", tb[:, 0:w], tb[:, 0:w], psB_[:, 0:w], ALU.add, reads=[tb, psB_], writes=[tb])
                g = k.col(LNG + 8 * ln_i + c)
                bb = k.col(LNB + 8 * ln_i + c)
                p.ts("dve", d0ap, tb[:, 0:w], g, bb, ALU.mult, ALU.add,
                     reads=[tb, k.consts], writes=[d0buf], tag=f"{nm}a2_{c}")
            else:
                p.tt("dve", d0ap, tb[:, 0:w], psB_[:, 0:w], ALU.add, reads=[tb, psB_], writes=[d0buf], tag=f"{nm}a2_{c}")
            for oap, obuf in outs[1:]:
                p.copy("dve", oap(c, a, w), d0ap, reads=[d0buf], writes=[obuf(c)], tag=f"{nm}cp{c}")


def alloc_mamba(k):
    k.MB1 = k.sub("MB1", 0, 0, 16 * B3, F32)
    k.mxc = k.sub("mxc", 1, 0, 16 * SEC, F32)
    k.ma1 = k.sub("ma1", 2, 0, 16 * SEC, F32)
    k.ma4 = k.sub("ma4", 3, 0, 16 * SEC, F32)
    k.MB2 = k.sub("MB2", 4, 0, 16 * SEC, BF16)
    k.mwbf = k.sub("mwbf", 4, 16 * SEC, 16 * SEC, BF16)
    k.meqb = k.sub("meqb", 5, 0, 16 * SEC, BF16)
    k.MB3 = k.sub("MB3", 5, 16 * SEC, 16 * SEC, BF16)


def emit_mamba_section(k, s, pass2):
    p = k.p
    base = s * SEC
    if s == 0:
        p.dma("sp", lambda e: e.dma_start(
            out=k.xh.ap.rearrange("p (cc f) -> p cc f", cc=8),
            in_=k.i_xhalo.ap.rearrange("(cc p) f -> p cc f", p=128)), writes=[k.xh], tag="ldxh")
        emit_ln(k, "lnh", lambda c, a, w: k.xh[:, 4 * c + a: 4 * c + a + w],
                lambda c: k.xh,
                [(lambda c, a, w: k.mlnb[c][:, a:a + w], lambda c: k.mlnb[c])], 0, 1e-6, width=3)
    else:
        for c in range(8):
            p.copy("dve", k.mlnb[c][:, 0:3], k.mlnb[c][:, B3 - 3:B3],
                   reads=[k.mlnb[c]], writes=[k.mlnb[c]], tag=f"lead{c}")
        xzv = blkv(k.MB1, 16)
        p.copy("dve", xzv[:, :, 0:3], xzv[:, :, B3 - 3:B3], reads=[k.MB1], writes=[k.MB1], tag="xzlead")
    emit_ln(k, f"ln0s{s}p{int(pass2)}",
            lambda c, a, w: k.h[c][:, base + a: base + a + w], lambda c: k.h[c],
            [(lambda c, a, w: k.mlnb[c][:, 3 + a:3 + a + w], lambda c: k.mlnb[c])], 0, 1e-6, width=SEC)

    rw0, rww = (0, B3) if s == 0 else (3, SEC)
    fts = list(range(16)) + (list(range(16, 32)) if pass2 else [])
    for fgi in range(0, len(fts), 2):
        wb = k.wst[(fgi // 2) % 2]
        f0 = fts[fgi]
        src = k.i_win.ap[:, 128 * f0:128 * f0 + 256].rearrange("(cc p) f -> p cc f", p=128)
        p.dma("pool", lambda e, wb=wb, src=src: e.dma_start(
            out=wb.ap.rearrange("p (cc f) -> p cc f", cc=8), in_=src), writes=[wb], tag=f"ldwin{f0}")
        for fo in range(2):
            ft = f0 + fo
            pst = (k.psA, k.psB)[(ft // 2) % 2]
            reg = pst[:, (ft % 2) * 512:(ft % 2) * 512 + rww]
            for c in range(8):
                p.mm(reg, wb[:, c * 256 + fo * 128: c * 256 + fo * 128 + 128],
                     k.mlnb[c][:, rw0:rw0 + rww], c == 0, c == 7,
                     reads=[wb, k.mlnb[c]], writes=[pst], tag=f"xz{ft}")
            if ft < 16:
                dst = k.MB1.ap[:, ft * B3 + rw0: ft * B3 + rw0 + rww]
                p.copy("dve", dst, reg, reads=[pst], writes=[k.MB1], tag=f"xzev{ft}")
            else:
                zdst = k.MB2.ap[:, (ft - 16) * SEC:(ft - 16) * SEC + SEC]
                srcp = pst[:, (ft % 2) * 512 + (3 if s == 0 else 0):(ft % 2) * 512 + (3 if s == 0 else 0) + SEC]
                p.act(zdst, srcp, AF.Silu, reads=[pst], writes=[k.MB2], tag=f"silz{ft}")

    for c in range(16):
        st = "dve"
        a4b = k.ma4.ap[:, c * SEC:(c + 1) * SEC]
        p.ts(st, a4b, k.MB1.ap[:, c * B3 + 0: c * B3 + SEC], k.col(CONVC + 0 * 16 + c), None,
             ALU.mult, ALU.bypass, reads=[k.MB1, k.consts], writes=[k.ma4], tag=f"cv0_{c}")
        for i in range(1, 4):
            p.stt(st, a4b, k.MB1.ap[:, c * B3 + i: c * B3 + i + SEC], k.col(CONVC + i * 16 + c), a4b,
                  ALU.mult, ALU.add, reads=[k.MB1, k.consts, k.ma4], writes=[k.ma4], tag=f"cv{i}_{c}")

    p.act(k.mxc.ap, k.ma4.ap, AF.Silu, reads=[k.ma4], writes=[k.mxc], tag="silxc")
    if pass2:
        p.copy("dve", k.mwbf.ap, k.mxc.ap, reads=[k.mxc], writes=[k.mwbf], tag="xcb")
        xcb = k.mwbf
    else:
        p.copy("dve", k.mwbf.ap, k.mxc.ap, reads=[k.mxc], writes=[k.mwbf], tag="xcb")
        xcb = k.mwbf
    for half, dstb, pst in ((0, k.mBb, k.psC), (1, k.mCb, k.psD)):
        if half == 1 and not pass2:
            continue
        wbb = k.wst[half]
        src_ = k.i_wbc.ap[:, 64 * half:64 * half + 64].rearrange("(cc p) q -> p cc q", p=128)
        p.dma("pool", lambda e, wbb=wbb, src_=src_: e.dma_start(
            out=wbb.ap[:, 0:1024].rearrange("p (cc q) -> p cc q", cc=16), in_=src_),
            writes=[wbb], tag=f"ldbc{half}")
        for c in range(16):
            p.mm(pst[0:64, 0:SEC], wbb[:, c * 64:(c + 1) * 64],
                 xcb.ap[:, c * SEC:(c + 1) * SEC], c == 0, c == 15,
                 reads=[wbb, xcb], writes=[pst], tag=f"bc{half}_{c}")
        p.copy("dve", dstb.ap, pst[0:64, 0:SEC], reads=[pst], writes=[dstb], tag=f"bcev{half}")

    a1v = blkv(k.ma1, 16)
    a4v = blkv(k.ma4, 16)
    xcv = blkv(k.mxc, 16)
    p.tt("dve", a1v, xcv, k.cbc(WDTC, 16, SEC), ALU.mult, reads=[k.mxc, k.consts], writes=[k.ma1], tag="dt1")
    p.tt("dve", a1v, a1v, k.cbc(DTBC, 16, SEC), ALU.add, reads=[k.ma1, k.consts], writes=[k.ma1], tag="dt2")
    p.act(k.ma4.ap, k.ma1.ap, AF.Abs, reads=[k.ma1], writes=[k.ma4], tag="sp_abs")
    p.act(k.ma4.ap, k.ma4.ap, AF.Exp, scale=-1.0, reads=[k.ma4], writes=[k.ma4], tag="sp_exp")
    p.act(k.ma4.ap, k.ma4.ap, AF.Ln, bias=k.onesf[:, 0:1], reads=[k.ma4, k.onesf], writes=[k.ma4], tag="sp_ln")
    p.stt("dve", k.ma1.ap, k.ma1.ap, 0.0, k.ma4.ap, ALU.max, ALU.add, reads=[k.ma1, k.ma4], writes=[k.ma1], tag="sp_fin")
    p.tt("dve", a4v, a1v, k.cbc(ALOGC, 16, SEC), ALU.mult, reads=[k.ma1, k.consts], writes=[k.ma4], tag="dtA")
    p.tt("dve", a1v, a1v, xcv, ALU.mult, reads=[k.ma1, k.mxc], writes=[k.ma1], tag="vv")

    for c in range(16):
        p.op("dve", lambda e, c=c: e.tensor_tensor_scan(
            k.MB1.ap[:, c * B3:c * B3 + SEC], k.onesf[:, 0:SEC], k.ma4.ap[:, c * SEC:(c + 1) * SEC],
            k.mqcol[:, c:c + 1], ALU.mult, ALU.add),
            reads=[k.onesf, k.ma4, k.mqcol], writes=[k.MB1], tag=f"qscan{c}")

    qv = blkv(k.MB1, 16)
    ecv = blkv(k.mec, 16)
    pcv = blkv(k.mpc, 16)
    mcv = blkv(k.mmc, 16)
    p.copy("dve", ecv, qv[:, :, CK - 1:SEC:CK], reads=[k.MB1], writes=[k.mec], tag="bnd_e")
    p.copy("dve", pcv[:, :, 0:1], k.mqcol.ap.unsqueeze(2), reads=[k.mqcol], writes=[k.mpc], tag="bnd_p0")
    if NCHS > 1:
        p.copy("dve", pcv[:, :, 1:NCHS], ecv[:, :, 0:NCHS - 1], reads=[k.mec], writes=[k.mpc], tag="bnd_p1")
    p.tt("dve", mcv, ecv, pcv, ALU.add, reads=[k.mec, k.mpc], writes=[k.mmc], tag="bnd_m1")
    p.ts("dve", mcv, mcv, 0.5, None, ALU.mult, ALU.bypass, reads=[k.mmc], writes=[k.mmc], tag="bnd_m2")
    p.tt("dve", blkv(k.mdf, 16), mcv, pcv, ALU.subtract, reads=[k.mmc, k.mpc], writes=[k.mdf], tag="bnd_d1")
    p.act(k.mdh.ap, k.mdf.ap, AF.Exp, reads=[k.mdf], writes=[k.mdh], tag="bnd_dh")
    p.tt("dve", k.mdf.ap, k.mdh.ap, k.mdh.ap, ALU.mult, reads=[k.mdh], writes=[k.mdf], tag="bnd_df")
    if not pass2:
        for j in range(NCHS):
            p.tt("dve", k.mdtot.ap, k.mdtot.ap, blkv(k.mdf, 16)[:, :, j:j + 1].squeeze(2),
                 ALU.mult, reads=[k.mdtot, k.mdf], writes=[k.mdtot], tag=f"dtot{j}")
    p.copy("dve", k.mqcol.ap, qv[:, :, SEC - 1:SEC].squeeze(2), reads=[k.MB1], writes=[k.mqcol], tag="qcol")

    qv4 = blkv(k.MB1, 16)[:, :, 0:SEC].rearrange("p b (j t) -> p b j t", j=NCHS)
    mc4 = blkv(k.mmc, 16).unsqueeze(3).broadcast_to([128, 16, NCHS, CK])
    a44 = k.ma4.ap.rearrange("p (b j t) -> p b j t", b=16, j=NCHS)
    p.tt("dve", a44, mc4, qv4, ALU.subtract, reads=[k.mmc, k.MB1], writes=[k.ma4], tag="exparg")
    if pass2:
        p.act(k.meqb.ap, k.ma4.ap, AF.Exp, scale=-1.0, reads=[k.ma4], writes=[k.meqb], tag="eqb")
    p.act(k.ma4.ap, k.ma4.ap, AF.Exp, reads=[k.ma4], writes=[k.ma4], tag="wexp")
    p.tt("dve", k.mwbf.ap, k.ma4.ap, k.ma1.ap, ALU.mult, reads=[k.ma4, k.ma1], writes=[k.mwbf], tag="wbf")

    for j in range(NCHS):
        p.tr(k.psT[0:64, 1920:1984], k.mBb[:, j * CK:(j + 1) * CK], k.identb[0:64, 0:64],
             reads=[k.mBb, k.identb], writes=[k.psT], tag=f"btr{j}")
        p.copy("dve", k.mbtm.ap, k.psT[0:64, 1920:1984], reads=[k.psT], writes=[k.mbtm], tag=f"btm{j}")
        for c in range(16):
            p.tr(k.psT[0:64, c * 128:(c + 1) * 128],
                 k.mwbf.ap[:, c * SEC + j * CK: c * SEC + j * CK + CK],
                 k.identb.ap, reads=[k.mwbf, k.identb], writes=[k.psT], tag=f"wtr{j}_{c}")
        p.copy("dve", k.mwtm.ap, k.psT[0:64, 0:2048], reads=[k.psT], writes=[k.mwtm], tag=f"wtmev{j}")
        if pass2:
            p.mm(k.psD[0:64, 0:CK], k.mBb[:, j * CK:(j + 1) * CK], k.mCb[:, j * CK:(j + 1) * CK],
                 True, True, reads=[k.mBb, k.mCb], writes=[k.psD], tag=f"g{j}")
            p.tt("dve", k.mgm.ap, k.psD[0:64, 0:CK], k.trib.ap, ALU.mult,
                 reads=[k.psD, k.trib], writes=[k.mgm], tag=f"gm{j}")
            p.tt("dve", blkv(k.mhbf, 16), blkv(k.mhst, 16),
                 blkv(k.mdh, 16)[:, :, j:j + 1].broadcast_to([128, 16, 64]),
                 ALU.mult, reads=[k.mhst, k.mdh], writes=[k.mhbf], tag=f"hbf{j}")
            for c in range(16):
                p.tr(k.psT[0:64, c * 128:(c + 1) * 128], k.mhbf[:, c * 64:(c + 1) * 64],
                     k.identb.ap, reads=[k.mhbf, k.identb], writes=[k.psT], tag=f"htr{j}_{c}")
            p.copy("dve", k.mhtm.ap, k.psT[0:64, 0:2048], reads=[k.psT], writes=[k.mhtm], tag=f"htmev{j}")
            for c in range(16):
                p.mm(k.psA[:, c * 64:(c + 1) * 64], k.mwtm[0:64, c * 128:(c + 1) * 128], k.mgm.ap,
                     c in (0, 8), False, reads=[k.mwtm, k.mgm], writes=[k.psA], tag=f"s1_{j}_{c}")
            for c in range(16):
                p.mm(k.psA[:, c * 64:(c + 1) * 64], k.mhtm[0:64, c * 128:(c + 1) * 128],
                     k.mCb[:, j * CK:(j + 1) * CK], False, True,
                     reads=[k.mhtm, k.mCb], writes=[k.psA], tag=f"s2_{j}_{c}")
            p.tt("dve", k.MB3.ap.rearrange("p (b j t) -> p b j t", b=16, j=NCHS)[:, :, j, :],
                 k.psA.ap.rearrange("p (b t) -> p b t", b=16),
                 k.meqb.ap.rearrange("p (b j t) -> p b j t", b=16, j=NCHS)[:, :, j, :],
                 ALU.mult, reads=[k.psA, k.meqb], writes=[k.MB3], tag=f"ylo{j}")
        for c in range(16):
            p.mm(k.psB[:, c * 64:(c + 1) * 64], k.mwtm[0:64, c * 128:(c + 1) * 128], k.mbtm.ap,
                 c in (0, 8), True, reads=[k.mwtm, k.mbtm], writes=[k.psB], tag=f"upd{j}_{c}")
        p.tt("dve", k.psB.ap.rearrange("p (b t) -> p b t", b=16), k.psB.ap.rearrange("p (b t) -> p b t", b=16),
             blkv(k.mdh, 16)[:, :, j:j + 1].broadcast_to([128, 16, 64]), ALU.mult,
             reads=[k.psB, k.mdh], writes=[k.psB], tag=f"hup1_{j}")
        p.tt("dve", blkv(k.mhst, 16), blkv(k.mhst, 16),
             blkv(k.mdf, 16)[:, :, j:j + 1].broadcast_to([128, 16, 64]), ALU.mult,
             reads=[k.mhst, k.mdf], writes=[k.mhst], tag=f"hup2_{j}")
        p.tt("dve", k.mhst.ap, k.mhst.ap, k.psB.ap, ALU.add,
             reads=[k.mhst, k.psB], writes=[k.mhst], tag=f"hup3_{j}")

    if pass2:
        p.tt("dve", a4v, xcv, k.cbc(DSKC, 16, SEC), ALU.mult, reads=[k.mxc, k.consts], writes=[k.ma4], tag="dsk1")
        p.tt("dve", k.MB3.ap, k.MB3.ap, k.ma4.ap, ALU.add, reads=[k.MB3, k.ma4], writes=[k.MB3], tag="dsk2")
        p.tt("dve", k.MB3.ap, k.MB3.ap, k.MB2.ap, ALU.mult, reads=[k.MB3, k.MB2], writes=[k.MB3], tag="gate")
        for f in range(8):
            wb = k.wst[f % 2]
            src = k.i_wout.ap[:, 128 * f:128 * (f + 1)].rearrange("(cc p) q -> p cc q", p=128)
            p.dma("pool", lambda e, wb=wb, src=src: e.dma_start(
                out=wb.ap.rearrange("p (cc q) -> p cc q", cc=16), in_=src), writes=[wb], tag=f"ldwo{f}")
            pst = (k.psC, k.psD)[f % 2]
            for c in range(16):
                p.mm(pst[:, 0:SEC], wb[:, c * 128:(c + 1) * 128], k.MB3.ap[:, c * SEC:(c + 1) * SEC],
                     c == 0, c == 15, reads=[wb, k.MB3], writes=[pst], tag=f"wo{f}_{c}")
            p.stt("dve", k.h[f][:, base:base + SEC], pst[:, 0:SEC], k.masks[:, 0:1],
                  k.h[f][:, base:base + SEC], ALU.mult, ALU.add,
                  reads=[pst, k.masks, k.h[f]], writes=[k.h[f]], tag=f"hout{f}")


def emit_mamba_exchange(k):
    p = k.p
    gi, go = k.g_mst_in, k.g_mst_out
    p.dma("sp", lambda e: e.dma_start(
        out=gi.ap.rearrange("(cc p) n -> p cc n", p=128),
        in_=k.mhst.ap.rearrange("p (cc n) -> p cc n", cc=16)),
        reads=[k.mhst], writes=[gi], tag="mst_st1")
    p.dma("sp", lambda e: e.dma_start(out=k.g_mp_in.ap, in_=k.mdtot.ap),
          reads=[k.mdtot], writes=[k.g_mp_in], tag="mst_st2")
    p.coll(lambda e: e.collective_compute("AllGather", ALU.bypass, replica_groups=[list(range(NCORES))],
                                          ins=[gi.ap.opt()], outs=[go.ap.opt()]),
           reads=[gi], writes=[go], tag="AG_mst")
    p.coll(lambda e: e.collective_compute("AllGather", ALU.bypass, replica_groups=[list(range(NCORES))],
                                          ins=[k.g_mp_in.ap.opt()], outs=[k.g_mp_out.ap.opt()]),
           reads=[k.g_mp_in], writes=[k.g_mp_out], tag="AG_mp")
    macc = k.sub("macc", 2, 0, 16 * 64, F32)
    p.op("dve", lambda e: e.memset(macc.ap, 0.0), writes=[macc])
    p.op("dve", lambda e: e.memset(k.mhst.ap, 0.0), writes=[k.mhst])
    mrb = [k.sub(f"mrb{i}", i, 0, 16 * 64, F32) for i in range(2)]
    mrp = [k.sub(f"mrp{i}", i, 16 * 64, 16, F32) for i in range(2)]
    for j in range(NCORES):
        rb, rp = mrb[j % 2], mrp[j % 2]
        src = go.ap[j * DI:(j + 1) * DI, :].rearrange("(cc p) n -> p cc n", p=128)
        p.dma("sp", lambda e, rb=rb, src=src: e.dma_start(
            out=rb.ap.rearrange("p (cc n) -> p cc n", cc=16), in_=src), reads=[go], writes=[rb], tag=f"mrb{j}")
        srcp = k.g_mp_out.ap[j * 128:(j + 1) * 128, :]
        p.dma("sp", lambda e, rp=rp, srcp=srcp: e.dma_start(out=rp.ap, in_=srcp),
              reads=[k.g_mp_out], writes=[rp], tag=f"mrp{j}")
        p.stt("dve", k.mhst.ap, macc.ap, k.col(SELR + j), k.mhst.ap, ALU.mult, ALU.add,
              reads=[macc, k.consts, k.mhst], writes=[k.mhst], tag=f"pick{j}")
        if j < NCORES - 1:
            p.tt("dve", blkv(macc, 16), blkv(macc, 16),
                 rp.ap.unsqueeze(2).broadcast_to([128, 16, 64]), ALU.mult,
                 reads=[macc, rp], writes=[macc], tag=f"accP{j}")
            p.tt("dve", blkv(macc, 16), blkv(macc, 16), blkv(rb, 16), ALU.add,
                 reads=[macc, rb], writes=[macc], tag=f"accH{j}")


def emit_mamba(k):
    p = k.p
    alloc_mamba(k)
    p.op("dve", lambda e: e.memset(k.mhst.ap, 0.0), writes=[k.mhst])
    p.op("dve", lambda e: e.memset(k.mqcol.ap, 0.0), writes=[k.mqcol])
    p.op("dve", lambda e: e.memset(k.mdtot.ap, 1.0), writes=[k.mdtot])
    for s in range(NSEC):
        emit_mamba_section(k, s, pass2=False)
    emit_mamba_exchange(k)
    p.op("dve", lambda e: e.memset(k.mqcol.ap, 0.0), writes=[k.mqcol])
    for s in range(NSEC):
        emit_mamba_section(k, s, pass2=True)


def emit_tc(k):
    p = k.p
    x1b = [k.sub(f"x1b{c}", 4 + c // 4, (c % 4) * 2048, 2048, BF16) for c in range(8)]
    x1f = [k.sub(f"x1f{c}", 2, c * 512, 512, F32) for c in range(8)]
    tcw = k.sub("tcw", 3, 0, 8192, BF16)
    p.dma("pool", lambda e: e.dma_start(
        out=tcw.ap.rearrange("p (cc f) -> p cc f", cc=8),
        in_=k.i_tcw.ap.rearrange("(cc p) f -> p cc f", p=128)), writes=[tcw], tag="ldtcw")
    tcc = k.sbuf("tcc", [128, 4], F32)
    thc = k.col(THC)
    p.act(tcc[:, 0:1], thc, AF.Sin, scale=0.5, reads=[k.consts], writes=[tcc], tag="tcsin")
    p.tt("dve", tcc[:, 1:2], tcc[:, 0:1], tcc[:, 0:1], ALU.mult, reads=[tcc], writes=[tcc])
    p.ts("dve", tcc[:, 2:3], tcc[:, 1:2], -1.0, 1.0, ALU.mult, ALU.add, reads=[tcc], writes=[tcc])
    p.act(tcc[:, 2:3], tcc[:, 2:3], AF.Ln, reads=[tcc], writes=[tcc])
    p.act(tcc[:, 2:3], tcc[:, 2:3], AF.Exp, scale=0.5, reads=[tcc], writes=[tcc])
    p.tt("dve", tcc[:, 3:4], tcc[:, 0:1], tcc[:, 2:3], ALU.mult, reads=[tcc], writes=[tcc])
    p.ts("dve", tcc[:, 3:4], tcc[:, 3:4], 2.0, None, ALU.mult, ALU.bypass, reads=[tcc], writes=[tcc])
    p.ts("dve", tcc[:, 1:2], tcc[:, 1:2], -2.0, 1.0, ALU.mult, ALU.add, reads=[tcc], writes=[tcc])
    p.tt("dve", tcc[:, 1:2], tcc[:, 1:2], k.masks[:, 1:2], ALU.mult, reads=[tcc, k.masks], writes=[tcc])
    p.tt("dve", tcc[:, 3:4], tcc[:, 3:4], k.masks[:, 1:2], ALU.mult, reads=[tcc, k.masks], writes=[tcc])
    for ci in range(4):
        a = ci * 512
        emit_ln(k, f"ln1c{ci}",
                lambda c, aa, w, a=a: k.h[c][:, a + aa: a + aa + w], lambda c: k.h[c],
                [(lambda c, aa, w: x1f[c].ap[:, aa:aa + w], lambda c: x1f[c]),
                 (lambda c, aa, w, a=a: x1b[c].ap[:, a + aa:a + aa + w], lambda c: x1b[c])],
                1, 1e-6, width=512)
        if ci == 0 and "tc" in DBG:
            d2 = p.dram_out("dbg_x1f0", [128, 512], F32)
            d3 = p.dram_out("dbg_tcc", [128, 4], F32)
            d4 = p.dram_out("dbg_masks", [128, 4], F32)
            p.dma("sp", lambda e: e.dma_start(out=d2.ap, in_=x1f[0].ap), reads=[x1f[0]], writes=[d2])
            p.dma("sp", lambda e: e.dma_start(out=d3.ap, in_=tcc.ap), reads=[tcc], writes=[d3])
            p.dma("sp", lambda e: e.dma_start(out=d4.ap, in_=k.masks.ap), reads=[k.masks], writes=[d4])
        for f in range(8):
            pst = (k.psC, k.psD)[f % 2]
            for c in range(8):
                p.mm(pst[:, 0:512], tcw[:, c * 1024 + f * 128: c * 1024 + f * 128 + 128],
                     x1b[c].ap[:, a:a + 512], c == 0, c == 7,
                     reads=[tcw, x1b[c]], writes=[pst], tag=f"tcmm{f}")
            p.act(k.ln_t1.ap, pst[:, 0:512], AF.Tanh, bias=k.col(TCBC + f),
                  reads=[pst, k.consts], writes=[k.ln_t1], tag=f"tctanh{f}")
            p.stt("dve", k.h[f][:, a:a + 512], x1f[f].ap, tcc[:, 1:2], k.h[f][:, a:a + 512],
                  ALU.mult, ALU.add, reads=[x1f[f], tcc, k.h[f]], writes=[k.h[f]], tag=f"tchx{f}")
            p.stt("dve", k.h[f][:, a:a + 512], k.ln_t1.ap, tcc[:, 3:4], k.h[f][:, a:a + 512],
                  ALU.mult, ALU.add, reads=[k.ln_t1, tcc, k.h[f]], writes=[k.h[f]], tag=f"tcht{f}")


def build_kmax(k, tiles, dstbuf, nm, dcol):
    p = k.p
    p.op("dve", lambda e: e.memset(k.ln_t1.ap, 0.0), writes=[k.ln_t1])
    for cc in range(8):
        for tch in range(2):
            p.act(k.ln_xb.ap, tiles[cc].ap[:, tch * 512:tch * 512 + 512], AF.Square,
                  reads=[tiles[cc]], writes=[k.ln_xb], tag=f"{nm}sq{cc}")
            for h2 in range(2):
                pst = k.psC
                p.mm(pst[h2 * 64:h2 * 64 + 1, 0:512], k.onesb[h2 * 64:h2 * 64 + 64, 0:1],
                     k.ln_xb[h2 * 64:h2 * 64 + 64, 0:512], True, True,
                     reads=[k.onesb, k.ln_xb], writes=[pst], tag=f"{nm}mm{cc}{h2}")
                col = 16 + 2 * cc + h2
                p.op("dve", lambda e, pst=pst, h2=h2, col=col: e.tensor_reduce(
                    k.ln_t1[h2 * 64:h2 * 64 + 1, col:col + 1], pst[h2 * 64:h2 * 64 + 1, 0:512], AX.X, ALU.max),
                    reads=[pst], writes=[k.ln_t1], tag=f"{nm}rd{cc}{h2}")
                p.tt("dve", k.ln_t1[h2 * 64:h2 * 64 + 1, 2 * cc + h2:2 * cc + h2 + 1],
                     k.ln_t1[h2 * 64:h2 * 64 + 1, 2 * cc + h2:2 * cc + h2 + 1],
                     k.ln_t1[h2 * 64:h2 * 64 + 1, col:col + 1], ALU.max,
                     reads=[k.ln_t1], writes=[k.ln_t1], tag=f"{nm}mx{cc}{h2}")
    pst = k.psD
    p.tr(pst[0:16, 0:128], k.ln_t1[:, 0:16], k.ident.ap, reads=[k.ln_t1, k.ident], writes=[pst], tag=f"{nm}tr")
    fv = k.ln_xb.ap.bitcast(F32)
    p.copy("dve", fv[0:16, 0:2], pst[0:16, 0:128:64], reads=[pst], writes=[k.ln_xb], tag=f"{nm}cp")
    p.op("dve", lambda e: e.tensor_reduce(fv[0:16, 2:3], fv[0:16, 0:2], AX.X, ALU.max),
         reads=[k.ln_xb], writes=[k.ln_xb], tag=f"{nm}rd2")
    p.tr(pst[0:1, 0:16], fv[0:16, 2:3], k.ident[0:16, 0:16], reads=[k.ln_xb, k.ident], writes=[pst], tag=f"{nm}tr2")
    p.copy("dve", dstbuf[0:1, dcol:dcol + 16], pst[0:1, 0:16], reads=[pst], writes=[dstbuf], tag=f"{nm}fin")


def emit_wlam(k):
    p = k.p
    INV = float(np.float32(0.7071067811865476))
    abf = [k.sub(f"abf{c}", 4, c * 1024, 1024, BF16) for c in range(8)]
    qbf = [k.sub(f"qbf{c}", 5, c * 1024, 1024, BF16) for c in range(8)]
    ds = [k.sub(f"wds{c}", c // 4, (c % 4) * 1024, 1024, F32) for c in range(8)]
    x2f = [k.sub(f"x2f{c}", 2, c * 512, 512, F32) for c in range(8)]
    kfm = [k.sub(f"kfm{c}", 3, c * 1024, 1024, BF16) for c in range(8)]
    for ci in range(4):
        a = ci * 512
        pa = ci * 256
        emit_ln(k, f"ln2c{ci}",
                lambda c, aa, w, a=a: k.h[c][:, a + aa: a + aa + w], lambda c: k.h[c],
                [(lambda c, aa, w: x2f[c].ap[:, aa:aa + w], lambda c: x2f[c])],
                2, 1e-6, width=512)
        for c in range(8):
            ev = x2f[c].ap[:, 0:512:2]
            od = x2f[c].ap[:, 1:512:2]
            p.tt("dve", k.ln_t1[:, 0:256], ev, od, ALU.add, reads=[x2f[c]], writes=[k.ln_t1], tag=f"pa{c}")
            p.ts("dve", abf[c].ap[:, pa:pa + 256], k.ln_t1[:, 0:256], INV, None, ALU.mult, ALU.bypass,
                 reads=[k.ln_t1], writes=[abf[c]], tag=f"pab{c}")
            p.tt("dve", k.ln_t1[:, 0:256], ev, od, ALU.subtract, reads=[x2f[c]], writes=[k.ln_t1], tag=f"pd{c}")
            p.ts("dve", ds[c].ap[:, pa:pa + 256], k.ln_t1[:, 0:256], 0.5, None, ALU.mult, ALU.bypass,
                 reads=[k.ln_t1], writes=[ds[c]], tag=f"pdb{c}")

    def prj(widram, dsts, tagp):
        for f in range(8):
            wb = k.wst[f % 2]
            src = widram.ap[:, 128 * f:128 * (f + 1)].rearrange("(cc p) q -> p cc q", p=128)
            p.dma("pool", lambda e, wb=wb, src=src: e.dma_start(
                out=wb.ap[:, 0:1024].rearrange("p (cc q) -> p cc q", cc=8), in_=src),
                writes=[wb], tag=f"ld{tagp}{f}")
            for tch in range(2):
                pst = (k.psC, k.psD)[tch]
                for c in range(8):
                    p.mm(pst[:, 0:512], wb[:, c * 128:(c + 1) * 128], abf[c].ap[:, tch * 512:tch * 512 + 512],
                         c == 0, c == 7, reads=[wb, abf[c]], writes=[pst], tag=f"{tagp}mm{f}")
                p.copy("dve", dsts[f].ap[:, tch * 512:tch * 512 + 512], pst[:, 0:512],
                       reads=[pst], writes=[dsts[f]], tag=f"{tagp}ev{f}")

    prj(k.i_wq, qbf, "wq")
    prj(k.i_wk, kfm, "wk")
    kmax = None
    build_kmax(k, kfm, k.rows, "kn", 272)
    for c in range(8):
        p.dma("sp", lambda e, c=c: e.dma_start(
            out=k.g_kv_in.ap[c * 128:(c + 1) * 128, 0:1024], in_=kfm[c].ap),
            reads=[kfm[c]], writes=[k.g_kv_in], tag=f"stk{c}")
    p.dma("pool", lambda e: e.dma_start(out=k.g_kv_in.ap[2 * D:2 * D + 1, 0:16], in_=k.rows[0:1, 272:288]),
          reads=[k.rows], writes=[k.g_kv_in], tag="stkn")
    prj(k.i_wv, kfm, "wv")
    vtm = [k.sub(f"vtm{c}", 4, c * 1040, 1040, BF16) for c in range(8)]
    for i in range(8):
        for c in range(8):
            p.tr(k.psT[:, c * 128:(c + 1) * 128], kfm[c].ap[:, i * 128:(i + 1) * 128], k.identb.ap,
                 reads=[kfm[c], k.identb], writes=[k.psT], tag=f"vtr{i}_{c}")
        p.op("dve", lambda e, i=i: e.memset(vtm[i].ap.rearrange("p (hh u) -> p hh u", u=65)[:, :, 64:65], 1.0),
             writes=[vtm[i]], tag=f"vone{i}")
        p.copy("dve", vtm[i].ap.rearrange("p (hh u) -> p hh u", u=65)[:, :, 0:64],
               k.psT[:, 0:1024].rearrange("p (hh d) -> p hh d", d=64),
               reads=[k.psT], writes=[vtm[i]], tag=f"vte{i}")
        p.dma("sp", lambda e, i=i: e.dma_start(
            out=k.g_kv_in.ap[D + i * 128: D + (i + 1) * 128, :], in_=vtm[i].ap),
            reads=[vtm[i]], writes=[k.g_kv_in], tag=f"stv{i}")
    gi, go = k.g_kv_in, k.g_kv_out
    p.coll(lambda e: e.collective_compute("AllGather", ALU.bypass, replica_groups=[list(range(NCORES))],
                                          ins=[gi.ap.opt()], outs=[go.ap.opt()]),
           reads=[gi], writes=[go], tag="AG_kv")
    build_kmax(k, qbf, k.rows, "qn", 288)
    qmax = None
    kng = None
    KST = 2 * D + 16
    for r in range(NCORES):
        p.dma("pool", lambda e, r=r: e.dma_start(
            out=k.rows[0:1, r * 16:(r + 1) * 16], in_=go.ap[r * KST + 2 * D: r * KST + 2 * D + 1, 0:16]),
            reads=[go], writes=[k.rows], tag=f"ldkn{r}")
    kngm = None
    kngm_ap = k.rows[0:1, 256:272]
    p.op("dve", lambda e: e.tensor_reduce(
        kngm_ap, k.rows[0:1, 0:128].rearrange("one (r hh) -> one hh r", r=NCORES), AX.X, ALU.max),
        reads=[k.rows], writes=[k.rows], tag="knmax")
    p.tt("dve", kngm_ap, kngm_ap, k.rows[0:1, 288:304], ALU.mult, reads=[k.rows], writes=[k.rows])
    p.act(kngm_ap, kngm_ap, AF.Ln, reads=[k.rows], writes=[k.rows])
    p.act(kngm_ap, kngm_ap, AF.Exp, scale=0.5, reads=[k.rows], writes=[k.rows])
    p.ts("dve", kngm_ap, kngm_ap, 1.01 / 8.0, -40.0, ALU.mult, ALU.add, reads=[k.rows], writes=[k.rows])
    p.ts("dve", kngm_ap, kngm_ap, 0.0, -1.0, ALU.max, ALU.mult, reads=[k.rows], writes=[k.rows])
    biasr = k.sbuf("biasr", [128, 16], F32)
    p.mm(k.psD[:, 0:16], k.onesf[0:1, 0:128], kngm_ap, True, True, reads=[k.onesf, k.rows], writes=[k.psD], tag="biasbc")
    p.copy("dve", biasr.ap, k.psD[:, 0:16], reads=[k.psD], writes=[biasr])
    ksb = k.sub("ksb", 3, 0, 8192, BF16)
    pt = [k.sub(f"pt{i}", 2, i * 2048, 2048, BF16) for i in range(2)]
    vt = [k.sub(f"vt{i}", 2, 4096 + i * 1040, 1040, BF16) for i in range(2)]
    ofm = vtm
    for cc in range(8):
        for r in range(NCORES):
            p.dma("sp", lambda e, cc=cc, r=r: e.dma_start(
                out=ksb.ap[:, r * 1024:(r + 1) * 1024],
                in_=go.ap[r * KST + cc * 128: r * KST + (cc + 1) * 128, 0:1024]),
                reads=[go], writes=[ksb], tag=f"ldK{cc}_{r}")
        for h2 in range(2):
            h = 2 * cc + h2
            for tt2 in range(64):
                vti = vt[tt2 % 2]
                r, loc = tt2 // 8, tt2 % 8
                p.dma("sp", lambda e, vti=vti, r=r, loc=loc: e.dma_start(
                    out=vti.ap, in_=go.ap[r * KST + D + loc * 128: r * KST + D + (loc + 1) * 128, :]),
                    reads=[go], writes=[vti], tag=f"ldV{cc}_{h2}_{tt2}")
                pss = (k.psA, k.psB)[tt2 % 2]
                ptb = pt[tt2 % 2]
                for tch in range(2):
                    p.mm(pss[:, tch * 512:tch * 512 + 512],
                         ksb.ap[64 * h2:64 * h2 + 64, tt2 * 128:(tt2 + 1) * 128],
                         qbf[cc].ap[64 * h2:64 * h2 + 64, tch * 512:tch * 512 + 512], True, True,
                         reads=[ksb, qbf[cc]], writes=[pss], tag=f"st{h}_{tt2}_{tch}")
                p.act(ptb.ap[:, 0:1024], pss.ap, AF.Exp, scale=0.125, bias=biasr[:, h:h + 1],
                      reads=[pss, biasr], writes=[ptb], tag=f"exp{h}_{tt2}")
                for tch in range(2):
                    acc = (k.psC, k.psD)[tch]
                    p.mm(acc[0:65, 0:512], vti.ap[:, h * 65:(h + 1) * 65],
                         ptb.ap[:, tch * 512:tch * 512 + 512], tt2 == 0, tt2 == 63,
                         reads=[vti, ptb], writes=[acc], tag=f"av{h}_{tt2}_{tch}")
            for tch in range(2):
                accp = (k.psC, k.psD)[tch]
                p.op("dve", lambda e, accp=accp: e.reciprocal(k.ln_t1[64:65, 0:512], accp[64:65, 0:512]),
                     reads=[accp], writes=[k.ln_t1], tag=f"rd{h}{tch}")
                p.mm(k.psB[0:64, 0:512], k.onesf[64:65, 0:64], k.ln_t1[64:65, 0:512], True, True,
                     reads=[k.onesf, k.ln_t1], writes=[k.psB], tag=f"rdb{h}{tch}")
                p.copy("dve", k.ln_t1[0:64, 0:512], k.psB[0:64, 0:512], reads=[k.psB], writes=[k.ln_t1],
                       tag=f"rde{h}{tch}")
                p.tt("dve", ofm[cc].ap[h2 * 64:h2 * 64 + 64, tch * 512:tch * 512 + 512],
                     accp[0:64, 0:512], k.ln_t1[0:64, 0:512], ALU.mult,
                     reads=[accp, k.ln_t1], writes=[ofm[cc]], tag=f"onorm{h}{tch}")
    ywloc = [(2, 0), (2, 2048), (3, 0), (3, 2048), (0, 0), (0, 2048), (1, 0), (1, 2048)]
    yw = [k.sub(f"yw{c}", ywloc[c][0], ywloc[c][1], 2048, F32) for c in range(8)]
    ds7v = SubBuf("ds7v", k.wst[1], 0, 1024, F32)
    p.copy("dve", ds7v.ap, ds[7].ap, reads=[ds[7]], writes=[k.wst[1]], tag="ds7cp")
    dsrc = ds[:7] + [ds7v]
    for f in range(8):
        wb = k.wst[0]
        src = k.i_wo.ap[:, 128 * f:128 * (f + 1)].rearrange("(cc p) q -> p cc q", p=128)
        p.dma("pool", lambda e, wb=wb, src=src: e.dma_start(
            out=wb.ap[:, 0:1024].rearrange("p (cc q) -> p cc q", cc=8), in_=src),
            writes=[wb], tag=f"ldwo2{f}")
        for tch in range(2):
            pst = (k.psC, k.psD)[tch]
            for c in range(8):
                p.mm(pst[:, 0:512], wb[:, c * 128:(c + 1) * 128], ofm[c].ap[:, tch * 512:tch * 512 + 512],
                     c == 0, c == 7, reads=[wb, ofm[c]], writes=[pst], tag=f"womm{f}")
            p.ts("dve", k.ln_t1.ap, pst[:, 0:512], INV, None, ALU.mult, ALU.bypass,
                 reads=[pst], writes=[k.ln_t1], tag=f"o2i{f}")
            p.tt("dve", yw[f].ap[:, tch * 1024:tch * 1024 + 1024:2], k.ln_t1.ap,
                 dsrc[f].ap[:, tch * 512:tch * 512 + 512], ALU.add,
                 reads=[k.ln_t1, dsrc[f]], writes=[yw[f]], tag=f"ywe{f}")
            p.tt("dve", yw[f].ap[:, tch * 1024 + 1:tch * 1024 + 1024:2], k.ln_t1.ap,
                 dsrc[f].ap[:, tch * 512:tch * 512 + 512], ALU.subtract,
                 reads=[k.ln_t1, dsrc[f]], writes=[yw[f]], tag=f"ywo{f}")
    for c in range(8):
        p.copy("dve", k.ln_t1[:, 0:1], yw[c].ap[:, 0:1], reads=[yw[c]], writes=[k.ln_t1])
        p.copy("dve", k.ln_t1[:, 1:3], yw[c].ap[:, 2046:2048], reads=[yw[c]], writes=[k.ln_t1])
        p.dma("sp", lambda e, c=c: e.dma_start(out=k.g_yw_in.ap[c * 128:(c + 1) * 128, 0:3],
                                               in_=k.ln_t1[:, 0:3]),
              reads=[k.ln_t1], writes=[k.g_yw_in], tag=f"styw{c}")
    p.coll(lambda e: e.collective_compute("AllGather", ALU.bypass, replica_groups=[list(range(NCORES))],
                                          ins=[k.g_yw_in.ap.opt()], outs=[k.g_yw_out.ap.opt()]),
           reads=[k.g_yw_in], writes=[k.g_yw_out], tag="AG_yw")
    ywh = k.sub("ywh", 4, 0, 256, F32)
    p.dma("sp", lambda e: e.dma_start(
        out=ywh.ap.rearrange("p (r cc f) -> p r cc f", r=8, cc=8),
        in_=k.g_yw_out.ap.rearrange("(r cc p) f -> p r cc f", r=8, p=128)),
        reads=[k.g_yw_out], writes=[ywh], tag="ldywh")
    hal = k.sub("hal", 4, 256, 24, F32)
    p.op("dve", lambda e: e.memset(hal.ap, 0.0), writes=[hal])
    hv = hal.ap.rearrange("p (cc u) -> p cc u", cc=8)
    yv = ywh.ap.rearrange("p (r cc f) -> p r cc f", r=8, cc=8)
    for r in range(NCORES):
        p.stt("dve", hv[:, :, 0:2], yv[:, r, :, 1:3], k.col(SELP + r), hv[:, :, 0:2],
              ALU.mult, ALU.add, reads=[ywh, k.consts, hal], writes=[hal], tag=f"halL{r}")
        p.stt("dve", hv[:, :, 2:3], yv[:, r, :, 0:1], k.col(SELN + r), hv[:, :, 2:3],
              ALU.mult, ALU.add, reads=[ywh, k.consts, hal], writes=[hal], tag=f"halR{r}")
    wmc = k.sbuf("wmc", [128, 32], F32)
    p.tt("dve", wmc.ap, k.consts[:, WCVC:WCVC + 32], k.masks[:, 2:3].broadcast_to([128, 32]),
         ALU.mult, reads=[k.consts, k.masks], writes=[wmc], tag="wmc")
    for c in range(8):
        for i in range(4):
            sh = i - 2
            lo = max(0, -sh)
            hi = min(T, T - sh)
            p.stt("dve", k.h[c][:, lo:hi], yw[c].ap[:, lo + sh:hi + sh], wmc[:, i * 8 + c:i * 8 + c + 1],
                  k.h[c][:, lo:hi], ALU.mult, ALU.add, reads=[yw[c], wmc, k.h[c]], writes=[k.h[c]],
                  tag=f"cvw{i}_{c}")
        p.stt("dve", k.h[c][:, 0:1], hv[:, c, 0:1], wmc[:, 0 * 8 + c:0 * 8 + c + 1], k.h[c][:, 0:1],
              ALU.mult, ALU.add, reads=[hal, wmc, k.h[c]], writes=[k.h[c]], tag=f"fxa{c}")
        p.stt("dve", k.h[c][:, 0:1], hv[:, c, 1:2], wmc[:, 1 * 8 + c:1 * 8 + c + 1], k.h[c][:, 0:1],
              ALU.mult, ALU.add, reads=[hal, wmc, k.h[c]], writes=[k.h[c]], tag=f"fxb{c}")
        p.stt("dve", k.h[c][:, 1:2], hv[:, c, 1:2], wmc[:, 0 * 8 + c:0 * 8 + c + 1], k.h[c][:, 1:2],
              ALU.mult, ALU.add, reads=[hal, wmc, k.h[c]], writes=[k.h[c]], tag=f"fxc{c}")
        p.stt("dve", k.h[c][:, T - 1:T], hv[:, c, 2:3], wmc[:, 3 * 8 + c:3 * 8 + c + 1], k.h[c][:, T - 1:T],
              ALU.mult, ALU.add, reads=[hal, wmc, k.h[c]], writes=[k.h[c]], tag=f"fxd{c}")


def emit_moe(k):
    p = k.p
    x3b = [k.sub(f"x3b{c}", 4 + c // 4, (c % 4) * 2048, 2048, BF16) for c in range(8)]
    x3f = [k.sub(f"x3f{c}", 2, c * 512, 512, F32) for c in range(8)]
    gtm = k.sub("gtm", 3, 1664, 128, F32)
    wgsb = k.sub("wgsb", 3, 1536, 64, F32)
    gsc = k.sub("gsc", 3, 1600, 8, F32)
    p.dma("sp", lambda e: e.dma_start(
        out=wgsb.ap.rearrange("p (cc e) -> p cc e", cc=8),
        in_=k.i_wg.ap.rearrange("(cc p) e -> p cc e", p=128)), writes=[wgsb], tag="ldwg")
    for ci in range(4):
        a = ci * 512
        emit_ln(k, f"ln3c{ci}",
                lambda c, aa, w, a=a: k.h[c][:, a + aa: a + aa + w], lambda c: k.h[c],
                [(lambda c, aa, w: x3f[c].ap[:, aa:aa + w], lambda c: x3f[c]),
                 (lambda c, aa, w, a=a: x3b[c].ap[:, a + aa:a + aa + w], lambda c: x3b[c])],
                3, 1e-6, width=512)
        for c in range(8):
            p.mm(k.psD[0:8, 0:512], wgsb[:, c * 8:(c + 1) * 8], x3f[c].ap, c == 0, c == 7,
                 reads=[wgsb, x3f[c]], writes=[k.psD], tag=f"lgmm{ci}")
        p.copy("dve", k.ln_t1[0:8, 0:512], k.psD[0:8, 0:512], reads=[k.psD], writes=[k.ln_t1], tag=f"lgev{ci}")
        for i in range(4):
            tl = ci * 4 + i
            p.tr(k.psC[0:128, 0:8], k.ln_t1[0:8, i * 128:(i + 1) * 128], k.ident[0:8, 0:8],
                 reads=[k.ln_t1, k.ident], writes=[k.psC], tag=f"lgtr{tl}")
            p.copy("dve", gtm[:, tl * 8:(tl + 1) * 8], k.psC[0:128, 0:8], reads=[k.psC], writes=[gtm],
                   tag=f"lgtm{tl}")
    fv = k.ln_xb.ap.bitcast(F32)
    for tl in range(16):
        g8 = gtm[:, tl * 8:(tl + 1) * 8]
        p.op("dve", lambda e, g8=g8: e.tensor_reduce(gsc[:, 0:1], g8, AX.X, ALU.max),
             reads=[gtm], writes=[gsc], tag=f"g1_{tl}")
        p.op("dve", lambda e, g8=g8: e.tensor_scalar(fv[:, 0:8], g8, gsc[:, 0:1], None,
                                                     ALU.is_equal, ALU.bypass),
             reads=[gtm, gsc], writes=[k.ln_xb], tag=f"geq{tl}")
        p.stt("dve", fv[:, 8:16], fv[:, 0:8], -1e30, g8, ALU.mult, ALU.add,
              reads=[k.ln_xb, gtm], writes=[k.ln_xb], tag=f"glm{tl}")
        p.op("dve", lambda e: e.tensor_reduce(gsc[:, 1:2], fv[:, 8:16], AX.X, ALU.max),
             reads=[k.ln_xb], writes=[gsc], tag=f"g2_{tl}")
        p.op("dve", lambda e, g8=g8: e.tensor_scalar(fv[:, 0:8], g8, gsc[:, 1:2], None,
                                                     ALU.is_ge, ALU.bypass),
             reads=[gtm, gsc], writes=[k.ln_xb], tag=f"gge{tl}")
        p.op("dve", lambda e, g8=g8: e.tensor_scalar(fv[:, 8:16], g8, gsc[:, 0:1], None,
                                                     ALU.subtract, ALU.bypass),
             reads=[gtm, gsc], writes=[k.ln_xb], tag=f"gsh{tl}")
        p.act(fv[:, 8:16], fv[:, 8:16], AF.Exp, reads=[k.ln_xb], writes=[k.ln_xb], tag=f"gexp{tl}")
        p.tt("dve", gsc[:, 2:3], gsc[:, 1:2], gsc[:, 0:1], ALU.subtract, reads=[gsc], writes=[gsc])
        p.act(gsc[:, 2:3], gsc[:, 2:3], AF.Exp, reads=[gsc], writes=[gsc], tag=f"gden{tl}")
        p.ts("dve", gsc[:, 2:3], gsc[:, 2:3], 1.0, None, ALU.add, ALU.bypass, reads=[gsc], writes=[gsc])
        p.op("dve", lambda e: e.reciprocal(gsc[:, 2:3], gsc[:, 2:3]), reads=[gsc], writes=[gsc])
        p.tt("dve", gsc[:, 2:3], gsc[:, 2:3], k.masks[:, 3:4], ALU.mult, reads=[gsc, k.masks], writes=[gsc])
        p.tt("dve", fv[:, 0:8], fv[:, 0:8], fv[:, 8:16], ALU.mult, reads=[k.ln_xb], writes=[k.ln_xb],
             tag=f"gm1_{tl}")
        p.op("dve", lambda e, g8=g8: e.tensor_scalar(g8, fv[:, 0:8], gsc[:, 2:3], None,
                                                     ALU.mult, ALU.bypass),
             reads=[k.ln_xb, gsc], writes=[gtm], tag=f"gfin{tl}")
    g8r = k.sub("g8r", 2, 0, 2048, BF16)
    for tl in range(16):
        p.tr(k.psC[0:8, 0:128], gtm[:, tl * 8:(tl + 1) * 8], k.ident.ap,
             reads=[gtm, k.ident], writes=[k.psC], tag=f"g8tr{tl}")
        p.copy("dve", g8r.ap[0:8, tl * 128:(tl + 1) * 128], k.psC[0:8, 0:128],
               reads=[k.psC], writes=[g8r], tag=f"g8ev{tl}")
    grep = k.sub("grep", 2, 1024, 2048, F32)
    h1g = [k.sub("h1gA", 0, 0, 8192, BF16), k.sub("h1gB", 1, 0, 8192, BF16)]
    gelsets = [[k.sub(f"gel{s}_{i}", 3, (s * 3 + i) * 512, 512, F32) for i in range(3)] for s in range(2)]
    b2sb = k.sub("b2sb", 3, 7168, 1024, BF16)
    k.ohot8 = k.sub("ohot8", 3, 6144, 1024, BF16)
    p.dma("pool", lambda e: e.dma_start(out=k.ohot8.ap[0:8, :], in_=k.i_ohot.ap), writes=[k.ohot8], tag="ldoh")
    p.dma("pool", lambda e: e.dma_start(out=b2sb.ap[0:8, :], in_=k.i_b2r.ap), writes=[b2sb], tag="ldb2")
    for e_ in range(E):
        for ch in range(4):
            pst = (k.psC, k.psD)[ch % 2]
            p.mm(pst[:, 0:512], k.ohot8.ap[0:8, e_ * 128:(e_ + 1) * 128],
                 g8r.ap[0:8, ch * 512:(ch + 1) * 512], True, True,
                 reads=[k.ohot8, g8r], writes=[pst], tag=f"grbc{e_}_{ch}")
            p.copy("dve", grep.ap[:, ch * 512:(ch + 1) * 512], pst[:, 0:512], reads=[pst], writes=[grep],
                   tag=f"grev{e_}_{ch}")
        for fq in range(4):
            for fg in range(4):
                wb = k.wst[fg % 2]
                c0 = fq * 1024 + fg * 256
                src = k.i_w1.ap[e_ * D:(e_ + 1) * D, c0:c0 + 256].rearrange("(cc p) q -> p cc q", p=128)
                p.dma("pool", lambda e, wb=wb, src=src: e.dma_start(
                    out=wb.ap.rearrange("p (cc q) -> p cc q", cc=8), in_=src),
                    writes=[wb], tag=f"ldw1_{e_}_{fq}{fg}")
                for fo in range(2):
                    ftq = fg * 2 + fo
                    b1c = k.col(MB1C + e_ * 32 + fq * 8 + ftq)
                    for tch in range(4):
                        gel = gelsets[(ftq * 4 + tch) % 2]
                        pst = (k.psA, k.psB)[ftq % 2]
                        reg = pst[:, (tch % 2) * 512:(tch % 2) * 512 + 512]
                        for c in range(8):
                            p.mm(reg, wb[:, c * 256 + fo * 128:c * 256 + fo * 128 + 128],
                                 x3b[c].ap[:, tch * 512:(tch + 1) * 512], c == 0, c == 7,
                                 reads=[wb, x3b[c]], writes=[pst], tag=f"o1_{e_}_{fq}{ftq}_{tch}")
                        p.act(gel[0].ap, reg, AF.Identity, bias=b1c,
                              reads=[pst, k.consts], writes=[gel[0]], tag=f"gx{e_}_{fq}{ftq}_{tch}")
                        p.act(gel[1].ap, reg, AF.Square, bias=b1c, reads=[pst, k.consts], writes=[gel[1]],
                              tag=f"ga{e_}_{fq}{ftq}_{tch}")
                        p.ts("dve", gel[1].ap, gel[1].ap, GELU_C, 1.0, ALU.mult, ALU.add,
                             reads=[gel[1]], writes=[gel[1]], tag=f"gb{e_}")
                        p.tt("dve", gel[2].ap, gel[1].ap, gel[0].ap, ALU.mult,
                             reads=[gel[1], gel[0]], writes=[gel[2]], tag=f"gc{e_}")
                        p.act(gel[2].ap, gel[2].ap, AF.Sigmoid, scale=GELU_S,
                              reads=[gel[2]], writes=[gel[2]], tag=f"gs{e_}")
                        p.tt("dve", gel[0].ap, gel[2].ap, gel[0].ap, ALU.mult,
                             reads=[gel[2], gel[0]], writes=[gel[0]], tag=f"gg{e_}")
                        h1gd = h1g[ftq // 4].ap[:, (ftq % 4) * 2048 + tch * 512:(ftq % 4) * 2048 + tch * 512 + 512]
                        p.tt("pool", h1gd, gel[0].ap, grep.ap[:, tch * 512:(tch + 1) * 512], ALU.mult,
                             reads=[gel[0], grep], writes=[h1g[ftq // 4]], tag=f"hg{e_}_{fq}{ftq}_{tch}")
            for f2 in range(8):
                wb = k.wst[f2 % 2]
                src = k.i_w2.ap[e_ * FF + fq * 1024: e_ * FF + (fq + 1) * 1024,
                                128 * f2:128 * (f2 + 1)].rearrange("(cc p) q -> p cc q", p=128)
                p.dma("pool", lambda e, wb=wb, src=src: e.dma_start(
                    out=wb.ap[:, 0:1024].rearrange("p (cc q) -> p cc q", cc=8), in_=src),
                    writes=[wb], tag=f"ldw2_{e_}_{fq}{f2}")
                for tch in range(4):
                    pst = (k.psC, k.psD)[f2 % 2]
                    for cq in range(8):
                        h1s = h1g[cq // 4].ap[:, (cq % 4) * 2048 + tch * 512:(cq % 4) * 2048 + tch * 512 + 512]
                        p.mm(pst[:, 0:512], wb[:, cq * 128:(cq + 1) * 128], h1s,
                             cq == 0, cq == 7 and fq < 3, reads=[wb, h1g[cq // 4]], writes=[pst],
                             tag=f"o2_{e_}_{fq}{f2}_{tch}")
                    if fq == 3:
                        p.mm(pst[:, 0:512], b2sb[0:8, f2 * 128:(f2 + 1) * 128],
                             g8r.ap[0:8, tch * 512:(tch + 1) * 512], False, True,
                             reads=[b2sb, g8r], writes=[pst], tag=f"b2_{e_}_{f2}_{tch}")
                    p.tt("dve", k.h[f2][:, tch * 512:(tch + 1) * 512], k.h[f2][:, tch * 512:(tch + 1) * 512],
                         pst[:, 0:512], ALU.add, reads=[k.h[f2], pst], writes=[k.h[f2]],
                         tag=f"hadd{e_}_{fq}{f2}_{tch}")


def emit_outer(k):
    p = k.p
    xtm = [k.sub(f"otm{i}", i, 0, 1024, F32) for i in range(2)]
    for i in range(16):
        b = xtm[i % 2]
        src = k.i_x[3 + 128 * i: 3 + 128 * (i + 1), :]
        p.dma("sp", lambda e, b=b, src=src: e.dma_start(out=b.ap, in_=src), writes=[b], tag=f"rlx{i}")
        ps = (k.psA, k.psB)[i % 2]
        for c in range(8):
            p.tr(ps[:, 128 * c:128 * (c + 1)], b.ap[:, 128 * c:128 * (c + 1)], k.ident.ap,
                 reads=[b, k.ident], writes=[ps], tag=f"rtr{i}_{c}")
        for c in range(8):
            p.tt("dve", k.h[c][:, 128 * i:128 * (i + 1)], k.h[c][:, 128 * i:128 * (i + 1)],
                 ps[:, 128 * c:128 * (c + 1)], ALU.add, reads=[k.h[c], ps], writes=[k.h[c]],
                 tag=f"radd{i}_{c}")
    for ci in range(4):
        a = ci * 512
        emit_ln(k, f"ln4c{ci}",
                lambda c, aa, w, a=a: k.h[c][:, a + aa: a + aa + w], lambda c: k.h[c],
                [(lambda c, aa, w, a=a: k.h[c][:, a + aa: a + aa + w], lambda c: k.h[c])],
                None, 1e-5, width=512)


def emit_ffn(k):
    p = k.p
    xpl = k.sbuf("xpl", [128, 8], F32)
    for c in range(8):
        p.copy("dve", xpl[:, c:c + 1], k.h[c][:, T - 1:T], reads=[k.h[c]], writes=[xpl], tag=f"xpl{c}")
    p.dma("sp", lambda e: e.dma_start(out=k.g_xp_in.ap, in_=xpl.ap), reads=[xpl], writes=[k.g_xp_in], tag="stxp")
    p.coll(lambda e: e.collective_compute("AllGather", ALU.bypass, replica_groups=[list(range(NCORES))],
                                          ins=[k.g_xp_in.ap.opt()], outs=[k.g_xp_out.ap.opt()]),
           reads=[k.g_xp_in], writes=[k.g_xp_out], tag="AG_xp")
    xpg = k.sbuf("xpg", [128, 64], F32)
    p.dma("sp", lambda e: e.dma_start(
        out=xpg.ap.rearrange("p (r cc) -> p r cc", r=8),
        in_=k.g_xp_out.ap.rearrange("(r p) cc -> p r cc", p=128)),
        reads=[k.g_xp_out], writes=[xpg], tag="ldxp")
    xpb = k.sbuf("xpb", [128, 8], F32)
    p.op("dve", lambda e: e.memset(xpb.ap, 0.0), writes=[xpb])
    for r in range(NCORES):
        p.stt("dve", xpb.ap, xpg.ap.rearrange("p (r cc) -> p r cc", r=8)[:, r, :], k.col(SELP + r),
              xpb.ap, ALU.mult, ALU.add, reads=[xpg, k.consts, xpb], writes=[xpb], tag=f"xpk{r}")
    xrb = [k.sub(f"xrb{c}", 4 + c // 4, (c % 4) * 2048, 2048, BF16) for c in range(8)]
    xkb = [k.sub(f"xkb{c}", c // 4, (c % 4) * 2048, 2048, BF16) for c in range(8)]
    dsc = k.sub("ffd", 2, 0, 2048, F32)
    for c in range(8):
        p.tt("dve", dsc.ap[:, 1:2048], k.h[c][:, 1:2048], k.h[c][:, 0:2047], ALU.subtract,
             reads=[k.h[c]], writes=[dsc], tag=f"fd{c}")
        p.tt("dve", dsc.ap[:, 0:1], k.h[c][:, 0:1], xpb[:, c:c + 1], ALU.subtract,
             reads=[k.h[c], xpb], writes=[dsc], tag=f"fd0{c}")
        p.stt("dve", xrb[c].ap[:, 1:2048], dsc.ap[:, 1:2048], k.col(MURC + c), k.h[c][:, 0:2047],
              ALU.mult, ALU.add, reads=[dsc, k.consts, k.h[c]], writes=[xrb[c]], tag=f"fr{c}")
        p.stt("dve", xkb[c].ap[:, 1:2048], dsc.ap[:, 1:2048], k.col(MUKC + c), k.h[c][:, 0:2047],
              ALU.mult, ALU.add, reads=[dsc, k.consts, k.h[c]], writes=[xkb[c]], tag=f"fk{c}")
        p.stt("dve", xrb[c].ap[:, 0:1], dsc.ap[:, 0:1], k.col(MURC + c), xpb[:, c:c + 1],
              ALU.mult, ALU.add, reads=[dsc, k.consts, xpb], writes=[xrb[c]], tag=f"fr0{c}")
        p.stt("dve", xkb[c].ap[:, 0:1], dsc.ap[:, 0:1], k.col(MUKC + c), xpb[:, c:c + 1],
              ALU.mult, ALU.add, reads=[dsc, k.consts, xpb], writes=[xkb[c]], tag=f"fk0{c}")
    wT = k.sub("ffwT", 3, 0, 8192, BF16)

    def build_wT(widram, nm):
        for di in range(8):
            wb = k.wst[di % 2]
            src = widram.ap[128 * di:128 * (di + 1), :].rearrange("p (cc q) -> p cc q", cc=8)
            p.dma("pool", lambda e, wb=wb, src=src: e.dma_start(
                out=wb.ap[:, 0:1024].rearrange("p (cc q) -> p cc q", cc=8), in_=src),
                writes=[wb], tag=f"ld{nm}{di}")
            for cj in range(8):
                p.tr(k.psT[:, cj * 128:(cj + 1) * 128], wb[:, cj * 128:(cj + 1) * 128], k.identb.ap,
                     reads=[wb, k.identb], writes=[k.psT], tag=f"{nm}tr{di}_{cj}")
            p.copy("dve", wT.ap.rearrange("p (cj di q) -> p cj di q", cj=8, di=8)[:, :, di, :],
                   k.psT[:, 0:1024].rearrange("p (cj q) -> p cj q", cj=8),
                   reads=[k.psT], writes=[wT], tag=f"{nm}ev{di}")

    def ffn_mm(xin, di, tch, pst):
        for cj in range(8):
            p.mm(pst[:, 0:512], wT.ap[:, (cj * 8 + di) * 128:(cj * 8 + di) * 128 + 128],
                 xin[cj].ap[:, tch * 512:(tch + 1) * 512], cj == 0, cj == 7,
                 reads=[wT, xin[cj]], writes=[pst], tag=f"fmm{di}_{tch}")

    rq = [k.sub(f"ffrq{d}", 2, d * 512, 512, BF16) for d in range(8)]
    kq = [k.sub(f"ffkq{d}", 2, 4096 + d * 512, 512, BF16) for d in range(8)]
    for tch in range(4):
        build_wT(k.i_wr, f"wr{tch}")
        for di in range(8):
            pst = (k.psC, k.psD)[di % 2]
            ffn_mm(xrb, di, tch, pst)
            p.act(rq[di].ap, pst[:, 0:512], AF.Sigmoid, reads=[pst], writes=[rq[di]], tag=f"sg{tch}{di}")
        build_wT(k.i_wkf, f"wk{tch}")
        for di in range(8):
            pst = (k.psC, k.psD)[di % 2]
            ffn_mm(xkb, di, tch, pst)
            p.ts("dve", k.ln_t1.ap, pst[:, 0:512], 0.0, None, ALU.max, ALU.bypass,
                 reads=[pst], writes=[k.ln_t1], tag=f"rel{tch}{di}")
            p.act(kq[di].ap, k.ln_t1.ap, AF.Square, reads=[k.ln_t1], writes=[kq[di]], tag=f"ksq{tch}{di}")
        build_wT(k.i_wvf, f"wv{tch}")
        for di in range(8):
            pst = (k.psC, k.psD)[di % 2]
            for cj in range(8):
                p.mm(pst[:, 0:512], wT.ap[:, (cj * 8 + di) * 128:(cj * 8 + di) * 128 + 128],
                     kq[cj].ap, cj == 0, cj == 7, reads=[wT, kq[cj]], writes=[pst], tag=f"vmm{tch}{di}")
            p.tt("dve", k.ln_t1.ap, pst[:, 0:512], rq[di].ap, ALU.mult,
                 reads=[pst, rq[di]], writes=[k.ln_t1], tag=f"fmul{tch}{di}")
            p.tt("dve", k.h[di][:, tch * 512:(tch + 1) * 512], k.h[di][:, tch * 512:(tch + 1) * 512],
                 k.ln_t1.ap, ALU.add, reads=[k.h[di], k.ln_t1], writes=[k.h[di]], tag=f"fadd{tch}{di}")
    for ci in range(4):
        a = ci * 512
        emit_ln(k, f"ln5c{ci}",
                lambda c, aa, w, a=a: k.h[c][:, a + aa: a + aa + w], lambda c: k.h[c],
                [(lambda c, aa, w, a=a: k.h[c][:, a + aa: a + aa + w], lambda c: k.h[c])],
                None, 1e-5, width=512)


def build_program(tap=None, debug=False, upto=None):
    nc = bass.Bass("TRN2", target_bir_lowering=False)
    k = Kern(nc, tap)
    emit_setup(k)
    emit_input(k)
    if tap:
        tap("input", k)
    stages = [("mamba", emit_mamba), ("tc", emit_tc), ("wlam", emit_wlam),
              ("moe", emit_moe), ("outer", emit_outer), ("ffn", emit_ffn)]
    for name, fn in stages:
        if upto is not None and name not in upto:
            continue
        fn(k)
        if tap:
            tap(name, k)
    emit_output_stub(k)
    k.p.finalize(debug=debug)
    return nc, k


def emit_output_stub(k):
    p = k.p
    otm = [k.sub(f"otm{i}", i, 2048, 1024, F32) for i in range(2)]
    for i in range(16):
        ps = (k.psA, k.psB)[i % 2]
        ob = otm[i % 2]
        for c in range(8):
            p.tr(ps[:, 128 * c:128 * (c + 1)], k.h[c][:, 128 * i:128 * (i + 1)], k.ident.ap,
                 reads=[k.h[c], k.ident], writes=[ps], tag=f"otr{i}_{c}")
        p.copy("dve", ob.ap, ps[:, 0:1024], reads=[ps], writes=[ob], tag=f"oev{i}")
        p.dma("sp", lambda e, ob=ob, i=i: e.dma_start(out=k.out.ap[128 * i:128 * (i + 1), :], in_=ob.ap),
              reads=[ob], writes=[k.out], tag=f"ost{i}")


def pack_cols(I):
    cols = np.zeros((128, COLS_W), np.float32)
    for i in range(4):
        for kk in range(8):
            cols[:, LNG + 8 * i + kk] = I["L0_attn_ln_g"][i][128 * kk:128 * (kk + 1)]
            cols[:, LNB + 8 * i + kk] = I["L0_attn_ln_b"][i][128 * kk:128 * (kk + 1)]
    for kk in range(16):
        cols[:, ALOGC + kk] = I["L0_attn_m_alog"][128 * kk:128 * (kk + 1)]
        cols[:, WDTC + kk] = I["L0_attn_m_wdt"][128 * kk:128 * (kk + 1)]
        cols[:, DTBC + kk] = I["L0_attn_m_dtb"][128 * kk:128 * (kk + 1)]
        cols[:, DSKC + kk] = I["L0_attn_m_dskip"][128 * kk:128 * (kk + 1)]
        for i in range(4):
            cols[:, CONVC + i * 16 + kk] = I["L0_attn_m_conv"][i][128 * kk:128 * (kk + 1)]
    for kk in range(8):
        cols[:, TCBC + kk] = I["L0_attn_tc_b"][128 * kk:128 * (kk + 1)]
        cols[:, MURC + kk] = I["L0_ffn_mu_r"][128 * kk:128 * (kk + 1)]
        cols[:, MUKC + kk] = I["L0_ffn_mu_k"][128 * kk:128 * (kk + 1)]
        for i in range(4):
            cols[:, WCVC + i * 8 + kk] = I["L0_attn_wlam_conv"][i][128 * kk:128 * (kk + 1)]
    cols[0:4, RBC] = I["L0_attn_router_b"]
    cols[:, THC] = float(I["L0_attn_tc_theta"])
    for e in range(8):
        for kf in range(32):
            cols[:, MB1C + e * 32 + kf] = I["L0_attn_moe_b1"][e][128 * kf:128 * (kf + 1)]
        for kk in range(8):
            cols[:, MB2C + e * 8 + kk] = I["L0_attn_moe_b2"][e][128 * kk:128 * (kk + 1)]
    return cols


def make_inmaps(inputs):
    I = {kk: np.asarray(v) for kk, v in inputs.items()}
    x = np.ascontiguousarray(I["L0_attn_hidden_states"][0])
    cols_base = pack_cols(I)
    tri = np.asarray(np.triu(np.ones((CK, CK))), np.float32)
    ident = np.eye(128, dtype=np.float32)
    shared = {
        "m_win": I["L0_attn_m_win"], "m_wbc": I["L0_attn_m_wbc"], "m_wout": I["L0_attn_m_wout"],
        "tc_w": I["L0_attn_tc_w"], "w_q": I["L0_attn_w_q"], "w_k": I["L0_attn_w_k"],
        "w_v": I["L0_attn_w_v"], "w_o": I["L0_attn_w_o"], "moe_wg": I["L0_attn_moe_wg"],
        "moe_w1": I["L0_attn_moe_w1"].reshape(E * D, FF),
        "moe_w2": I["L0_attn_moe_w2"].reshape(E * FF, D),
        "ffn_wr": I["L0_ffn_W_r"], "ffn_wk": I["L0_ffn_W_k"], "ffn_wv": I["L0_ffn_W_v"],
        "router_w": I["L0_attn_router_w"], "identity": ident, "tri": tri,
        "moe_b2r": I["L0_attn_moe_b2"],
        "ohot": np.repeat(np.eye(8, dtype=np.float32), 128, axis=1).reshape(8, 8, 128).transpose(0, 1, 2).reshape(8, 1024) if False else np.kron(np.eye(8, dtype=np.float32), np.ones((1, 128), np.float32)),
    }
    shared = {kk: np.ascontiguousarray(np.asarray(v, np.float32)) for kk, v in shared.items()}
    in_maps = []
    for r in range(NCORES):
        t0 = r * T
        xp = np.zeros((T + 3, D), np.float32)
        xp[3:] = x[t0:t0 + T]
        if r > 0:
            xp[0:3] = x[t0 - 3:t0]
        halo = np.zeros((D, 4), np.float32)
        halo[:, 0:3] = xp[0:3].T
        cols = cols_base.copy()
        for j in range(8):
            cols[:, SELR + j] = 1.0 if j == r else 0.0
            cols[:, SELP + j] = 1.0 if j == r - 1 else 0.0
            cols[:, SELN + j] = 1.0 if j == r + 1 else 0.0
        m = dict(shared)
        m["x_pad"] = xp
        m["x_halo_fm"] = np.ascontiguousarray(halo)
        m["cols"] = np.ascontiguousarray(cols)
        in_maps.append(m)
    return in_maps


def kernel(**inputs):
    nc, _ = build_program()
    in_maps = make_inmaps(inputs)
    res = run_bass_kernel_spmd(nc, in_maps, core_ids=list(range(NCORES)))
    return np.concatenate([r["y"] for r in res.results], axis=0)[None]


# version 06
# speedup vs baseline: 1.0748x; 1.0748x over previous
import contextlib
import numpy as np
import concourse.bass as bass
import concourse.mybir as mybir
from concourse.bass_utils import run_bass_kernel_spmd

F32 = mybir.dt.float32
BF16 = mybir.dt.bfloat16
AF = mybir.ActivationFunctionType
ALU = mybir.AluOpType
AX = mybir.AxisListType

STREAMS = ("pe", "dve", "act", "pool", "sp")

NCORES = 8
T = 2048
D = 1024
DI = 2048
NST = 64
CK = 64
SEC = 256
NSEC = T // SEC
NCHS = SEC // CK
B3 = SEC + 3
NH = 16
HD = 64
TP = T // 2
E = 8
FF = 4096
LOG_THR = float(np.log(np.float32(0.3) / np.float32(0.7)))
GELU_S = float(2.0 * 0.7978845608028654)
GELU_C = 0.044715
DBG = []


class Buf:
    __slots__ = ("name", "t", "lw", "readers")

    def __init__(self, name, t):
        self.name, self.t = name, t
        self.lw, self.readers = None, []

    @property
    def root(self):
        return self

    def __getitem__(self, idx):
        return self.t[idx]

    @property
    def ap(self):
        return self.t[:]


class SubBuf:
    __slots__ = ("name", "parent", "off", "width", "dtype")

    def __init__(self, name, parent, off, width, dtype):
        self.name, self.parent, self.off, self.width, self.dtype = name, parent, off, width, dtype

    def __getitem__(self, idx):
        return self.ap[idx]

    @property
    def root(self):
        return self.parent

    @property
    def ap(self):
        a = self.parent.t[:].bitcast(self.dtype)
        return a[:, self.off:self.off + self.width]


class Op:
    __slots__ = ("stream", "sidx", "fn", "dma", "coll", "inc_marked",
                 "milestone", "waits", "drain_before", "tag")

    def __init__(self, stream, sidx, fn, dma, coll, tag):
        self.stream, self.sidx, self.fn = stream, sidx, fn
        self.dma, self.coll = dma, coll
        self.inc_marked = dma or coll
        self.milestone = None
        self.waits = {}
        self.drain_before = False
        self.tag = tag


class Prog:
    def __init__(self, nc):
        self.nc = nc
        self.es = contextlib.ExitStack()
        self.sops = {s: [] for s in STREAMS}
        self.nops = 0
        self._names = set()
        self.csem = {s: self.es.enter_context(nc.semaphore("c_" + s)) for s in STREAMS}
        self.NQL = 8
        self.qsem = {s: [self.es.enter_context(nc.semaphore(f"q_{s}{i}")) for i in range(self.NQL)]
                     for s in ("sp", "act", "pool")}
        self.ccsem = self.es.enter_context(nc.semaphore("ccq"))
        self._drain_cover = {s: -1 for s in STREAMS}
        self._qcnt = {s: 0 for s in ("sp", "act", "pool")}
        self._cccnt = 0

    def _nm(self, name):
        assert name not in self._names, name
        self._names.add(name)
        return name

    def sbuf(self, name, shape, dtype):
        return Buf(name, self.es.enter_context(self.nc.sbuf_tensor(self._nm(name), list(shape), dtype)))

    def psum(self, name, shape, dtype):
        return Buf(name, self.es.enter_context(self.nc.psum_tensor(self._nm(name), list(shape), dtype)))

    def dram_in(self, name, shape, dtype):
        return Buf(name, self.nc.dram_tensor(self._nm(name), list(shape), dtype, kind="ExternalInput").ap())

    def dram_out(self, name, shape, dtype):
        return Buf(name, self.nc.dram_tensor(self._nm(name), list(shape), dtype, kind="ExternalOutput").ap())

    def dram_tmp(self, name, shape, dtype):
        return Buf(name, self.nc.dram_tensor(self._nm(name), list(shape), dtype).ap())

    def op(self, stream, fn, reads=(), writes=(), dma=False, coll=False, tag=""):
        o = Op(stream, len(self.sops[stream]), fn, dma, coll, tag)
        self.nops += 1
        self.sops[stream].append(o)
        if dma:
            qn = self._qcnt[stream]
            self._qcnt[stream] = qn + 1
            o.milestone = (f"q_{stream}{qn % self.NQL}", 16 * (qn // self.NQL + 1))
        elif coll:
            self._cccnt += 1
            o.milestone = ("cc", self._cccnt)
        deps = []
        for b in reads:
            b = b.root
            if b.lw is not None:
                deps.append(b.lw)
        wroots = [b.root for b in writes]
        for b in wroots:
            if b.lw is not None:
                deps.append(b.lw)
            deps.extend(b.readers)
        best = {}
        for d in deps:
            if d is o:
                continue
            kk = d.milestone[0] if (d.dma or d.coll) else (d.stream,)
            prev = best.get(kk)
            if prev is None or d.sidx > prev.sidx:
                best[kk] = d
        need_drain = False
        for d in best.values():
            if d.stream == stream and not d.dma and not d.coll:
                if stream == "pe":
                    continue
                if d.sidx > self._drain_cover[stream]:
                    need_drain = True
                continue
            d.inc_marked = True
            o.waits[d] = True
        if need_drain:
            o.drain_before = True
            self._drain_cover[stream] = o.sidx - 1
        for b in reads:
            b.root.readers.append(o)
        for b in wroots:
            b.lw = o
            b.readers = []
        return o

    def dma(self, stream, fn, reads=(), writes=(), tag=""):
        assert stream in ("sp", "act", "pool")
        return self.op(stream, fn, reads, writes, dma=True, tag=tag)

    def coll(self, fn, reads=(), writes=(), tag=""):
        return self.op("pool", fn, reads, writes, coll=True, tag=tag)

    def tt(self, st, out, a, b, alu, reads, writes, tag=""):
        self.op(st, lambda e: e.tensor_tensor(out, a, b, alu), reads, writes, tag=tag)

    def ts(self, st, out, a, s1, s2, op0, op1, reads, writes, tag=""):
        self.op(st, lambda e: e.tensor_scalar(out, a, s1, s2, op0, op1), reads, writes, tag=tag)

    def stt(self, st, out, a, s, b, op0, op1, reads, writes, tag=""):
        self.op(st, lambda e: e.scalar_tensor_tensor(out, a, s, b, op0, op1), reads, writes, tag=tag)

    def act(self, out, a, fn, reads, writes, scale=None, bias=None, accum=None, tag=""):
        kw = {}
        if scale is not None:
            kw["scale"] = scale
        if bias is not None:
            kw["bias"] = bias
        if accum is not None:
            kw["accum_out"] = accum
        self.op("act", lambda e: e.activation(out, a, fn, **kw), reads, writes, tag=tag)

    def copy(self, st, out, a, reads, writes, tag=""):
        self.op(st, lambda e: e.tensor_copy(out, a), reads, writes, tag=tag)

    def mm(self, out, lhsT, rhs, start, stop, reads, writes, tag=""):
        self.op("pe", lambda e: e.matmul(out, lhsT, rhs, start=start, stop=stop), reads, writes, tag=tag)

    def tr(self, out, in_, ident, reads, writes, tag=""):
        self.op("pe", lambda e: e.transpose(out, in_, ident), reads, writes, tag=tag)

    def finalize(self, debug=False):
        totals = {}
        qlane_tot = {}
        for s in STREAMS:
            cc = 0
            for o in self.sops[s]:
                if o.dma or o.coll or not o.inc_marked:
                    continue
                cc += 1
                o.milestone = ("c_" + s, cc)
            totals[s] = (self._qcnt.get(s, 0) * 16, self._cccnt if s == "pool" else 0, cc)
        for s in ("sp", "act", "pool"):
            qn = self._qcnt[s]
            qlane_tot[s] = [16 * ((qn - 1 - i) // self.NQL + 1) if i < min(qn, self.NQL) else 0
                            for i in range(self.NQL)]
        sem_of = {("c_" + s): self.csem[s] for s in STREAMS}
        for s in ("sp", "act", "pool"):
            for i in range(self.NQL):
                sem_of[f"q_{s}{i}"] = self.qsem[s][i]
        sem_of["cc"] = self.ccsem
        nwaits = [0]

        def emit(eng, s):
            hw = {}
            for o in self.sops[s]:
                if o.drain_before:
                    eng.drain()
                byname = {}
                for d in o.waits:
                    nm, v = d.milestone
                    if byname.get(nm, 0) < v:
                        byname[nm] = v
                for nm in sorted(byname):
                    v = byname[nm]
                    if hw.get(nm, 0) >= v:
                        continue
                    eng.wait_ge(sem_of[nm], v)
                    hw[nm] = v
                    nwaits[0] += 1
                inst = o.fn(eng)
                if o.inc_marked:
                    nm, _ = o.milestone
                    inst.then_inc(sem_of[nm], 16 if o.dma else 1)

        block = self.es.enter_context(self.nc.Block())

        @block.tensor
        def _(e):
            emit(e, "pe")

        @block.vector
        def _(e):
            emit(e, "dve")

        @block.scalar
        def _(e):
            emit(e, "act")

        @block.gpsimd
        def _(e):
            emit(e, "pool")

        @block.sync
        def _(e):
            emit(e, "sp")
            for s in ("sp", "act", "pool"):
                for i in range(self.NQL):
                    if qlane_tot[s][i]:
                        e.wait_ge(self.qsem[s][i], qlane_tot[s][i])
            if totals["pool"][1]:
                e.wait_ge(self.ccsem, totals["pool"][1])

        if debug:
            print(f"Prog: {self.nops} ops, {nwaits[0]} waits; totals={totals}")
        return totals


def ceil_div(a, b):
    return -(-a // b)


LNG, LNB = 0, 32
ALOGC, WDTC, DTBC, DSKC = 64, 80, 96, 112
TCBC, MURC, MUKC = 128, 136, 144
SELR, SELP, SELN = 152, 160, 168
CONVC = 176
RBC, THC = 240, 241
MB1C, MB2C = 256, 512
WCVC = 576
COLS_W = 640
KVW = 1040


class Kern:
    def __init__(self, nc, tap=None):
        self.nc = nc
        self.p = Prog(nc)
        self.tap = tap
        p = self.p
        s = p.sbuf

        self.i_x = p.dram_in("x_pad", [T + 3, D], F32)
        self.i_xhalo = p.dram_in("x_halo_fm", [D, 4], F32)
        self.i_win = p.dram_in("m_win", [D, 2 * DI], F32)
        self.i_wbc = p.dram_in("m_wbc", [DI, 2 * NST], F32)
        self.i_wout = p.dram_in("m_wout", [DI, D], F32)
        self.i_tcw = p.dram_in("tc_w", [D, D], F32)
        self.i_wq = p.dram_in("w_q", [D, D], F32)
        self.i_wk = p.dram_in("w_k", [D, D], F32)
        self.i_wv = p.dram_in("w_v", [D, D], F32)
        self.i_wo = p.dram_in("w_o", [D, D], F32)
        self.i_wg = p.dram_in("moe_wg", [D, E], F32)
        self.i_w1 = p.dram_in("moe_w1", [E * D, FF], F32)
        self.i_w2 = p.dram_in("moe_w2", [E * FF, D], F32)
        self.i_wr = p.dram_in("ffn_wr", [D, D], F32)
        self.i_wkf = p.dram_in("ffn_wk", [D, D], F32)
        self.i_wvf = p.dram_in("ffn_wv", [D, D], F32)
        self.i_rw = p.dram_in("router_w", [D, 4], F32)
        self.i_b2r = p.dram_in("moe_b2r", [8, 1024], F32)
        self.i_ohot = p.dram_in("ohot", [8, 1024], F32)
        self.i_cols = p.dram_in("cols", [128, COLS_W], F32)
        self.i_ident = p.dram_in("identity", [128, 128], F32)
        self.i_tri = p.dram_in("tri", [CK, CK], F32)
        self.out = p.dram_out("y", [T, D], F32)

        self.g_pool_in = p.dram_tmp("cc_pool_in", [128, 8], F32)
        self.g_pool_out = p.dram_tmp("cc_pool_out", [128, 8], F32)
        self.g_mst_in = p.dram_tmp("cc_mst_in", [DI, NST], F32)
        self.g_mp_in = p.dram_tmp("cc_mp_in", [128, 16], F32)
        self.g_mp_out = p.dram_tmp("cc_mp_out", [NCORES * 128, 16], F32)
        self.g_mst_out = p.dram_tmp("cc_mst_out", [NCORES * DI, NST], F32)
        self.g_kv_in = p.dram_tmp("cc_kv_in", [2 * D + 16, KVW], BF16)
        self.g_kv_out = p.dram_tmp("cc_kv_out", [NCORES * (2 * D + 16), KVW], BF16)
        self.g_yw_in = p.dram_tmp("cc_yw_in", [D, 4], F32)
        self.g_yw_out = p.dram_tmp("cc_yw_out", [NCORES * D, 4], F32)
        self.g_xp_in = p.dram_tmp("cc_xp_in", [128, 8], F32)
        self.g_xp_out = p.dram_tmp("cc_xp_out", [NCORES * 128, 8], F32)

        self.h = [s(f"h{c}", [128, T], F32) for c in range(8)]
        self.consts = s("consts", [128, COLS_W], F32)
        self.ident = s("identf", [128, 128], F32)
        self.identb = s("identb", [128, 128], BF16)
        self.onesb = s("onesb", [128, 128], BF16)
        self.onesf = s("onesf", [128, 256], F32)
        self.trib = s("trib", [CK, CK], F32)
        self.masks = s("masksrep", [128, 4], F32)
        self.rows = s("rows", [1, 1536], F32)

        _arw = [4144, 4096, 4096, 4096, 4160, 4096]
        self.ar = [s(f"ar{i}", [128, _arw[i]], F32) for i in range(6)]

        self.psA = p.psum("psA", [128, 1024], F32)
        self.psB = p.psum("psB", [128, 1024], F32)
        self.psC = p.psum("psC", [128, 512], F32)
        self.psD = p.psum("psD", [128, 512], F32)
        self.psT = p.psum("psT", [128, 2048], BF16)
        self._psi = 0

        self.ln_xb = s("ln_xb", [128, 512], BF16)
        self.ln_xb2 = s("ln_xb2", [128, 512], BF16)
        self.ln_sq = s("ln_sq", [128, 512], BF16)
        self.ln_sq2 = s("ln_sq2", [128, 512], BF16)
        self.ln_t1 = s("ln_t1", [128, 512], F32)

        self.wst = [s(f"wst{i}", [128, 2048], BF16) for i in range(2)]

        self.xh = s("xh", [128, 32], F32)
        self.mlnb = [s(f"mlnb{c}", [128, B3], BF16) for c in range(8)]
        self.mBb = s("mBb", [64, SEC], BF16)
        self.mCb = s("mCb", [64, SEC], BF16)
        self.mwtm = s("mwtm", [64, 2048], BF16)
        self.mhtm = s("mhtm", [64, 2048], BF16)
        self.mbtm = s("mbtm", [64, 64], BF16)
        self.mhbf = s("mhbf", [128, 16 * 64], BF16)
        self.mhst = s("mhst", [128, 16 * 64], F32)
        self.mqcol = s("mqcol", [128, 16], F32)
        self.mdtot = s("mdtot", [128, 16], F32)
        self.mec = s("mec", [128, 16 * NCHS], F32)
        self.mpc = s("mpc", [128, 16 * NCHS], F32)
        self.mmc = s("mmc", [128, 16 * NCHS], F32)
        self.mdh = s("mdh", [128, 16 * NCHS], F32)
        self.mdf = s("mdf", [128, 16 * NCHS], F32)
        self.mgm = s("mgm", [64, CK], BF16)
        self.pooled = s("pooled", [128, 8], F32)
        self.poolg = s("poolg", [128, 8], F32)
        self.rwsb = s("rwsb", [128, 32], F32)
        self.lg = s("lg", [4, 1], F32)
        self.mrow = s("mrow", [1, 4], F32)

    def sbuf(self, name, shape, dtype):
        return self.p.sbuf(name, shape, dtype)

    def next_ps(self):
        b = (self.psA, self.psB)[self._psi % 2]
        self._psi += 1
        return b

    def sub(self, name, ai, off, width, dtype):
        return SubBuf(name, self.ar[ai], off, width, dtype)

    def col(self, idx, n=1):
        return self.consts[:, idx:idx + n]

    def cbc(self, idx, nblk, w):
        return self.consts[:, idx:idx + nblk].unsqueeze(2).broadcast_to([128, nblk, w])


def blkv(buf, nblk, off=0, w=None):
    a = buf.ap
    q = a.shape[1] // nblk
    a = a.rearrange("p (b q) -> p b q", b=nblk)
    if w is None:
        return a
    return a[:, :, off:off + w]


def emit_setup(k):
    p = k.p
    p.dma("sp", lambda e: e.dma_start(out=k.consts.ap, in_=k.i_cols.ap), writes=[k.consts], tag="ld_cols")
    p.dma("sp", lambda e: e.dma_start(out=k.ident.ap, in_=k.i_ident.ap), writes=[k.ident])
    p.dma("pool", lambda e: e.dma_start(out=k.identb.ap, in_=k.i_ident.ap), writes=[k.identb])
    p.dma("sp", lambda e: e.dma_start(out=k.trib.ap, in_=k.i_tri.ap), writes=[k.trib])
    p.op("dve", lambda e: e.memset(k.onesf.ap, 1.0), writes=[k.onesf])
    p.op("dve", lambda e: e.memset(k.onesb.ap, 1.0), writes=[k.onesb])
    acol = k.consts[:, ALOGC:ALOGC + 16]
    p.act(acol, acol, AF.Exp, reads=[k.consts], writes=[k.consts], tag="A_exp")
    p.ts("dve", acol, acol, -1.0, None, ALU.mult, ALU.bypass, reads=[k.consts], writes=[k.consts], tag="A_neg")


def emit_input(k):
    p = k.p
    xtm = [k.sub(f"xtm{i}", i, 0, 1024, F32) for i in range(2)]
    for i in range(16):
        b = xtm[i % 2]
        src = k.i_x[3 + 128 * i: 3 + 128 * (i + 1), :]
        p.dma("sp", lambda e, b=b, src=src: e.dma_start(out=b.ap, in_=src), writes=[b], tag=f"ldx{i}")
        ps = (k.psA, k.psB)[i % 2]
        for c in range(8):
            p.tr(ps[:, 128 * c:128 * (c + 1)], b.ap[:, 128 * c:128 * (c + 1)], k.ident.ap,
                 reads=[b, k.ident], writes=[ps], tag=f"trx{i}_{c}")
        for c in range(8):
            p.copy("dve", k.h[c][:, 128 * i:128 * (i + 1)], ps[:, 128 * c:128 * (c + 1)],
                   reads=[ps], writes=[k.h[c]], tag=f"evx{i}_{c}")
    for c in range(8):
        p.op("dve", lambda e, c=c: e.tensor_reduce(k.pooled[:, c:c + 1], k.h[c].ap, AX.X, ALU.add),
             reads=[k.h[c]], writes=[k.pooled], tag=f"pool{c}")
    gi, go = k.g_pool_in, k.g_pool_out
    p.dma("sp", lambda e: e.dma_start(out=gi.ap, in_=k.pooled.ap), reads=[k.pooled], writes=[gi])
    p.coll(lambda e: e.collective_compute("AllReduce", ALU.add, replica_groups=[list(range(NCORES))],
                                          ins=[gi.ap.opt()], outs=[go.ap.opt()]),
           reads=[gi], writes=[go], tag="AR_pool")
    p.dma("sp", lambda e: e.dma_start(out=k.poolg.ap, in_=go.ap), reads=[go], writes=[k.poolg])
    p.ts("dve", k.poolg.ap, k.poolg.ap, 1.0 / (NCORES * T), None, ALU.mult, ALU.bypass,
         reads=[k.poolg], writes=[k.poolg], tag="poolmean")
    p.dma("sp", lambda e: e.dma_start(out=k.rwsb.ap.rearrange("p (kk f) -> p kk f", kk=8),
                                      in_=k.i_rw.ap.rearrange("(kk p) f -> p kk f", p=128)),
          writes=[k.rwsb])
    psr = k.psC
    for c in range(8):
        p.mm(psr[0:4, 0:1], k.rwsb[:, 4 * c:4 * c + 4], k.poolg[:, c:c + 1], c == 0, c == 7,
             reads=[k.rwsb, k.poolg], writes=[psr], tag=f"rlog{c}")
    p.tt("dve", k.lg.ap, psr[0:4, 0:1], k.consts[0:4, RBC:RBC + 1], ALU.add,
         reads=[psr, k.consts], writes=[k.lg])
    p.ts("dve", k.lg.ap, k.lg.ap, LOG_THR, None, ALU.is_gt, ALU.bypass, reads=[k.lg], writes=[k.lg])
    if "rt" in DBG:
        dp = p.dram_out("dbg_pooled", [128, 8], F32)
        p.dma("sp", lambda e: e.dma_start(out=dp.ap, in_=k.pooled.ap), reads=[k.pooled], writes=[dp])
        d5 = p.dram_out("dbg_lg", [4, 1], F32)
        d6 = p.dram_out("dbg_poolg", [128, 8], F32)
        p.dma("sp", lambda e: e.dma_start(out=d5.ap, in_=k.lg.ap), reads=[k.lg], writes=[d5])
        p.dma("sp", lambda e: e.dma_start(out=d6.ap, in_=k.poolg.ap), reads=[k.poolg], writes=[d6])
    p.tr(psr[0:1, 4:8], k.lg.ap, k.ident[0:4, 0:4], reads=[k.lg, k.ident], writes=[psr], tag="masktr")
    p.copy("dve", k.mrow.ap, psr[0:1, 4:8], reads=[psr], writes=[k.mrow])
    p.mm(psr[:, 8:12], k.onesf[0:1, 0:128], k.mrow.ap, True, True,
         reads=[k.onesf, k.mrow], writes=[psr], tag="maskbc")
    p.copy("dve", k.masks.ap, psr[:, 8:12], reads=[psr], writes=[k.masks])


def emit_ln(k, nm, src_aps, src_bufs, outs, ln_i, eps, width):
    p = k.p
    nch = ceil_div(width, 512)
    for ci in range(nch):
        a = ci * 512
        w = min(512, width - a)
        psS, psQ = k.psC, k.psD
        for c in range(8):
            sap = src_aps(c, a, w)
            rb = src_bufs(c)
            xb = (k.ln_xb, k.ln_xb2)[c % 2]
            sq = (k.ln_sq, k.ln_sq2)[c % 2]
            p.copy("dve", xb[:, 0:w], sap, reads=[rb], writes=[xb], tag=f"{nm}cv{c}")
            p.act(sq[:, 0:w], sap, AF.Square, reads=[rb], writes=[sq], tag=f"{nm}sq{c}")
            p.mm(psS[0:1, 0:w], k.onesb[:, 0:1], xb[:, 0:w], c == 0, c == 7,
                 reads=[k.onesb, xb], writes=[psS], tag=f"{nm}mS{c}")
            p.mm(psQ[0:1, 0:w], k.onesb[:, 0:1], sq[:, 0:w], c == 0, c == 7,
                 reads=[k.onesb, sq], writes=[psQ], tag=f"{nm}mQ{c}")
        m_r, v_r, nm_r = k.rows[0:1, 0:w], k.rows[0:1, 512:512 + w], k.rows[0:1, 1024:1024 + w]
        p.ts("dve", m_r, psS[0:1, 0:w], 1.0 / D, None, ALU.mult, ALU.bypass, reads=[psS], writes=[k.rows])
        p.tt("dve", v_r, m_r, m_r, ALU.mult, reads=[k.rows], writes=[k.rows])
        p.stt("dve", v_r, psQ[0:1, 0:w], 1.0 / D, v_r, ALU.mult, ALU.subtract,
              reads=[psQ, k.rows], writes=[k.rows])
        p.ts("dve", v_r, v_r, float(eps), None, ALU.add, ALU.bypass, reads=[k.rows], writes=[k.rows])
        p.act(v_r, v_r, AF.Ln, reads=[k.rows], writes=[k.rows])
        p.act(v_r, v_r, AF.Exp, scale=-0.5, reads=[k.rows], writes=[k.rows])
        p.tt("dve", nm_r, m_r, v_r, ALU.mult, reads=[k.rows], writes=[k.rows])
        p.ts("dve", nm_r, nm_r, -1.0, None, ALU.mult, ALU.bypass, reads=[k.rows], writes=[k.rows])
        psR, psB_ = k.psA, k.psB
        p.mm(psR[:, 0:w], k.onesf[0:1, 0:128], v_r, True, True, reads=[k.onesf, k.rows], writes=[psR], tag=f"{nm}bR")
        p.mm(psB_[:, 0:w], k.onesf[0:1, 0:128], nm_r, True, True, reads=[k.onesf, k.rows], writes=[psB_], tag=f"{nm}bB")
        for c in range(8):
            sap = src_aps(c, a, w)
            rb = src_bufs(c)
            tb = k.ln_t1
            p.tt("dve", tb[:, 0:w], sap, psR[:, 0:w], ALU.mult, reads=[rb, psR], writes=[tb], tag=f"{nm}a1_{c}")
            d0ap, d0buf = outs[0][0](c, a, w), outs[0][1](c)
            if ln_i is not None:
                p.tt("dve", tb[:, 0:w], tb[:, 0:w], psB_[:, 0:w], ALU.add, reads=[tb, psB_], writes=[tb])
                g = k.col(LNG + 8 * ln_i + c)
                bb = k.col(LNB + 8 * ln_i + c)
                p.ts("dve", d0ap, tb[:, 0:w], g, bb, ALU.mult, ALU.add,
                     reads=[tb, k.consts], writes=[d0buf], tag=f"{nm}a2_{c}")
            else:
                p.tt("dve", d0ap, tb[:, 0:w], psB_[:, 0:w], ALU.add, reads=[tb, psB_], writes=[d0buf], tag=f"{nm}a2_{c}")
            for oap, obuf in outs[1:]:
                p.copy("dve", oap(c, a, w), d0ap, reads=[d0buf], writes=[obuf(c)], tag=f"{nm}cp{c}")


def alloc_mamba(k):
    k.MB1 = k.sub("MB1", 0, 0, 16 * B3, F32)
    k.mxc = k.sub("mxc", 1, 0, 16 * SEC, F32)
    k.ma1 = k.sub("ma1", 2, 0, 16 * SEC, F32)
    k.ma4 = k.sub("ma4", 3, 0, 16 * SEC, F32)
    k.MB2 = k.sub("MB2", 4, 0, 16 * SEC, BF16)
    k.mwbf = k.sub("mwbf", 4, 16 * SEC, 16 * SEC, BF16)
    k.meqb = k.sub("meqb", 5, 0, 16 * SEC, BF16)
    k.MB3 = k.sub("MB3", 5, 16 * SEC, 16 * SEC, BF16)


def emit_mamba_section(k, s, pass2):
    p = k.p
    base = s * SEC
    if s == 0:
        p.dma("sp", lambda e: e.dma_start(
            out=k.xh.ap.rearrange("p (cc f) -> p cc f", cc=8),
            in_=k.i_xhalo.ap.rearrange("(cc p) f -> p cc f", p=128)), writes=[k.xh], tag="ldxh")
        emit_ln(k, "lnh", lambda c, a, w: k.xh[:, 4 * c + a: 4 * c + a + w],
                lambda c: k.xh,
                [(lambda c, a, w: k.mlnb[c][:, a:a + w], lambda c: k.mlnb[c])], 0, 1e-6, width=3)
    else:
        for c in range(8):
            p.copy("dve", k.mlnb[c][:, 0:3], k.mlnb[c][:, B3 - 3:B3],
                   reads=[k.mlnb[c]], writes=[k.mlnb[c]], tag=f"lead{c}")
        xzv = blkv(k.MB1, 16)
        p.copy("dve", xzv[:, :, 0:3], xzv[:, :, B3 - 3:B3], reads=[k.MB1], writes=[k.MB1], tag="xzlead")
    emit_ln(k, f"ln0s{s}p{int(pass2)}",
            lambda c, a, w: k.h[c][:, base + a: base + a + w], lambda c: k.h[c],
            [(lambda c, a, w: k.mlnb[c][:, 3 + a:3 + a + w], lambda c: k.mlnb[c])], 0, 1e-6, width=SEC)

    rw0, rww = (0, B3) if s == 0 else (3, SEC)
    fts = list(range(16)) + (list(range(16, 32)) if pass2 else [])
    for fgi in range(0, len(fts), 2):
        wb = k.wst[(fgi // 2) % 2]
        f0 = fts[fgi]
        src = k.i_win.ap[:, 128 * f0:128 * f0 + 256].rearrange("(cc p) f -> p cc f", p=128)
        p.dma("pool", lambda e, wb=wb, src=src: e.dma_start(
            out=wb.ap.rearrange("p (cc f) -> p cc f", cc=8), in_=src), writes=[wb], tag=f"ldwin{f0}")
        for fo in range(2):
            ft = f0 + fo
            pst = (k.psA, k.psB)[(ft // 2) % 2]
            reg = pst[:, (ft % 2) * 512:(ft % 2) * 512 + rww]
            for c in range(8):
                p.mm(reg, wb[:, c * 256 + fo * 128: c * 256 + fo * 128 + 128],
                     k.mlnb[c][:, rw0:rw0 + rww], c == 0, c == 7,
                     reads=[wb, k.mlnb[c]], writes=[pst], tag=f"xz{ft}")
            if ft < 16:
                dst = k.MB1.ap[:, ft * B3 + rw0: ft * B3 + rw0 + rww]
                p.copy("dve", dst, reg, reads=[pst], writes=[k.MB1], tag=f"xzev{ft}")
            else:
                zdst = k.MB2.ap[:, (ft - 16) * SEC:(ft - 16) * SEC + SEC]
                srcp = pst[:, (ft % 2) * 512 + (3 if s == 0 else 0):(ft % 2) * 512 + (3 if s == 0 else 0) + SEC]
                p.act(zdst, srcp, AF.Silu, reads=[pst], writes=[k.MB2], tag=f"silz{ft}")

    for c in range(16):
        st = "dve"
        a4b = k.ma4.ap[:, c * SEC:(c + 1) * SEC]
        p.ts(st, a4b, k.MB1.ap[:, c * B3 + 0: c * B3 + SEC], k.col(CONVC + 0 * 16 + c), None,
             ALU.mult, ALU.bypass, reads=[k.MB1, k.consts], writes=[k.ma4], tag=f"cv0_{c}")
        for i in range(1, 4):
            p.stt(st, a4b, k.MB1.ap[:, c * B3 + i: c * B3 + i + SEC], k.col(CONVC + i * 16 + c), a4b,
                  ALU.mult, ALU.add, reads=[k.MB1, k.consts, k.ma4], writes=[k.ma4], tag=f"cv{i}_{c}")

    p.act(k.mxc.ap, k.ma4.ap, AF.Silu, reads=[k.ma4], writes=[k.mxc], tag="silxc")
    if pass2:
        p.copy("dve", k.mwbf.ap, k.mxc.ap, reads=[k.mxc], writes=[k.mwbf], tag="xcb")
        xcb = k.mwbf
    else:
        p.copy("dve", k.mwbf.ap, k.mxc.ap, reads=[k.mxc], writes=[k.mwbf], tag="xcb")
        xcb = k.mwbf
    for half, dstb, pst in ((0, k.mBb, k.psC), (1, k.mCb, k.psD)):
        if half == 1 and not pass2:
            continue
        wbb = k.wst[half]
        src_ = k.i_wbc.ap[:, 64 * half:64 * half + 64].rearrange("(cc p) q -> p cc q", p=128)
        p.dma("pool", lambda e, wbb=wbb, src_=src_: e.dma_start(
            out=wbb.ap[:, 0:1024].rearrange("p (cc q) -> p cc q", cc=16), in_=src_),
            writes=[wbb], tag=f"ldbc{half}")
        for c in range(16):
            p.mm(pst[0:64, 0:SEC], wbb[:, c * 64:(c + 1) * 64],
                 xcb.ap[:, c * SEC:(c + 1) * SEC], c == 0, c == 15,
                 reads=[wbb, xcb], writes=[pst], tag=f"bc{half}_{c}")
        p.copy("dve", dstb.ap, pst[0:64, 0:SEC], reads=[pst], writes=[dstb], tag=f"bcev{half}")

    a1v = blkv(k.ma1, 16)
    a4v = blkv(k.ma4, 16)
    xcv = blkv(k.mxc, 16)
    p.tt("dve", a1v, xcv, k.cbc(WDTC, 16, SEC), ALU.mult, reads=[k.mxc, k.consts], writes=[k.ma1], tag="dt1")
    p.tt("dve", a1v, a1v, k.cbc(DTBC, 16, SEC), ALU.add, reads=[k.ma1, k.consts], writes=[k.ma1], tag="dt2")
    p.act(k.ma4.ap, k.ma1.ap, AF.Abs, reads=[k.ma1], writes=[k.ma4], tag="sp_abs")
    p.act(k.ma4.ap, k.ma4.ap, AF.Exp, scale=-1.0, reads=[k.ma4], writes=[k.ma4], tag="sp_exp")
    p.act(k.ma4.ap, k.ma4.ap, AF.Ln, bias=k.onesf[:, 0:1], reads=[k.ma4, k.onesf], writes=[k.ma4], tag="sp_ln")
    p.stt("dve", k.ma1.ap, k.ma1.ap, 0.0, k.ma4.ap, ALU.max, ALU.add, reads=[k.ma1, k.ma4], writes=[k.ma1], tag="sp_fin")
    p.tt("dve", a4v, a1v, k.cbc(ALOGC, 16, SEC), ALU.mult, reads=[k.ma1, k.consts], writes=[k.ma4], tag="dtA")
    p.tt("dve", a1v, a1v, xcv, ALU.mult, reads=[k.ma1, k.mxc], writes=[k.ma1], tag="vv")

    for c in range(16):
        p.op("dve", lambda e, c=c: e.tensor_tensor_scan(
            k.MB1.ap[:, c * B3:c * B3 + SEC], k.onesf[:, 0:SEC], k.ma4.ap[:, c * SEC:(c + 1) * SEC],
            k.mqcol[:, c:c + 1], ALU.mult, ALU.add),
            reads=[k.onesf, k.ma4, k.mqcol], writes=[k.MB1], tag=f"qscan{c}")

    qv = blkv(k.MB1, 16)
    ecv = blkv(k.mec, 16)
    pcv = blkv(k.mpc, 16)
    mcv = blkv(k.mmc, 16)
    p.copy("dve", ecv, qv[:, :, CK - 1:SEC:CK], reads=[k.MB1], writes=[k.mec], tag="bnd_e")
    p.copy("dve", pcv[:, :, 0:1], k.mqcol.ap.unsqueeze(2), reads=[k.mqcol], writes=[k.mpc], tag="bnd_p0")
    if NCHS > 1:
        p.copy("dve", pcv[:, :, 1:NCHS], ecv[:, :, 0:NCHS - 1], reads=[k.mec], writes=[k.mpc], tag="bnd_p1")
    p.tt("dve", mcv, ecv, pcv, ALU.add, reads=[k.mec, k.mpc], writes=[k.mmc], tag="bnd_m1")
    p.ts("dve", mcv, mcv, 0.5, None, ALU.mult, ALU.bypass, reads=[k.mmc], writes=[k.mmc], tag="bnd_m2")
    p.tt("dve", blkv(k.mdf, 16), mcv, pcv, ALU.subtract, reads=[k.mmc, k.mpc], writes=[k.mdf], tag="bnd_d1")
    p.act(k.mdh.ap, k.mdf.ap, AF.Exp, reads=[k.mdf], writes=[k.mdh], tag="bnd_dh")
    p.tt("dve", k.mdf.ap, k.mdh.ap, k.mdh.ap, ALU.mult, reads=[k.mdh], writes=[k.mdf], tag="bnd_df")
    if not pass2:
        for j in range(NCHS):
            p.tt("dve", k.mdtot.ap, k.mdtot.ap, blkv(k.mdf, 16)[:, :, j:j + 1].squeeze(2),
                 ALU.mult, reads=[k.mdtot, k.mdf], writes=[k.mdtot], tag=f"dtot{j}")
    p.copy("dve", k.mqcol.ap, qv[:, :, SEC - 1:SEC].squeeze(2), reads=[k.MB1], writes=[k.mqcol], tag="qcol")

    qv4 = blkv(k.MB1, 16)[:, :, 0:SEC].rearrange("p b (j t) -> p b j t", j=NCHS)
    mc4 = blkv(k.mmc, 16).unsqueeze(3).broadcast_to([128, 16, NCHS, CK])
    a44 = k.ma4.ap.rearrange("p (b j t) -> p b j t", b=16, j=NCHS)
    p.tt("dve", a44, mc4, qv4, ALU.subtract, reads=[k.mmc, k.MB1], writes=[k.ma4], tag="exparg")
    if pass2:
        p.act(k.meqb.ap, k.ma4.ap, AF.Exp, scale=-1.0, reads=[k.ma4], writes=[k.meqb], tag="eqb")
    p.act(k.ma4.ap, k.ma4.ap, AF.Exp, reads=[k.ma4], writes=[k.ma4], tag="wexp")
    p.tt("dve", k.mwbf.ap, k.ma4.ap, k.ma1.ap, ALU.mult, reads=[k.ma4, k.ma1], writes=[k.mwbf], tag="wbf")

    for j in range(NCHS):
        p.tr(k.psT[0:64, 1920:1984], k.mBb[:, j * CK:(j + 1) * CK], k.identb[0:64, 0:64],
             reads=[k.mBb, k.identb], writes=[k.psT], tag=f"btr{j}")
        p.copy("dve", k.mbtm.ap, k.psT[0:64, 1920:1984], reads=[k.psT], writes=[k.mbtm], tag=f"btm{j}")
        for c in range(16):
            p.tr(k.psT[0:64, c * 128:(c + 1) * 128],
                 k.mwbf.ap[:, c * SEC + j * CK: c * SEC + j * CK + CK],
                 k.identb.ap, reads=[k.mwbf, k.identb], writes=[k.psT], tag=f"wtr{j}_{c}")
        p.copy("dve", k.mwtm.ap, k.psT[0:64, 0:2048], reads=[k.psT], writes=[k.mwtm], tag=f"wtmev{j}")
        if pass2:
            p.mm(k.psD[0:64, 0:CK], k.mBb[:, j * CK:(j + 1) * CK], k.mCb[:, j * CK:(j + 1) * CK],
                 True, True, reads=[k.mBb, k.mCb], writes=[k.psD], tag=f"g{j}")
            p.tt("dve", k.mgm.ap, k.psD[0:64, 0:CK], k.trib.ap, ALU.mult,
                 reads=[k.psD, k.trib], writes=[k.mgm], tag=f"gm{j}")
            p.tt("dve", blkv(k.mhbf, 16), blkv(k.mhst, 16),
                 blkv(k.mdh, 16)[:, :, j:j + 1].broadcast_to([128, 16, 64]),
                 ALU.mult, reads=[k.mhst, k.mdh], writes=[k.mhbf], tag=f"hbf{j}")
            for c in range(16):
                p.tr(k.psT[0:64, c * 128:(c + 1) * 128], k.mhbf[:, c * 64:(c + 1) * 64],
                     k.identb.ap, reads=[k.mhbf, k.identb], writes=[k.psT], tag=f"htr{j}_{c}")
            p.copy("dve", k.mhtm.ap, k.psT[0:64, 0:2048], reads=[k.psT], writes=[k.mhtm], tag=f"htmev{j}")
            for c in range(16):
                p.mm(k.psA[:, c * 64:(c + 1) * 64], k.mwtm[0:64, c * 128:(c + 1) * 128], k.mgm.ap,
                     c in (0, 8), False, reads=[k.mwtm, k.mgm], writes=[k.psA], tag=f"s1_{j}_{c}")
            for c in range(16):
                p.mm(k.psA[:, c * 64:(c + 1) * 64], k.mhtm[0:64, c * 128:(c + 1) * 128],
                     k.mCb[:, j * CK:(j + 1) * CK], False, True,
                     reads=[k.mhtm, k.mCb], writes=[k.psA], tag=f"s2_{j}_{c}")
            p.tt("dve", k.MB3.ap.rearrange("p (b j t) -> p b j t", b=16, j=NCHS)[:, :, j, :],
                 k.psA.ap.rearrange("p (b t) -> p b t", b=16),
                 k.meqb.ap.rearrange("p (b j t) -> p b j t", b=16, j=NCHS)[:, :, j, :],
                 ALU.mult, reads=[k.psA, k.meqb], writes=[k.MB3], tag=f"ylo{j}")
        for c in range(16):
            p.mm(k.psB[:, c * 64:(c + 1) * 64], k.mwtm[0:64, c * 128:(c + 1) * 128], k.mbtm.ap,
                 c in (0, 8), True, reads=[k.mwtm, k.mbtm], writes=[k.psB], tag=f"upd{j}_{c}")
        p.tt("dve", k.psB.ap.rearrange("p (b t) -> p b t", b=16), k.psB.ap.rearrange("p (b t) -> p b t", b=16),
             blkv(k.mdh, 16)[:, :, j:j + 1].broadcast_to([128, 16, 64]), ALU.mult,
             reads=[k.psB, k.mdh], writes=[k.psB], tag=f"hup1_{j}")
        p.tt("dve", blkv(k.mhst, 16), blkv(k.mhst, 16),
             blkv(k.mdf, 16)[:, :, j:j + 1].broadcast_to([128, 16, 64]), ALU.mult,
             reads=[k.mhst, k.mdf], writes=[k.mhst], tag=f"hup2_{j}")
        p.tt("dve", k.mhst.ap, k.mhst.ap, k.psB.ap, ALU.add,
             reads=[k.mhst, k.psB], writes=[k.mhst], tag=f"hup3_{j}")

    if pass2:
        p.tt("dve", a4v, xcv, k.cbc(DSKC, 16, SEC), ALU.mult, reads=[k.mxc, k.consts], writes=[k.ma4], tag="dsk1")
        p.tt("dve", k.MB3.ap, k.MB3.ap, k.ma4.ap, ALU.add, reads=[k.MB3, k.ma4], writes=[k.MB3], tag="dsk2")
        p.tt("dve", k.MB3.ap, k.MB3.ap, k.MB2.ap, ALU.mult, reads=[k.MB3, k.MB2], writes=[k.MB3], tag="gate")
        for f in range(8):
            wb = k.wst[f % 2]
            src = k.i_wout.ap[:, 128 * f:128 * (f + 1)].rearrange("(cc p) q -> p cc q", p=128)
            p.dma("pool", lambda e, wb=wb, src=src: e.dma_start(
                out=wb.ap.rearrange("p (cc q) -> p cc q", cc=16), in_=src), writes=[wb], tag=f"ldwo{f}")
            pst = (k.psC, k.psD)[f % 2]
            for c in range(16):
                p.mm(pst[:, 0:SEC], wb[:, c * 128:(c + 1) * 128], k.MB3.ap[:, c * SEC:(c + 1) * SEC],
                     c == 0, c == 15, reads=[wb, k.MB3], writes=[pst], tag=f"wo{f}_{c}")
            p.stt("dve", k.h[f][:, base:base + SEC], pst[:, 0:SEC], k.masks[:, 0:1],
                  k.h[f][:, base:base + SEC], ALU.mult, ALU.add,
                  reads=[pst, k.masks, k.h[f]], writes=[k.h[f]], tag=f"hout{f}")


def emit_mamba_exchange(k):
    p = k.p
    gi, go = k.g_mst_in, k.g_mst_out
    p.dma("sp", lambda e: e.dma_start(
        out=gi.ap.rearrange("(cc p) n -> p cc n", p=128),
        in_=k.mhst.ap.rearrange("p (cc n) -> p cc n", cc=16)),
        reads=[k.mhst], writes=[gi], tag="mst_st1")
    p.dma("sp", lambda e: e.dma_start(out=k.g_mp_in.ap, in_=k.mdtot.ap),
          reads=[k.mdtot], writes=[k.g_mp_in], tag="mst_st2")
    p.coll(lambda e: e.collective_compute("AllGather", ALU.bypass, replica_groups=[list(range(NCORES))],
                                          ins=[gi.ap.opt()], outs=[go.ap.opt()]),
           reads=[gi], writes=[go], tag="AG_mst")
    p.coll(lambda e: e.collective_compute("AllGather", ALU.bypass, replica_groups=[list(range(NCORES))],
                                          ins=[k.g_mp_in.ap.opt()], outs=[k.g_mp_out.ap.opt()]),
           reads=[k.g_mp_in], writes=[k.g_mp_out], tag="AG_mp")
    macc = k.sub("macc", 2, 0, 16 * 64, F32)
    p.op("dve", lambda e: e.memset(macc.ap, 0.0), writes=[macc])
    p.op("dve", lambda e: e.memset(k.mhst.ap, 0.0), writes=[k.mhst])
    mrb = [k.sub(f"mrb{i}", i, 0, 16 * 64, F32) for i in range(2)]
    mrp = [k.sub(f"mrp{i}", i, 16 * 64, 16, F32) for i in range(2)]
    for j in range(NCORES):
        rb, rp = mrb[j % 2], mrp[j % 2]
        src = go.ap[j * DI:(j + 1) * DI, :].rearrange("(cc p) n -> p cc n", p=128)
        p.dma("sp", lambda e, rb=rb, src=src: e.dma_start(
            out=rb.ap.rearrange("p (cc n) -> p cc n", cc=16), in_=src), reads=[go], writes=[rb], tag=f"mrb{j}")
        srcp = k.g_mp_out.ap[j * 128:(j + 1) * 128, :]
        p.dma("sp", lambda e, rp=rp, srcp=srcp: e.dma_start(out=rp.ap, in_=srcp),
              reads=[k.g_mp_out], writes=[rp], tag=f"mrp{j}")
        p.stt("dve", k.mhst.ap, macc.ap, k.col(SELR + j), k.mhst.ap, ALU.mult, ALU.add,
              reads=[macc, k.consts, k.mhst], writes=[k.mhst], tag=f"pick{j}")
        if j < NCORES - 1:
            p.tt("dve", blkv(macc, 16), blkv(macc, 16),
                 rp.ap.unsqueeze(2).broadcast_to([128, 16, 64]), ALU.mult,
                 reads=[macc, rp], writes=[macc], tag=f"accP{j}")
            p.tt("dve", blkv(macc, 16), blkv(macc, 16), blkv(rb, 16), ALU.add,
                 reads=[macc, rb], writes=[macc], tag=f"accH{j}")


def emit_mamba(k):
    p = k.p
    alloc_mamba(k)
    p.op("dve", lambda e: e.memset(k.mhst.ap, 0.0), writes=[k.mhst])
    p.op("dve", lambda e: e.memset(k.mqcol.ap, 0.0), writes=[k.mqcol])
    p.op("dve", lambda e: e.memset(k.mdtot.ap, 1.0), writes=[k.mdtot])
    for s in range(NSEC):
        emit_mamba_section(k, s, pass2=False)
    emit_mamba_exchange(k)
    p.op("dve", lambda e: e.memset(k.mqcol.ap, 0.0), writes=[k.mqcol])
    for s in range(NSEC):
        emit_mamba_section(k, s, pass2=True)


def emit_tc(k):
    p = k.p
    x1b = [k.sub(f"x1b{c}", 4 + c // 4, (c % 4) * 2048, 2048, BF16) for c in range(8)]
    x1f = [k.sub(f"x1f{c}", 2, c * 512, 512, F32) for c in range(8)]
    tcw = k.sub("tcw", 3, 0, 8192, BF16)
    p.dma("pool", lambda e: e.dma_start(
        out=tcw.ap.rearrange("p (cc f) -> p cc f", cc=8),
        in_=k.i_tcw.ap.rearrange("(cc p) f -> p cc f", p=128)), writes=[tcw], tag="ldtcw")
    tcc = k.sbuf("tcc", [128, 4], F32)
    thc = k.col(THC)
    p.act(tcc[:, 0:1], thc, AF.Sin, scale=0.5, reads=[k.consts], writes=[tcc], tag="tcsin")
    p.tt("dve", tcc[:, 1:2], tcc[:, 0:1], tcc[:, 0:1], ALU.mult, reads=[tcc], writes=[tcc])
    p.ts("dve", tcc[:, 2:3], tcc[:, 1:2], -1.0, 1.0, ALU.mult, ALU.add, reads=[tcc], writes=[tcc])
    p.act(tcc[:, 2:3], tcc[:, 2:3], AF.Ln, reads=[tcc], writes=[tcc])
    p.act(tcc[:, 2:3], tcc[:, 2:3], AF.Exp, scale=0.5, reads=[tcc], writes=[tcc])
    p.tt("dve", tcc[:, 3:4], tcc[:, 0:1], tcc[:, 2:3], ALU.mult, reads=[tcc], writes=[tcc])
    p.ts("dve", tcc[:, 3:4], tcc[:, 3:4], 2.0, None, ALU.mult, ALU.bypass, reads=[tcc], writes=[tcc])
    p.ts("dve", tcc[:, 1:2], tcc[:, 1:2], -2.0, 1.0, ALU.mult, ALU.add, reads=[tcc], writes=[tcc])
    p.tt("dve", tcc[:, 1:2], tcc[:, 1:2], k.masks[:, 1:2], ALU.mult, reads=[tcc, k.masks], writes=[tcc])
    p.tt("dve", tcc[:, 3:4], tcc[:, 3:4], k.masks[:, 1:2], ALU.mult, reads=[tcc, k.masks], writes=[tcc])
    for ci in range(4):
        a = ci * 512
        emit_ln(k, f"ln1c{ci}",
                lambda c, aa, w, a=a: k.h[c][:, a + aa: a + aa + w], lambda c: k.h[c],
                [(lambda c, aa, w: x1f[c].ap[:, aa:aa + w], lambda c: x1f[c]),
                 (lambda c, aa, w, a=a: x1b[c].ap[:, a + aa:a + aa + w], lambda c: x1b[c])],
                1, 1e-6, width=512)
        if ci == 0 and "tc" in DBG:
            d2 = p.dram_out("dbg_x1f0", [128, 512], F32)
            d3 = p.dram_out("dbg_tcc", [128, 4], F32)
            d4 = p.dram_out("dbg_masks", [128, 4], F32)
            p.dma("sp", lambda e: e.dma_start(out=d2.ap, in_=x1f[0].ap), reads=[x1f[0]], writes=[d2])
            p.dma("sp", lambda e: e.dma_start(out=d3.ap, in_=tcc.ap), reads=[tcc], writes=[d3])
            p.dma("sp", lambda e: e.dma_start(out=d4.ap, in_=k.masks.ap), reads=[k.masks], writes=[d4])
        for f in range(8):
            pst = (k.psC, k.psD)[f % 2]
            for c in range(8):
                p.mm(pst[:, 0:512], tcw[:, c * 1024 + f * 128: c * 1024 + f * 128 + 128],
                     x1b[c].ap[:, a:a + 512], c == 0, c == 7,
                     reads=[tcw, x1b[c]], writes=[pst], tag=f"tcmm{f}")
            p.act(k.ln_t1.ap, pst[:, 0:512], AF.Tanh, bias=k.col(TCBC + f),
                  reads=[pst, k.consts], writes=[k.ln_t1], tag=f"tctanh{f}")
            p.stt("dve", k.h[f][:, a:a + 512], x1f[f].ap, tcc[:, 1:2], k.h[f][:, a:a + 512],
                  ALU.mult, ALU.add, reads=[x1f[f], tcc, k.h[f]], writes=[k.h[f]], tag=f"tchx{f}")
            p.stt("dve", k.h[f][:, a:a + 512], k.ln_t1.ap, tcc[:, 3:4], k.h[f][:, a:a + 512],
                  ALU.mult, ALU.add, reads=[k.ln_t1, tcc, k.h[f]], writes=[k.h[f]], tag=f"tcht{f}")


def build_kmax(k, tiles, dstbuf, nm, dcol):
    p = k.p
    p.op("dve", lambda e: e.memset(k.ln_t1.ap, 0.0), writes=[k.ln_t1])
    for cc in range(8):
        for tch in range(2):
            p.act(k.ln_xb.ap, tiles[cc].ap[:, tch * 512:tch * 512 + 512], AF.Square,
                  reads=[tiles[cc]], writes=[k.ln_xb], tag=f"{nm}sq{cc}")
            for h2 in range(2):
                pst = k.psC
                p.mm(pst[h2 * 64:h2 * 64 + 1, 0:512], k.onesb[h2 * 64:h2 * 64 + 64, 0:1],
                     k.ln_xb[h2 * 64:h2 * 64 + 64, 0:512], True, True,
                     reads=[k.onesb, k.ln_xb], writes=[pst], tag=f"{nm}mm{cc}{h2}")
                col = 16 + 2 * cc + h2
                p.op("dve", lambda e, pst=pst, h2=h2, col=col: e.tensor_reduce(
                    k.ln_t1[h2 * 64:h2 * 64 + 1, col:col + 1], pst[h2 * 64:h2 * 64 + 1, 0:512], AX.X, ALU.max),
                    reads=[pst], writes=[k.ln_t1], tag=f"{nm}rd{cc}{h2}")
                p.tt("dve", k.ln_t1[h2 * 64:h2 * 64 + 1, 2 * cc + h2:2 * cc + h2 + 1],
                     k.ln_t1[h2 * 64:h2 * 64 + 1, 2 * cc + h2:2 * cc + h2 + 1],
                     k.ln_t1[h2 * 64:h2 * 64 + 1, col:col + 1], ALU.max,
                     reads=[k.ln_t1], writes=[k.ln_t1], tag=f"{nm}mx{cc}{h2}")
    pst = k.psD
    p.tr(pst[0:16, 0:128], k.ln_t1[:, 0:16], k.ident.ap, reads=[k.ln_t1, k.ident], writes=[pst], tag=f"{nm}tr")
    fv = k.ln_xb.ap.bitcast(F32)
    p.copy("dve", fv[0:16, 0:2], pst[0:16, 0:128:64], reads=[pst], writes=[k.ln_xb], tag=f"{nm}cp")
    p.op("dve", lambda e: e.tensor_reduce(fv[0:16, 2:3], fv[0:16, 0:2], AX.X, ALU.max),
         reads=[k.ln_xb], writes=[k.ln_xb], tag=f"{nm}rd2")
    p.tr(pst[0:1, 0:16], fv[0:16, 2:3], k.ident[0:16, 0:16], reads=[k.ln_xb, k.ident], writes=[pst], tag=f"{nm}tr2")
    p.copy("dve", dstbuf[0:1, dcol:dcol + 16], pst[0:1, 0:16], reads=[pst], writes=[dstbuf], tag=f"{nm}fin")


def emit_wlam(k):
    p = k.p
    INV = float(np.float32(0.7071067811865476))
    abf = [k.sub(f"abf{c}", 4, c * 1024, 1024, BF16) for c in range(8)]
    qbf = [k.sub(f"qbf{c}", 5, c * 1024, 1024, BF16) for c in range(8)]
    ds = [k.sub(f"wds{c}", c // 4, (c % 4) * 1024, 1024, F32) for c in range(8)]
    x2f = [k.sub(f"x2f{c}", 2, c * 512, 512, F32) for c in range(8)]
    kfm = [k.sub(f"kfm{c}", 3, c * 1024, 1024, BF16) for c in range(8)]
    for ci in range(4):
        a = ci * 512
        pa = ci * 256
        emit_ln(k, f"ln2c{ci}",
                lambda c, aa, w, a=a: k.h[c][:, a + aa: a + aa + w], lambda c: k.h[c],
                [(lambda c, aa, w: x2f[c].ap[:, aa:aa + w], lambda c: x2f[c])],
                2, 1e-6, width=512)
        for c in range(8):
            ev = x2f[c].ap[:, 0:512:2]
            od = x2f[c].ap[:, 1:512:2]
            p.tt("dve", k.ln_t1[:, 0:256], ev, od, ALU.add, reads=[x2f[c]], writes=[k.ln_t1], tag=f"pa{c}")
            p.ts("dve", abf[c].ap[:, pa:pa + 256], k.ln_t1[:, 0:256], INV, None, ALU.mult, ALU.bypass,
                 reads=[k.ln_t1], writes=[abf[c]], tag=f"pab{c}")
            p.tt("dve", k.ln_t1[:, 0:256], ev, od, ALU.subtract, reads=[x2f[c]], writes=[k.ln_t1], tag=f"pd{c}")
            p.ts("dve", ds[c].ap[:, pa:pa + 256], k.ln_t1[:, 0:256], 0.5, None, ALU.mult, ALU.bypass,
                 reads=[k.ln_t1], writes=[ds[c]], tag=f"pdb{c}")

    def prj(widram, dsts, tagp):
        for f in range(8):
            wb = k.wst[f % 2]
            src = widram.ap[:, 128 * f:128 * (f + 1)].rearrange("(cc p) q -> p cc q", p=128)
            p.dma("pool", lambda e, wb=wb, src=src: e.dma_start(
                out=wb.ap[:, 0:1024].rearrange("p (cc q) -> p cc q", cc=8), in_=src),
                writes=[wb], tag=f"ld{tagp}{f}")
            for tch in range(2):
                pst = (k.psC, k.psD)[tch]
                for c in range(8):
                    p.mm(pst[:, 0:512], wb[:, c * 128:(c + 1) * 128], abf[c].ap[:, tch * 512:tch * 512 + 512],
                         c == 0, c == 7, reads=[wb, abf[c]], writes=[pst], tag=f"{tagp}mm{f}")
                p.copy("dve", dsts[f].ap[:, tch * 512:tch * 512 + 512], pst[:, 0:512],
                       reads=[pst], writes=[dsts[f]], tag=f"{tagp}ev{f}")

    prj(k.i_wq, qbf, "wq")
    prj(k.i_wk, kfm, "wk")
    kmax = None
    build_kmax(k, kfm, k.rows, "kn", 272)
    for c in range(8):
        p.dma("sp", lambda e, c=c: e.dma_start(
            out=k.g_kv_in.ap[c * 128:(c + 1) * 128, 0:1024], in_=kfm[c].ap),
            reads=[kfm[c]], writes=[k.g_kv_in], tag=f"stk{c}")
    p.dma("pool", lambda e: e.dma_start(out=k.g_kv_in.ap[2 * D:2 * D + 1, 0:16], in_=k.rows[0:1, 272:288]),
          reads=[k.rows], writes=[k.g_kv_in], tag="stkn")
    prj(k.i_wv, kfm, "wv")
    vtm = [k.sub(f"vtm{c}", 4, c * 1040, 1040, BF16) for c in range(8)]
    for i in range(8):
        for c in range(8):
            p.tr(k.psT[:, c * 128:(c + 1) * 128], kfm[c].ap[:, i * 128:(i + 1) * 128], k.identb.ap,
                 reads=[kfm[c], k.identb], writes=[k.psT], tag=f"vtr{i}_{c}")
        p.op("dve", lambda e, i=i: e.memset(vtm[i].ap.rearrange("p (hh u) -> p hh u", u=65)[:, :, 64:65], 1.0),
             writes=[vtm[i]], tag=f"vone{i}")
        p.copy("dve", vtm[i].ap.rearrange("p (hh u) -> p hh u", u=65)[:, :, 0:64],
               k.psT[:, 0:1024].rearrange("p (hh d) -> p hh d", d=64),
               reads=[k.psT], writes=[vtm[i]], tag=f"vte{i}")
        p.dma("sp", lambda e, i=i: e.dma_start(
            out=k.g_kv_in.ap[D + i * 128: D + (i + 1) * 128, :], in_=vtm[i].ap),
            reads=[vtm[i]], writes=[k.g_kv_in], tag=f"stv{i}")
    gi, go = k.g_kv_in, k.g_kv_out
    p.coll(lambda e: e.collective_compute("AllGather", ALU.bypass, replica_groups=[list(range(NCORES))],
                                          ins=[gi.ap.opt()], outs=[go.ap.opt()]),
           reads=[gi], writes=[go], tag="AG_kv")
    build_kmax(k, qbf, k.rows, "qn", 288)
    qmax = None
    kng = None
    KST = 2 * D + 16
    for r in range(NCORES):
        p.dma("pool", lambda e, r=r: e.dma_start(
            out=k.rows[0:1, r * 16:(r + 1) * 16], in_=go.ap[r * KST + 2 * D: r * KST + 2 * D + 1, 0:16]),
            reads=[go], writes=[k.rows], tag=f"ldkn{r}")
    kngm = None
    kngm_ap = k.rows[0:1, 256:272]
    p.op("dve", lambda e: e.tensor_reduce(
        kngm_ap, k.rows[0:1, 0:128].rearrange("one (r hh) -> one hh r", r=NCORES), AX.X, ALU.max),
        reads=[k.rows], writes=[k.rows], tag="knmax")
    p.tt("dve", kngm_ap, kngm_ap, k.rows[0:1, 288:304], ALU.mult, reads=[k.rows], writes=[k.rows])
    p.act(kngm_ap, kngm_ap, AF.Ln, reads=[k.rows], writes=[k.rows])
    p.act(kngm_ap, kngm_ap, AF.Exp, scale=0.5, reads=[k.rows], writes=[k.rows])
    p.ts("dve", kngm_ap, kngm_ap, 1.01 / 8.0, -40.0, ALU.mult, ALU.add, reads=[k.rows], writes=[k.rows])
    p.ts("dve", kngm_ap, kngm_ap, 0.0, -1.0, ALU.max, ALU.mult, reads=[k.rows], writes=[k.rows])
    biasr = k.sbuf("biasr", [128, 16], F32)
    p.mm(k.psD[:, 0:16], k.onesf[0:1, 0:128], kngm_ap, True, True, reads=[k.onesf, k.rows], writes=[k.psD], tag="biasbc")
    p.copy("dve", biasr.ap, k.psD[:, 0:16], reads=[k.psD], writes=[biasr])
    ksb = k.sub("ksb", 3, 0, 8192, BF16)
    pt = [k.sub(f"pt{i}", 2, i * 2048, 2048, BF16) for i in range(2)]
    vt = [k.sub(f"vt{i}", 2, 4096 + i * 1040, 1040, BF16) for i in range(2)]
    ofm = vtm
    for cc in range(8):
        for r in range(NCORES):
            p.dma("sp", lambda e, cc=cc, r=r: e.dma_start(
                out=ksb.ap[:, r * 1024:(r + 1) * 1024],
                in_=go.ap[r * KST + cc * 128: r * KST + (cc + 1) * 128, 0:1024]),
                reads=[go], writes=[ksb], tag=f"ldK{cc}_{r}")
        for h2 in range(2):
            h = 2 * cc + h2
            for tt2 in range(64):
                vti = vt[tt2 % 2]
                r, loc = tt2 // 8, tt2 % 8
                p.dma("sp", lambda e, vti=vti, r=r, loc=loc: e.dma_start(
                    out=vti.ap, in_=go.ap[r * KST + D + loc * 128: r * KST + D + (loc + 1) * 128, :]),
                    reads=[go], writes=[vti], tag=f"ldV{cc}_{h2}_{tt2}")
                pss = (k.psA, k.psB)[tt2 % 2]
                ptb = pt[tt2 % 2]
                for tch in range(2):
                    p.mm(pss[:, tch * 512:tch * 512 + 512],
                         ksb.ap[64 * h2:64 * h2 + 64, tt2 * 128:(tt2 + 1) * 128],
                         qbf[cc].ap[64 * h2:64 * h2 + 64, tch * 512:tch * 512 + 512], True, True,
                         reads=[ksb, qbf[cc]], writes=[pss], tag=f"st{h}_{tt2}_{tch}")
                p.act(ptb.ap[:, 0:1024], pss.ap, AF.Exp, scale=0.125, bias=biasr[:, h:h + 1],
                      reads=[pss, biasr], writes=[ptb], tag=f"exp{h}_{tt2}")
                for tch in range(2):
                    acc = (k.psC, k.psD)[tch]
                    p.mm(acc[0:65, 0:512], vti.ap[:, h * 65:(h + 1) * 65],
                         ptb.ap[:, tch * 512:tch * 512 + 512], tt2 == 0, tt2 == 63,
                         reads=[vti, ptb], writes=[acc], tag=f"av{h}_{tt2}_{tch}")
            for tch in range(2):
                accp = (k.psC, k.psD)[tch]
                p.op("dve", lambda e, accp=accp: e.reciprocal(k.ln_t1[64:65, 0:512], accp[64:65, 0:512]),
                     reads=[accp], writes=[k.ln_t1], tag=f"rd{h}{tch}")
                p.mm(k.psB[0:64, 0:512], k.onesf[64:65, 0:64], k.ln_t1[64:65, 0:512], True, True,
                     reads=[k.onesf, k.ln_t1], writes=[k.psB], tag=f"rdb{h}{tch}")
                p.copy("dve", k.ln_t1[0:64, 0:512], k.psB[0:64, 0:512], reads=[k.psB], writes=[k.ln_t1],
                       tag=f"rde{h}{tch}")
                p.tt("dve", ofm[cc].ap[h2 * 64:h2 * 64 + 64, tch * 512:tch * 512 + 512],
                     accp[0:64, 0:512], k.ln_t1[0:64, 0:512], ALU.mult,
                     reads=[accp, k.ln_t1], writes=[ofm[cc]], tag=f"onorm{h}{tch}")
    ywloc = [(2, 0), (2, 2048), (3, 0), (3, 2048), (0, 0), (0, 2048), (1, 0), (1, 2048)]
    yw = [k.sub(f"yw{c}", ywloc[c][0], ywloc[c][1], 2048, F32) for c in range(8)]
    ds7v = SubBuf("ds7v", k.wst[1], 0, 1024, F32)
    p.copy("dve", ds7v.ap, ds[7].ap, reads=[ds[7]], writes=[k.wst[1]], tag="ds7cp")
    dsrc = ds[:7] + [ds7v]
    for f in range(8):
        wb = k.wst[0]
        src = k.i_wo.ap[:, 128 * f:128 * (f + 1)].rearrange("(cc p) q -> p cc q", p=128)
        p.dma("pool", lambda e, wb=wb, src=src: e.dma_start(
            out=wb.ap[:, 0:1024].rearrange("p (cc q) -> p cc q", cc=8), in_=src),
            writes=[wb], tag=f"ldwo2{f}")
        for tch in range(2):
            pst = (k.psC, k.psD)[tch]
            for c in range(8):
                p.mm(pst[:, 0:512], wb[:, c * 128:(c + 1) * 128], ofm[c].ap[:, tch * 512:tch * 512 + 512],
                     c == 0, c == 7, reads=[wb, ofm[c]], writes=[pst], tag=f"womm{f}")
            p.ts("dve", k.ln_t1.ap, pst[:, 0:512], INV, None, ALU.mult, ALU.bypass,
                 reads=[pst], writes=[k.ln_t1], tag=f"o2i{f}")
            p.tt("dve", yw[f].ap[:, tch * 1024:tch * 1024 + 1024:2], k.ln_t1.ap,
                 dsrc[f].ap[:, tch * 512:tch * 512 + 512], ALU.add,
                 reads=[k.ln_t1, dsrc[f]], writes=[yw[f]], tag=f"ywe{f}")
            p.tt("dve", yw[f].ap[:, tch * 1024 + 1:tch * 1024 + 1024:2], k.ln_t1.ap,
                 dsrc[f].ap[:, tch * 512:tch * 512 + 512], ALU.subtract,
                 reads=[k.ln_t1, dsrc[f]], writes=[yw[f]], tag=f"ywo{f}")
    for c in range(8):
        p.copy("dve", k.ln_t1[:, 0:1], yw[c].ap[:, 0:1], reads=[yw[c]], writes=[k.ln_t1])
        p.copy("dve", k.ln_t1[:, 1:3], yw[c].ap[:, 2046:2048], reads=[yw[c]], writes=[k.ln_t1])
        p.dma("sp", lambda e, c=c: e.dma_start(out=k.g_yw_in.ap[c * 128:(c + 1) * 128, 0:3],
                                               in_=k.ln_t1[:, 0:3]),
              reads=[k.ln_t1], writes=[k.g_yw_in], tag=f"styw{c}")
    p.coll(lambda e: e.collective_compute("AllGather", ALU.bypass, replica_groups=[list(range(NCORES))],
                                          ins=[k.g_yw_in.ap.opt()], outs=[k.g_yw_out.ap.opt()]),
           reads=[k.g_yw_in], writes=[k.g_yw_out], tag="AG_yw")
    ywh = k.sub("ywh", 4, 0, 256, F32)
    p.dma("sp", lambda e: e.dma_start(
        out=ywh.ap.rearrange("p (r cc f) -> p r cc f", r=8, cc=8),
        in_=k.g_yw_out.ap.rearrange("(r cc p) f -> p r cc f", r=8, p=128)),
        reads=[k.g_yw_out], writes=[ywh], tag="ldywh")
    hal = k.sub("hal", 4, 256, 24, F32)
    p.op("dve", lambda e: e.memset(hal.ap, 0.0), writes=[hal])
    hv = hal.ap.rearrange("p (cc u) -> p cc u", cc=8)
    yv = ywh.ap.rearrange("p (r cc f) -> p r cc f", r=8, cc=8)
    for r in range(NCORES):
        p.stt("dve", hv[:, :, 0:2], yv[:, r, :, 1:3], k.col(SELP + r), hv[:, :, 0:2],
              ALU.mult, ALU.add, reads=[ywh, k.consts, hal], writes=[hal], tag=f"halL{r}")
        p.stt("dve", hv[:, :, 2:3], yv[:, r, :, 0:1], k.col(SELN + r), hv[:, :, 2:3],
              ALU.mult, ALU.add, reads=[ywh, k.consts, hal], writes=[hal], tag=f"halR{r}")
    wmc = k.sbuf("wmc", [128, 32], F32)
    p.tt("dve", wmc.ap, k.consts[:, WCVC:WCVC + 32], k.masks[:, 2:3].broadcast_to([128, 32]),
         ALU.mult, reads=[k.consts, k.masks], writes=[wmc], tag="wmc")
    for c in range(8):
        for i in range(4):
            sh = i - 2
            lo = max(0, -sh)
            hi = min(T, T - sh)
            p.stt("dve", k.h[c][:, lo:hi], yw[c].ap[:, lo + sh:hi + sh], wmc[:, i * 8 + c:i * 8 + c + 1],
                  k.h[c][:, lo:hi], ALU.mult, ALU.add, reads=[yw[c], wmc, k.h[c]], writes=[k.h[c]],
                  tag=f"cvw{i}_{c}")
        p.stt("dve", k.h[c][:, 0:1], hv[:, c, 0:1], wmc[:, 0 * 8 + c:0 * 8 + c + 1], k.h[c][:, 0:1],
              ALU.mult, ALU.add, reads=[hal, wmc, k.h[c]], writes=[k.h[c]], tag=f"fxa{c}")
        p.stt("dve", k.h[c][:, 0:1], hv[:, c, 1:2], wmc[:, 1 * 8 + c:1 * 8 + c + 1], k.h[c][:, 0:1],
              ALU.mult, ALU.add, reads=[hal, wmc, k.h[c]], writes=[k.h[c]], tag=f"fxb{c}")
        p.stt("dve", k.h[c][:, 1:2], hv[:, c, 1:2], wmc[:, 0 * 8 + c:0 * 8 + c + 1], k.h[c][:, 1:2],
              ALU.mult, ALU.add, reads=[hal, wmc, k.h[c]], writes=[k.h[c]], tag=f"fxc{c}")
        p.stt("dve", k.h[c][:, T - 1:T], hv[:, c, 2:3], wmc[:, 3 * 8 + c:3 * 8 + c + 1], k.h[c][:, T - 1:T],
              ALU.mult, ALU.add, reads=[hal, wmc, k.h[c]], writes=[k.h[c]], tag=f"fxd{c}")


def emit_moe(k):
    p = k.p
    x3b = [k.sub(f"x3b{c}", 4 + c // 4, (c % 4) * 2048, 2048, BF16) for c in range(8)]
    x3f = [k.sub(f"x3f{c}", 2, c * 512, 512, F32) for c in range(8)]
    gtm = k.sub("gtm", 3, 1664, 128, F32)
    wgsb = k.sub("wgsb", 3, 1536, 64, F32)
    gsc = k.sub("gsc", 3, 1600, 8, F32)
    p.dma("sp", lambda e: e.dma_start(
        out=wgsb.ap.rearrange("p (cc e) -> p cc e", cc=8),
        in_=k.i_wg.ap.rearrange("(cc p) e -> p cc e", p=128)), writes=[wgsb], tag="ldwg")
    for ci in range(4):
        a = ci * 512
        emit_ln(k, f"ln3c{ci}",
                lambda c, aa, w, a=a: k.h[c][:, a + aa: a + aa + w], lambda c: k.h[c],
                [(lambda c, aa, w: x3f[c].ap[:, aa:aa + w], lambda c: x3f[c]),
                 (lambda c, aa, w, a=a: x3b[c].ap[:, a + aa:a + aa + w], lambda c: x3b[c])],
                3, 1e-6, width=512)
        for c in range(8):
            p.mm(k.psD[0:8, 0:512], wgsb[:, c * 8:(c + 1) * 8], x3f[c].ap, c == 0, c == 7,
                 reads=[wgsb, x3f[c]], writes=[k.psD], tag=f"lgmm{ci}")
        p.copy("dve", k.ln_t1[0:8, 0:512], k.psD[0:8, 0:512], reads=[k.psD], writes=[k.ln_t1], tag=f"lgev{ci}")
        for i in range(4):
            tl = ci * 4 + i
            p.tr(k.psC[0:128, 0:8], k.ln_t1[0:8, i * 128:(i + 1) * 128], k.ident[0:8, 0:8],
                 reads=[k.ln_t1, k.ident], writes=[k.psC], tag=f"lgtr{tl}")
            p.copy("dve", gtm[:, tl * 8:(tl + 1) * 8], k.psC[0:128, 0:8], reads=[k.psC], writes=[gtm],
                   tag=f"lgtm{tl}")
    fv = k.ln_xb.ap.bitcast(F32)
    for tl in range(16):
        g8 = gtm[:, tl * 8:(tl + 1) * 8]
        p.op("dve", lambda e, g8=g8: e.tensor_reduce(gsc[:, 0:1], g8, AX.X, ALU.max),
             reads=[gtm], writes=[gsc], tag=f"g1_{tl}")
        p.op("dve", lambda e, g8=g8: e.tensor_scalar(fv[:, 0:8], g8, gsc[:, 0:1], None,
                                                     ALU.is_equal, ALU.bypass),
             reads=[gtm, gsc], writes=[k.ln_xb], tag=f"geq{tl}")
        p.stt("dve", fv[:, 8:16], fv[:, 0:8], -1e30, g8, ALU.mult, ALU.add,
              reads=[k.ln_xb, gtm], writes=[k.ln_xb], tag=f"glm{tl}")
        p.op("dve", lambda e: e.tensor_reduce(gsc[:, 1:2], fv[:, 8:16], AX.X, ALU.max),
             reads=[k.ln_xb], writes=[gsc], tag=f"g2_{tl}")
        p.op("dve", lambda e, g8=g8: e.tensor_scalar(fv[:, 0:8], g8, gsc[:, 1:2], None,
                                                     ALU.is_ge, ALU.bypass),
             reads=[gtm, gsc], writes=[k.ln_xb], tag=f"gge{tl}")
        p.op("dve", lambda e, g8=g8: e.tensor_scalar(fv[:, 8:16], g8, gsc[:, 0:1], None,
                                                     ALU.subtract, ALU.bypass),
             reads=[gtm, gsc], writes=[k.ln_xb], tag=f"gsh{tl}")
        p.act(fv[:, 8:16], fv[:, 8:16], AF.Exp, reads=[k.ln_xb], writes=[k.ln_xb], tag=f"gexp{tl}")
        p.tt("dve", gsc[:, 2:3], gsc[:, 1:2], gsc[:, 0:1], ALU.subtract, reads=[gsc], writes=[gsc])
        p.act(gsc[:, 2:3], gsc[:, 2:3], AF.Exp, reads=[gsc], writes=[gsc], tag=f"gden{tl}")
        p.ts("dve", gsc[:, 2:3], gsc[:, 2:3], 1.0, None, ALU.add, ALU.bypass, reads=[gsc], writes=[gsc])
        p.op("dve", lambda e: e.reciprocal(gsc[:, 2:3], gsc[:, 2:3]), reads=[gsc], writes=[gsc])
        p.tt("dve", gsc[:, 2:3], gsc[:, 2:3], k.masks[:, 3:4], ALU.mult, reads=[gsc, k.masks], writes=[gsc])
        p.tt("dve", fv[:, 0:8], fv[:, 0:8], fv[:, 8:16], ALU.mult, reads=[k.ln_xb], writes=[k.ln_xb],
             tag=f"gm1_{tl}")
        p.op("dve", lambda e, g8=g8: e.tensor_scalar(g8, fv[:, 0:8], gsc[:, 2:3], None,
                                                     ALU.mult, ALU.bypass),
             reads=[k.ln_xb, gsc], writes=[gtm], tag=f"gfin{tl}")
    g8r = k.sub("g8r", 2, 0, 2048, BF16)
    for tl in range(16):
        p.tr(k.psC[0:8, 0:128], gtm[:, tl * 8:(tl + 1) * 8], k.ident.ap,
             reads=[gtm, k.ident], writes=[k.psC], tag=f"g8tr{tl}")
        p.copy("dve", g8r.ap[0:8, tl * 128:(tl + 1) * 128], k.psC[0:8, 0:128],
               reads=[k.psC], writes=[g8r], tag=f"g8ev{tl}")
    grep = k.sub("grep", 2, 1024, 2048, F32)
    h1g = [k.sub("h1gA", 0, 0, 8192, BF16), k.sub("h1gB", 1, 0, 8192, BF16)]
    gelsets = [[k.sub(f"gel{s}_{i}", 3, (s * 3 + i) * 512, 512, F32) for i in range(3)] for s in range(2)]
    b2sb = k.sub("b2sb", 3, 7168, 1024, BF16)
    k.ohot8 = k.sub("ohot8", 3, 6144, 1024, BF16)
    p.dma("pool", lambda e: e.dma_start(out=k.ohot8.ap[0:8, :], in_=k.i_ohot.ap), writes=[k.ohot8], tag="ldoh")
    p.dma("pool", lambda e: e.dma_start(out=b2sb.ap[0:8, :], in_=k.i_b2r.ap), writes=[b2sb], tag="ldb2")
    for e_ in range(E):
        for ch in range(4):
            pst = (k.psC, k.psD)[ch % 2]
            p.mm(pst[:, 0:512], k.ohot8.ap[0:8, e_ * 128:(e_ + 1) * 128],
                 g8r.ap[0:8, ch * 512:(ch + 1) * 512], True, True,
                 reads=[k.ohot8, g8r], writes=[pst], tag=f"grbc{e_}_{ch}")
            p.copy("dve", grep.ap[:, ch * 512:(ch + 1) * 512], pst[:, 0:512], reads=[pst], writes=[grep],
                   tag=f"grev{e_}_{ch}")
        for fq in range(4):
            for fg in range(4):
                wb = k.wst[fg % 2]
                c0 = fq * 1024 + fg * 256
                src = k.i_w1.ap[e_ * D:(e_ + 1) * D, c0:c0 + 256].rearrange("(cc p) q -> p cc q", p=128)
                p.dma("pool", lambda e, wb=wb, src=src: e.dma_start(
                    out=wb.ap.rearrange("p (cc q) -> p cc q", cc=8), in_=src),
                    writes=[wb], tag=f"ldw1_{e_}_{fq}{fg}")
                for fo in range(2):
                    ftq = fg * 2 + fo
                    b1c = k.col(MB1C + e_ * 32 + fq * 8 + ftq)
                    for tch in range(4):
                        gel = gelsets[(ftq * 4 + tch) % 2]
                        pst = (k.psA, k.psB)[ftq % 2]
                        reg = pst[:, (tch % 2) * 512:(tch % 2) * 512 + 512]
                        for c in range(8):
                            p.mm(reg, wb[:, c * 256 + fo * 128:c * 256 + fo * 128 + 128],
                                 x3b[c].ap[:, tch * 512:(tch + 1) * 512], c == 0, c == 7,
                                 reads=[wb, x3b[c]], writes=[pst], tag=f"o1_{e_}_{fq}{ftq}_{tch}")
                        p.act(gel[0].ap, reg, AF.Identity, bias=b1c,
                              reads=[pst, k.consts], writes=[gel[0]], tag=f"gx{e_}_{fq}{ftq}_{tch}")
                        p.act(gel[1].ap, reg, AF.Square, bias=b1c, reads=[pst, k.consts], writes=[gel[1]],
                              tag=f"ga{e_}_{fq}{ftq}_{tch}")
                        p.ts("dve", gel[1].ap, gel[1].ap, GELU_C, 1.0, ALU.mult, ALU.add,
                             reads=[gel[1]], writes=[gel[1]], tag=f"gb{e_}")
                        p.tt("dve", gel[2].ap, gel[1].ap, gel[0].ap, ALU.mult,
                             reads=[gel[1], gel[0]], writes=[gel[2]], tag=f"gc{e_}")
                        p.act(gel[2].ap, gel[2].ap, AF.Sigmoid, scale=GELU_S,
                              reads=[gel[2]], writes=[gel[2]], tag=f"gs{e_}")
                        p.tt("dve", gel[0].ap, gel[2].ap, gel[0].ap, ALU.mult,
                             reads=[gel[2], gel[0]], writes=[gel[0]], tag=f"gg{e_}")
                        h1gd = h1g[ftq // 4].ap[:, (ftq % 4) * 2048 + tch * 512:(ftq % 4) * 2048 + tch * 512 + 512]
                        p.tt("dve", h1gd, gel[0].ap, grep.ap[:, tch * 512:(tch + 1) * 512], ALU.mult,
                             reads=[gel[0], grep], writes=[h1g[ftq // 4]], tag=f"hg{e_}_{fq}{ftq}_{tch}")
            for f2 in range(8):
                wb = k.wst[f2 % 2]
                src = k.i_w2.ap[e_ * FF + fq * 1024: e_ * FF + (fq + 1) * 1024,
                                128 * f2:128 * (f2 + 1)].rearrange("(cc p) q -> p cc q", p=128)
                p.dma("pool", lambda e, wb=wb, src=src: e.dma_start(
                    out=wb.ap[:, 0:1024].rearrange("p (cc q) -> p cc q", cc=8), in_=src),
                    writes=[wb], tag=f"ldw2_{e_}_{fq}{f2}")
                for tch in range(4):
                    pst = (k.psC, k.psD)[f2 % 2]
                    for cq in range(8):
                        h1s = h1g[cq // 4].ap[:, (cq % 4) * 2048 + tch * 512:(cq % 4) * 2048 + tch * 512 + 512]
                        p.mm(pst[:, 0:512], wb[:, cq * 128:(cq + 1) * 128], h1s,
                             cq == 0, cq == 7 and fq < 3, reads=[wb, h1g[cq // 4]], writes=[pst],
                             tag=f"o2_{e_}_{fq}{f2}_{tch}")
                    if fq == 3:
                        p.mm(pst[:, 0:512], b2sb[0:8, f2 * 128:(f2 + 1) * 128],
                             g8r.ap[0:8, tch * 512:(tch + 1) * 512], False, True,
                             reads=[b2sb, g8r], writes=[pst], tag=f"b2_{e_}_{f2}_{tch}")
                    p.tt("dve", k.h[f2][:, tch * 512:(tch + 1) * 512], k.h[f2][:, tch * 512:(tch + 1) * 512],
                         pst[:, 0:512], ALU.add, reads=[k.h[f2], pst], writes=[k.h[f2]],
                         tag=f"hadd{e_}_{fq}{f2}_{tch}")


def emit_outer(k):
    p = k.p
    xtm = [k.sub(f"otm{i}", i, 0, 1024, F32) for i in range(2)]
    for i in range(16):
        b = xtm[i % 2]
        src = k.i_x[3 + 128 * i: 3 + 128 * (i + 1), :]
        p.dma("sp", lambda e, b=b, src=src: e.dma_start(out=b.ap, in_=src), writes=[b], tag=f"rlx{i}")
        ps = (k.psA, k.psB)[i % 2]
        for c in range(8):
            p.tr(ps[:, 128 * c:128 * (c + 1)], b.ap[:, 128 * c:128 * (c + 1)], k.ident.ap,
                 reads=[b, k.ident], writes=[ps], tag=f"rtr{i}_{c}")
        for c in range(8):
            p.tt("dve", k.h[c][:, 128 * i:128 * (i + 1)], k.h[c][:, 128 * i:128 * (i + 1)],
                 ps[:, 128 * c:128 * (c + 1)], ALU.add, reads=[k.h[c], ps], writes=[k.h[c]],
                 tag=f"radd{i}_{c}")
    for ci in range(4):
        a = ci * 512
        emit_ln(k, f"ln4c{ci}",
                lambda c, aa, w, a=a: k.h[c][:, a + aa: a + aa + w], lambda c: k.h[c],
                [(lambda c, aa, w, a=a: k.h[c][:, a + aa: a + aa + w], lambda c: k.h[c])],
                None, 1e-5, width=512)


def emit_ffn(k):
    p = k.p
    xpl = k.sbuf("xpl", [128, 8], F32)
    for c in range(8):
        p.copy("dve", xpl[:, c:c + 1], k.h[c][:, T - 1:T], reads=[k.h[c]], writes=[xpl], tag=f"xpl{c}")
    p.dma("sp", lambda e: e.dma_start(out=k.g_xp_in.ap, in_=xpl.ap), reads=[xpl], writes=[k.g_xp_in], tag="stxp")
    p.coll(lambda e: e.collective_compute("AllGather", ALU.bypass, replica_groups=[list(range(NCORES))],
                                          ins=[k.g_xp_in.ap.opt()], outs=[k.g_xp_out.ap.opt()]),
           reads=[k.g_xp_in], writes=[k.g_xp_out], tag="AG_xp")
    xpg = k.sbuf("xpg", [128, 64], F32)
    p.dma("sp", lambda e: e.dma_start(
        out=xpg.ap.rearrange("p (r cc) -> p r cc", r=8),
        in_=k.g_xp_out.ap.rearrange("(r p) cc -> p r cc", p=128)),
        reads=[k.g_xp_out], writes=[xpg], tag="ldxp")
    xpb = k.sbuf("xpb", [128, 8], F32)
    p.op("dve", lambda e: e.memset(xpb.ap, 0.0), writes=[xpb])
    for r in range(NCORES):
        p.stt("dve", xpb.ap, xpg.ap.rearrange("p (r cc) -> p r cc", r=8)[:, r, :], k.col(SELP + r),
              xpb.ap, ALU.mult, ALU.add, reads=[xpg, k.consts, xpb], writes=[xpb], tag=f"xpk{r}")
    xrb = [k.sub(f"xrb{c}", 4 + c // 4, (c % 4) * 2048, 2048, BF16) for c in range(8)]
    xkb = [k.sub(f"xkb{c}", c // 4, (c % 4) * 2048, 2048, BF16) for c in range(8)]
    dsc = k.sub("ffd", 2, 0, 2048, F32)
    for c in range(8):
        p.tt("dve", dsc.ap[:, 1:2048], k.h[c][:, 1:2048], k.h[c][:, 0:2047], ALU.subtract,
             reads=[k.h[c]], writes=[dsc], tag=f"fd{c}")
        p.tt("dve", dsc.ap[:, 0:1], k.h[c][:, 0:1], xpb[:, c:c + 1], ALU.subtract,
             reads=[k.h[c], xpb], writes=[dsc], tag=f"fd0{c}")
        p.stt("dve", xrb[c].ap[:, 1:2048], dsc.ap[:, 1:2048], k.col(MURC + c), k.h[c][:, 0:2047],
              ALU.mult, ALU.add, reads=[dsc, k.consts, k.h[c]], writes=[xrb[c]], tag=f"fr{c}")
        p.stt("dve", xkb[c].ap[:, 1:2048], dsc.ap[:, 1:2048], k.col(MUKC + c), k.h[c][:, 0:2047],
              ALU.mult, ALU.add, reads=[dsc, k.consts, k.h[c]], writes=[xkb[c]], tag=f"fk{c}")
        p.stt("dve", xrb[c].ap[:, 0:1], dsc.ap[:, 0:1], k.col(MURC + c), xpb[:, c:c + 1],
              ALU.mult, ALU.add, reads=[dsc, k.consts, xpb], writes=[xrb[c]], tag=f"fr0{c}")
        p.stt("dve", xkb[c].ap[:, 0:1], dsc.ap[:, 0:1], k.col(MUKC + c), xpb[:, c:c + 1],
              ALU.mult, ALU.add, reads=[dsc, k.consts, xpb], writes=[xkb[c]], tag=f"fk0{c}")
    wT = k.sub("ffwT", 3, 0, 8192, BF16)

    def build_wT(widram, nm):
        for di in range(8):
            wb = k.wst[di % 2]
            src = widram.ap[128 * di:128 * (di + 1), :].rearrange("p (cc q) -> p cc q", cc=8)
            p.dma("pool", lambda e, wb=wb, src=src: e.dma_start(
                out=wb.ap[:, 0:1024].rearrange("p (cc q) -> p cc q", cc=8), in_=src),
                writes=[wb], tag=f"ld{nm}{di}")
            for cj in range(8):
                p.tr(k.psT[:, cj * 128:(cj + 1) * 128], wb[:, cj * 128:(cj + 1) * 128], k.identb.ap,
                     reads=[wb, k.identb], writes=[k.psT], tag=f"{nm}tr{di}_{cj}")
            p.copy("dve", wT.ap.rearrange("p (cj di q) -> p cj di q", cj=8, di=8)[:, :, di, :],
                   k.psT[:, 0:1024].rearrange("p (cj q) -> p cj q", cj=8),
                   reads=[k.psT], writes=[wT], tag=f"{nm}ev{di}")

    def ffn_mm(xin, di, tch, pst):
        for cj in range(8):
            p.mm(pst[:, 0:512], wT.ap[:, (cj * 8 + di) * 128:(cj * 8 + di) * 128 + 128],
                 xin[cj].ap[:, tch * 512:(tch + 1) * 512], cj == 0, cj == 7,
                 reads=[wT, xin[cj]], writes=[pst], tag=f"fmm{di}_{tch}")

    rq = [k.sub(f"ffrq{d}", 2, d * 512, 512, BF16) for d in range(8)]
    kq = [k.sub(f"ffkq{d}", 2, 4096 + d * 512, 512, BF16) for d in range(8)]
    for tch in range(4):
        build_wT(k.i_wr, f"wr{tch}")
        for di in range(8):
            pst = (k.psC, k.psD)[di % 2]
            ffn_mm(xrb, di, tch, pst)
            p.act(rq[di].ap, pst[:, 0:512], AF.Sigmoid, reads=[pst], writes=[rq[di]], tag=f"sg{tch}{di}")
        build_wT(k.i_wkf, f"wk{tch}")
        for di in range(8):
            pst = (k.psC, k.psD)[di % 2]
            ffn_mm(xkb, di, tch, pst)
            p.ts("dve", k.ln_t1.ap, pst[:, 0:512], 0.0, None, ALU.max, ALU.bypass,
                 reads=[pst], writes=[k.ln_t1], tag=f"rel{tch}{di}")
            p.act(kq[di].ap, k.ln_t1.ap, AF.Square, reads=[k.ln_t1], writes=[kq[di]], tag=f"ksq{tch}{di}")
        build_wT(k.i_wvf, f"wv{tch}")
        for di in range(8):
            pst = (k.psC, k.psD)[di % 2]
            for cj in range(8):
                p.mm(pst[:, 0:512], wT.ap[:, (cj * 8 + di) * 128:(cj * 8 + di) * 128 + 128],
                     kq[cj].ap, cj == 0, cj == 7, reads=[wT, kq[cj]], writes=[pst], tag=f"vmm{tch}{di}")
            p.tt("dve", k.ln_t1.ap, pst[:, 0:512], rq[di].ap, ALU.mult,
                 reads=[pst, rq[di]], writes=[k.ln_t1], tag=f"fmul{tch}{di}")
            p.tt("dve", k.h[di][:, tch * 512:(tch + 1) * 512], k.h[di][:, tch * 512:(tch + 1) * 512],
                 k.ln_t1.ap, ALU.add, reads=[k.h[di], k.ln_t1], writes=[k.h[di]], tag=f"fadd{tch}{di}")
    for ci in range(4):
        a = ci * 512
        emit_ln(k, f"ln5c{ci}",
                lambda c, aa, w, a=a: k.h[c][:, a + aa: a + aa + w], lambda c: k.h[c],
                [(lambda c, aa, w, a=a: k.h[c][:, a + aa: a + aa + w], lambda c: k.h[c])],
                None, 1e-5, width=512)


def build_program(tap=None, debug=False, upto=None):
    nc = bass.Bass("TRN2", target_bir_lowering=False)
    k = Kern(nc, tap)
    emit_setup(k)
    emit_input(k)
    if tap:
        tap("input", k)
    stages = [("mamba", emit_mamba), ("tc", emit_tc), ("wlam", emit_wlam),
              ("moe", emit_moe), ("outer", emit_outer), ("ffn", emit_ffn)]
    for name, fn in stages:
        if upto is not None and name not in upto:
            continue
        fn(k)
        if tap:
            tap(name, k)
    emit_output_stub(k)
    k.p.finalize(debug=debug)
    return nc, k


def emit_output_stub(k):
    p = k.p
    otm = [k.sub(f"otm{i}", i, 2048, 1024, F32) for i in range(2)]
    for i in range(16):
        ps = (k.psA, k.psB)[i % 2]
        ob = otm[i % 2]
        for c in range(8):
            p.tr(ps[:, 128 * c:128 * (c + 1)], k.h[c][:, 128 * i:128 * (i + 1)], k.ident.ap,
                 reads=[k.h[c], k.ident], writes=[ps], tag=f"otr{i}_{c}")
        p.copy("dve", ob.ap, ps[:, 0:1024], reads=[ps], writes=[ob], tag=f"oev{i}")
        p.dma("sp", lambda e, ob=ob, i=i: e.dma_start(out=k.out.ap[128 * i:128 * (i + 1), :], in_=ob.ap),
              reads=[ob], writes=[k.out], tag=f"ost{i}")


def pack_cols(I):
    cols = np.zeros((128, COLS_W), np.float32)
    for i in range(4):
        for kk in range(8):
            cols[:, LNG + 8 * i + kk] = I["L0_attn_ln_g"][i][128 * kk:128 * (kk + 1)]
            cols[:, LNB + 8 * i + kk] = I["L0_attn_ln_b"][i][128 * kk:128 * (kk + 1)]
    for kk in range(16):
        cols[:, ALOGC + kk] = I["L0_attn_m_alog"][128 * kk:128 * (kk + 1)]
        cols[:, WDTC + kk] = I["L0_attn_m_wdt"][128 * kk:128 * (kk + 1)]
        cols[:, DTBC + kk] = I["L0_attn_m_dtb"][128 * kk:128 * (kk + 1)]
        cols[:, DSKC + kk] = I["L0_attn_m_dskip"][128 * kk:128 * (kk + 1)]
        for i in range(4):
            cols[:, CONVC + i * 16 + kk] = I["L0_attn_m_conv"][i][128 * kk:128 * (kk + 1)]
    for kk in range(8):
        cols[:, TCBC + kk] = I["L0_attn_tc_b"][128 * kk:128 * (kk + 1)]
        cols[:, MURC + kk] = I["L0_ffn_mu_r"][128 * kk:128 * (kk + 1)]
        cols[:, MUKC + kk] = I["L0_ffn_mu_k"][128 * kk:128 * (kk + 1)]
        for i in range(4):
            cols[:, WCVC + i * 8 + kk] = I["L0_attn_wlam_conv"][i][128 * kk:128 * (kk + 1)]
    cols[0:4, RBC] = I["L0_attn_router_b"]
    cols[:, THC] = float(I["L0_attn_tc_theta"])
    for e in range(8):
        for kf in range(32):
            cols[:, MB1C + e * 32 + kf] = I["L0_attn_moe_b1"][e][128 * kf:128 * (kf + 1)]
        for kk in range(8):
            cols[:, MB2C + e * 8 + kk] = I["L0_attn_moe_b2"][e][128 * kk:128 * (kk + 1)]
    return cols


def make_inmaps(inputs):
    I = {kk: np.asarray(v) for kk, v in inputs.items()}
    x = np.ascontiguousarray(I["L0_attn_hidden_states"][0])
    cols_base = pack_cols(I)
    tri = np.asarray(np.triu(np.ones((CK, CK))), np.float32)
    ident = np.eye(128, dtype=np.float32)
    shared = {
        "m_win": I["L0_attn_m_win"], "m_wbc": I["L0_attn_m_wbc"], "m_wout": I["L0_attn_m_wout"],
        "tc_w": I["L0_attn_tc_w"], "w_q": I["L0_attn_w_q"], "w_k": I["L0_attn_w_k"],
        "w_v": I["L0_attn_w_v"], "w_o": I["L0_attn_w_o"], "moe_wg": I["L0_attn_moe_wg"],
        "moe_w1": I["L0_attn_moe_w1"].reshape(E * D, FF),
        "moe_w2": I["L0_attn_moe_w2"].reshape(E * FF, D),
        "ffn_wr": I["L0_ffn_W_r"], "ffn_wk": I["L0_ffn_W_k"], "ffn_wv": I["L0_ffn_W_v"],
        "router_w": I["L0_attn_router_w"], "identity": ident, "tri": tri,
        "moe_b2r": I["L0_attn_moe_b2"],
        "ohot": np.repeat(np.eye(8, dtype=np.float32), 128, axis=1).reshape(8, 8, 128).transpose(0, 1, 2).reshape(8, 1024) if False else np.kron(np.eye(8, dtype=np.float32), np.ones((1, 128), np.float32)),
    }
    shared = {kk: np.ascontiguousarray(np.asarray(v, np.float32)) for kk, v in shared.items()}
    in_maps = []
    for r in range(NCORES):
        t0 = r * T
        xp = np.zeros((T + 3, D), np.float32)
        xp[3:] = x[t0:t0 + T]
        if r > 0:
            xp[0:3] = x[t0 - 3:t0]
        halo = np.zeros((D, 4), np.float32)
        halo[:, 0:3] = xp[0:3].T
        cols = cols_base.copy()
        for j in range(8):
            cols[:, SELR + j] = 1.0 if j == r else 0.0
            cols[:, SELP + j] = 1.0 if j == r - 1 else 0.0
            cols[:, SELN + j] = 1.0 if j == r + 1 else 0.0
        m = dict(shared)
        m["x_pad"] = xp
        m["x_halo_fm"] = np.ascontiguousarray(halo)
        m["cols"] = np.ascontiguousarray(cols)
        in_maps.append(m)
    return in_maps


def kernel(**inputs):
    nc, _ = build_program()
    in_maps = make_inmaps(inputs)
    res = run_bass_kernel_spmd(nc, in_maps, core_ids=list(range(NCORES)))
    return np.concatenate([r["y"] for r in res.results], axis=0)[None]


# version 07
# speedup vs baseline: 1.1368x; 1.0577x over previous
import contextlib
import numpy as np
import concourse.bass as bass
import concourse.mybir as mybir
from concourse.bass_utils import run_bass_kernel_spmd

F32 = mybir.dt.float32
BF16 = mybir.dt.bfloat16
AF = mybir.ActivationFunctionType
ALU = mybir.AluOpType
AX = mybir.AxisListType

STREAMS = ("pe", "dve", "act", "pool", "sp")

NCORES = 8
T = 2048
D = 1024
DI = 2048
NST = 64
CK = 64
SEC = 256
NSEC = T // SEC
NCHS = SEC // CK
B3 = SEC + 3
NH = 16
HD = 64
TP = T // 2
E = 8
FF = 4096
LOG_THR = float(np.log(np.float32(0.3) / np.float32(0.7)))
GELU_S = float(2.0 * 0.7978845608028654)
GELU_C = 0.044715
DBG = []


class Buf:
    __slots__ = ("name", "t", "lw", "readers")

    def __init__(self, name, t):
        self.name, self.t = name, t
        self.lw, self.readers = None, []

    @property
    def root(self):
        return self

    def __getitem__(self, idx):
        return self.t[idx]

    @property
    def ap(self):
        return self.t[:]


class SubBuf:
    __slots__ = ("name", "parent", "off", "width", "dtype")

    def __init__(self, name, parent, off, width, dtype):
        self.name, self.parent, self.off, self.width, self.dtype = name, parent, off, width, dtype

    def __getitem__(self, idx):
        return self.ap[idx]

    @property
    def root(self):
        return self.parent

    @property
    def ap(self):
        a = self.parent.t[:].bitcast(self.dtype)
        return a[:, self.off:self.off + self.width]


class Op:
    __slots__ = ("stream", "sidx", "fn", "dma", "coll", "inc_marked",
                 "milestone", "waits", "drain_before", "tag")

    def __init__(self, stream, sidx, fn, dma, coll, tag):
        self.stream, self.sidx, self.fn = stream, sidx, fn
        self.dma, self.coll = dma, coll
        self.inc_marked = dma or coll
        self.milestone = None
        self.waits = {}
        self.drain_before = False
        self.tag = tag


class Prog:
    def __init__(self, nc):
        self.nc = nc
        self.es = contextlib.ExitStack()
        self.sops = {s: [] for s in STREAMS}
        self.nops = 0
        self._names = set()
        self.csem = {s: self.es.enter_context(nc.semaphore("c_" + s)) for s in STREAMS}
        self.NQL = 8
        self.qsem = {s: [self.es.enter_context(nc.semaphore(f"q_{s}{i}")) for i in range(self.NQL)]
                     for s in ("sp", "act", "pool")}
        self.ccsem = self.es.enter_context(nc.semaphore("ccq"))
        self._drain_cover = {s: -1 for s in STREAMS}
        self._qcnt = {s: 0 for s in ("sp", "act", "pool")}
        self._cccnt = 0

    def _nm(self, name):
        assert name not in self._names, name
        self._names.add(name)
        return name

    def sbuf(self, name, shape, dtype):
        return Buf(name, self.es.enter_context(self.nc.sbuf_tensor(self._nm(name), list(shape), dtype)))

    def psum(self, name, shape, dtype):
        return Buf(name, self.es.enter_context(self.nc.psum_tensor(self._nm(name), list(shape), dtype)))

    def dram_in(self, name, shape, dtype):
        return Buf(name, self.nc.dram_tensor(self._nm(name), list(shape), dtype, kind="ExternalInput").ap())

    def dram_out(self, name, shape, dtype):
        return Buf(name, self.nc.dram_tensor(self._nm(name), list(shape), dtype, kind="ExternalOutput").ap())

    def dram_tmp(self, name, shape, dtype):
        return Buf(name, self.nc.dram_tensor(self._nm(name), list(shape), dtype).ap())

    def op(self, stream, fn, reads=(), writes=(), dma=False, coll=False, tag=""):
        o = Op(stream, len(self.sops[stream]), fn, dma, coll, tag)
        self.nops += 1
        self.sops[stream].append(o)
        if dma:
            qn = self._qcnt[stream]
            self._qcnt[stream] = qn + 1
            o.milestone = (f"q_{stream}{qn % self.NQL}", 16 * (qn // self.NQL + 1))
        elif coll:
            self._cccnt += 1
            o.milestone = ("cc", self._cccnt)
        deps = []
        for b in reads:
            b = b.root
            if b.lw is not None:
                deps.append(b.lw)
        wroots = [b.root for b in writes]
        for b in wroots:
            if b.lw is not None:
                deps.append(b.lw)
            deps.extend(b.readers)
        best = {}
        for d in deps:
            if d is o:
                continue
            kk = d.milestone[0] if (d.dma or d.coll) else (d.stream,)
            prev = best.get(kk)
            if prev is None or d.sidx > prev.sidx:
                best[kk] = d
        need_drain = False
        for d in best.values():
            if d.stream == stream and not d.dma and not d.coll:
                if stream == "pe":
                    continue
                if d.sidx > self._drain_cover[stream]:
                    need_drain = True
                continue
            d.inc_marked = True
            o.waits[d] = True
        if need_drain:
            o.drain_before = True
            self._drain_cover[stream] = o.sidx - 1
        for b in reads:
            b.root.readers.append(o)
        for b in wroots:
            b.lw = o
            b.readers = []
        return o

    def dma(self, stream, fn, reads=(), writes=(), tag=""):
        assert stream in ("sp", "act", "pool")
        return self.op(stream, fn, reads, writes, dma=True, tag=tag)

    def coll(self, fn, reads=(), writes=(), tag=""):
        return self.op("pool", fn, reads, writes, coll=True, tag=tag)

    def tt(self, st, out, a, b, alu, reads, writes, tag=""):
        self.op(st, lambda e: e.tensor_tensor(out, a, b, alu), reads, writes, tag=tag)

    def ts(self, st, out, a, s1, s2, op0, op1, reads, writes, tag=""):
        self.op(st, lambda e: e.tensor_scalar(out, a, s1, s2, op0, op1), reads, writes, tag=tag)

    def stt(self, st, out, a, s, b, op0, op1, reads, writes, tag=""):
        self.op(st, lambda e: e.scalar_tensor_tensor(out, a, s, b, op0, op1), reads, writes, tag=tag)

    def act(self, out, a, fn, reads, writes, scale=None, bias=None, accum=None, tag=""):
        kw = {}
        if scale is not None:
            kw["scale"] = scale
        if bias is not None:
            kw["bias"] = bias
        if accum is not None:
            kw["accum_out"] = accum
        self.op("act", lambda e: e.activation(out, a, fn, **kw), reads, writes, tag=tag)

    def copy(self, st, out, a, reads, writes, tag=""):
        self.op(st, lambda e: e.tensor_copy(out, a), reads, writes, tag=tag)

    def mm(self, out, lhsT, rhs, start, stop, reads, writes, tag=""):
        self.op("pe", lambda e: e.matmul(out, lhsT, rhs, start=start, stop=stop), reads, writes, tag=tag)

    def tr(self, out, in_, ident, reads, writes, tag=""):
        self.op("pe", lambda e: e.transpose(out, in_, ident), reads, writes, tag=tag)

    def finalize(self, debug=False):
        totals = {}
        qlane_tot = {}
        for s in STREAMS:
            cc = 0
            for o in self.sops[s]:
                if o.dma or o.coll or not o.inc_marked:
                    continue
                cc += 1
                o.milestone = ("c_" + s, cc)
            totals[s] = (self._qcnt.get(s, 0) * 16, self._cccnt if s == "pool" else 0, cc)
        for s in ("sp", "act", "pool"):
            qn = self._qcnt[s]
            qlane_tot[s] = [16 * ((qn - 1 - i) // self.NQL + 1) if i < min(qn, self.NQL) else 0
                            for i in range(self.NQL)]
        sem_of = {("c_" + s): self.csem[s] for s in STREAMS}
        for s in ("sp", "act", "pool"):
            for i in range(self.NQL):
                sem_of[f"q_{s}{i}"] = self.qsem[s][i]
        sem_of["cc"] = self.ccsem
        nwaits = [0]

        def emit(eng, s):
            hw = {}
            for o in self.sops[s]:
                if o.drain_before:
                    eng.drain()
                byname = {}
                for d in o.waits:
                    nm, v = d.milestone
                    if byname.get(nm, 0) < v:
                        byname[nm] = v
                for nm in sorted(byname):
                    v = byname[nm]
                    if hw.get(nm, 0) >= v:
                        continue
                    eng.wait_ge(sem_of[nm], v)
                    hw[nm] = v
                    nwaits[0] += 1
                inst = o.fn(eng)
                if o.inc_marked:
                    nm, _ = o.milestone
                    inst.then_inc(sem_of[nm], 16 if o.dma else 1)

        block = self.es.enter_context(self.nc.Block())

        @block.tensor
        def _(e):
            emit(e, "pe")

        @block.vector
        def _(e):
            emit(e, "dve")

        @block.scalar
        def _(e):
            emit(e, "act")

        @block.gpsimd
        def _(e):
            emit(e, "pool")

        @block.sync
        def _(e):
            emit(e, "sp")
            for s in ("sp", "act", "pool"):
                for i in range(self.NQL):
                    if qlane_tot[s][i]:
                        e.wait_ge(self.qsem[s][i], qlane_tot[s][i])
            if totals["pool"][1]:
                e.wait_ge(self.ccsem, totals["pool"][1])

        if debug:
            print(f"Prog: {self.nops} ops, {nwaits[0]} waits; totals={totals}")
        return totals


def ceil_div(a, b):
    return -(-a // b)


LNG, LNB = 0, 32
ALOGC, WDTC, DTBC, DSKC = 64, 80, 96, 112
TCBC, MURC, MUKC = 128, 136, 144
SELR, SELP, SELN = 152, 160, 168
CONVC = 176
RBC, THC = 240, 241
MB1C, MB2C = 256, 512
WCVC = 576
COLS_W = 640
KVW = 1040


class Kern:
    def __init__(self, nc, tap=None):
        self.nc = nc
        self.p = Prog(nc)
        self.tap = tap
        p = self.p
        s = p.sbuf

        self.i_x = p.dram_in("x_pad", [T + 3, D], F32)
        self.i_xhalo = p.dram_in("x_halo_fm", [D, 4], F32)
        self.i_win = p.dram_in("m_win", [D, 2 * DI], F32)
        self.i_wbc = p.dram_in("m_wbc", [DI, 2 * NST], F32)
        self.i_wout = p.dram_in("m_wout", [DI, D], F32)
        self.i_tcw = p.dram_in("tc_w", [D, D], F32)
        self.i_wq = p.dram_in("w_q", [D, D], F32)
        self.i_wk = p.dram_in("w_k", [D, D], F32)
        self.i_wv = p.dram_in("w_v", [D, D], F32)
        self.i_wo = p.dram_in("w_o", [D, D], F32)
        self.i_wg = p.dram_in("moe_wg", [D, E], F32)
        self.i_w1 = p.dram_in("moe_w1", [E * D, FF], F32)
        self.i_w2 = p.dram_in("moe_w2", [E * FF, D], F32)
        self.i_wr = p.dram_in("ffn_wr", [D, D], F32)
        self.i_wkf = p.dram_in("ffn_wk", [D, D], F32)
        self.i_wvf = p.dram_in("ffn_wv", [D, D], F32)
        self.i_rw = p.dram_in("router_w", [D, 4], F32)
        self.i_b2r = p.dram_in("moe_b2r", [8, 1024], F32)
        self.i_ohot = p.dram_in("ohot", [8, 1024], F32)
        self.i_cols = p.dram_in("cols", [128, COLS_W], F32)
        self.i_ident = p.dram_in("identity", [128, 128], F32)
        self.i_tri = p.dram_in("tri", [CK, CK], F32)
        self.out = p.dram_out("y", [T, D], F32)

        self.g_pool_in = p.dram_tmp("cc_pool_in", [128, 8], F32)
        self.g_pool_out = p.dram_tmp("cc_pool_out", [128, 8], F32)
        self.g_mst_in = p.dram_tmp("cc_mst_in", [DI, NST], F32)
        self.g_mp_in = p.dram_tmp("cc_mp_in", [128, 16], F32)
        self.g_mp_out = p.dram_tmp("cc_mp_out", [NCORES * 128, 16], F32)
        self.g_mst_out = p.dram_tmp("cc_mst_out", [NCORES * DI, NST], F32)
        self.g_kv_in = p.dram_tmp("cc_kv_in", [2 * D + 16, KVW], BF16)
        self.g_kv_out = p.dram_tmp("cc_kv_out", [NCORES * (2 * D + 16), KVW], BF16)
        self.g_yw_in = p.dram_tmp("cc_yw_in", [D, 4], F32)
        self.g_yw_out = p.dram_tmp("cc_yw_out", [NCORES * D, 4], F32)
        self.g_xp_in = p.dram_tmp("cc_xp_in", [128, 8], F32)
        self.g_xp_out = p.dram_tmp("cc_xp_out", [NCORES * 128, 8], F32)

        self.h = [s(f"h{c}", [128, T], F32) for c in range(8)]
        self.consts = s("consts", [128, COLS_W], F32)
        self.ident = s("identf", [128, 128], F32)
        self.identb = s("identb", [128, 128], BF16)
        self.onesb = s("onesb", [128, 128], BF16)
        self.onesf = s("onesf", [128, 256], F32)
        self.trib = s("trib", [CK, CK], F32)
        self.masks = s("masksrep", [128, 4], F32)
        self.rows = s("rows", [1, 1536], F32)

        _arw = [4144, 4096, 4096, 4096, 4160, 4096]
        self.ar = [s(f"ar{i}", [128, _arw[i]], F32) for i in range(6)]

        self.psA = p.psum("psA", [128, 1024], F32)
        self.psB = p.psum("psB", [128, 1024], F32)
        self.psC = p.psum("psC", [128, 512], F32)
        self.psD = p.psum("psD", [128, 512], F32)
        self.psT = p.psum("psT", [128, 2048], BF16)
        self._psi = 0

        self.ln_xb = s("ln_xb", [128, 512], BF16)
        self.ln_xb2 = s("ln_xb2", [128, 512], BF16)
        self.ln_sq = s("ln_sq", [128, 512], BF16)
        self.ln_sq2 = s("ln_sq2", [128, 512], BF16)
        self.ln_t1 = s("ln_t1", [128, 512], F32)

        self.wst = [s(f"wst{i}", [128, 2048], BF16) for i in range(2)]

        self.xh = s("xh", [128, 32], F32)
        self.mlnb = [s(f"mlnb{c}", [128, B3], BF16) for c in range(8)]
        self.mBb = s("mBb", [64, SEC], BF16)
        self.mCb = s("mCb", [64, SEC], BF16)
        self.mwtm = s("mwtm", [64, 2048], BF16)
        self.mhtm = s("mhtm", [64, 2048], BF16)
        self.mbtm = s("mbtm", [64, 64], BF16)
        self.mhbf = s("mhbf", [128, 16 * 64], BF16)
        self.mhst = s("mhst", [128, 16 * 64], F32)
        self.mqcol = s("mqcol", [128, 16], F32)
        self.mdtot = s("mdtot", [128, 16], F32)
        self.mec = s("mec", [128, 16 * NCHS], F32)
        self.mpc = s("mpc", [128, 16 * NCHS], F32)
        self.mmc = s("mmc", [128, 16 * NCHS], F32)
        self.mdh = s("mdh", [128, 16 * NCHS], F32)
        self.mdf = s("mdf", [128, 16 * NCHS], F32)
        self.mgm = s("mgm", [64, CK], BF16)
        self.pooled = s("pooled", [128, 8], F32)
        self.poolg = s("poolg", [128, 8], F32)
        self.rwsb = s("rwsb", [128, 32], F32)
        self.lg = s("lg", [4, 1], F32)
        self.mrow = s("mrow", [1, 4], F32)

    def sbuf(self, name, shape, dtype):
        return self.p.sbuf(name, shape, dtype)

    def next_ps(self):
        b = (self.psA, self.psB)[self._psi % 2]
        self._psi += 1
        return b

    def sub(self, name, ai, off, width, dtype):
        return SubBuf(name, self.ar[ai], off, width, dtype)

    def col(self, idx, n=1):
        return self.consts[:, idx:idx + n]

    def cbc(self, idx, nblk, w):
        return self.consts[:, idx:idx + nblk].unsqueeze(2).broadcast_to([128, nblk, w])


def blkv(buf, nblk, off=0, w=None):
    a = buf.ap
    q = a.shape[1] // nblk
    a = a.rearrange("p (b q) -> p b q", b=nblk)
    if w is None:
        return a
    return a[:, :, off:off + w]


def emit_setup(k):
    p = k.p
    p.dma("sp", lambda e: e.dma_start(out=k.consts.ap, in_=k.i_cols.ap), writes=[k.consts], tag="ld_cols")
    p.dma("sp", lambda e: e.dma_start(out=k.ident.ap, in_=k.i_ident.ap), writes=[k.ident])
    p.dma("pool", lambda e: e.dma_start(out=k.identb.ap, in_=k.i_ident.ap), writes=[k.identb])
    p.dma("sp", lambda e: e.dma_start(out=k.trib.ap, in_=k.i_tri.ap), writes=[k.trib])
    p.op("dve", lambda e: e.memset(k.onesf.ap, 1.0), writes=[k.onesf])
    p.op("dve", lambda e: e.memset(k.onesb.ap, 1.0), writes=[k.onesb])
    acol = k.consts[:, ALOGC:ALOGC + 16]
    p.act(acol, acol, AF.Exp, reads=[k.consts], writes=[k.consts], tag="A_exp")
    p.ts("dve", acol, acol, -1.0, None, ALU.mult, ALU.bypass, reads=[k.consts], writes=[k.consts], tag="A_neg")


def emit_input(k):
    p = k.p
    xtm = [k.sub(f"xtm{i}", i, 0, 1024, F32) for i in range(2)]
    for i in range(16):
        b = xtm[i % 2]
        src = k.i_x[3 + 128 * i: 3 + 128 * (i + 1), :]
        p.dma("sp", lambda e, b=b, src=src: e.dma_start(out=b.ap, in_=src), writes=[b], tag=f"ldx{i}")
        ps = (k.psA, k.psB)[i % 2]
        for c in range(8):
            p.tr(ps[:, 128 * c:128 * (c + 1)], b.ap[:, 128 * c:128 * (c + 1)], k.ident.ap,
                 reads=[b, k.ident], writes=[ps], tag=f"trx{i}_{c}")
        for c in range(8):
            p.copy("dve", k.h[c][:, 128 * i:128 * (i + 1)], ps[:, 128 * c:128 * (c + 1)],
                   reads=[ps], writes=[k.h[c]], tag=f"evx{i}_{c}")
    for c in range(8):
        p.op("dve", lambda e, c=c: e.tensor_reduce(k.pooled[:, c:c + 1], k.h[c].ap, AX.X, ALU.add),
             reads=[k.h[c]], writes=[k.pooled], tag=f"pool{c}")
    gi, go = k.g_pool_in, k.g_pool_out
    p.dma("sp", lambda e: e.dma_start(out=gi.ap, in_=k.pooled.ap), reads=[k.pooled], writes=[gi])
    p.coll(lambda e: e.collective_compute("AllReduce", ALU.add, replica_groups=[list(range(NCORES))],
                                          ins=[gi.ap.opt()], outs=[go.ap.opt()]),
           reads=[gi], writes=[go], tag="AR_pool")
    p.dma("sp", lambda e: e.dma_start(out=k.poolg.ap, in_=go.ap), reads=[go], writes=[k.poolg])
    p.ts("dve", k.poolg.ap, k.poolg.ap, 1.0 / (NCORES * T), None, ALU.mult, ALU.bypass,
         reads=[k.poolg], writes=[k.poolg], tag="poolmean")
    p.dma("sp", lambda e: e.dma_start(out=k.rwsb.ap.rearrange("p (kk f) -> p kk f", kk=8),
                                      in_=k.i_rw.ap.rearrange("(kk p) f -> p kk f", p=128)),
          writes=[k.rwsb])
    psr = k.psC
    for c in range(8):
        p.mm(psr[0:4, 0:1], k.rwsb[:, 4 * c:4 * c + 4], k.poolg[:, c:c + 1], c == 0, c == 7,
             reads=[k.rwsb, k.poolg], writes=[psr], tag=f"rlog{c}")
    p.tt("dve", k.lg.ap, psr[0:4, 0:1], k.consts[0:4, RBC:RBC + 1], ALU.add,
         reads=[psr, k.consts], writes=[k.lg])
    p.ts("dve", k.lg.ap, k.lg.ap, LOG_THR, None, ALU.is_gt, ALU.bypass, reads=[k.lg], writes=[k.lg])
    if "rt" in DBG:
        dp = p.dram_out("dbg_pooled", [128, 8], F32)
        p.dma("sp", lambda e: e.dma_start(out=dp.ap, in_=k.pooled.ap), reads=[k.pooled], writes=[dp])
        d5 = p.dram_out("dbg_lg", [4, 1], F32)
        d6 = p.dram_out("dbg_poolg", [128, 8], F32)
        p.dma("sp", lambda e: e.dma_start(out=d5.ap, in_=k.lg.ap), reads=[k.lg], writes=[d5])
        p.dma("sp", lambda e: e.dma_start(out=d6.ap, in_=k.poolg.ap), reads=[k.poolg], writes=[d6])
    p.tr(psr[0:1, 4:8], k.lg.ap, k.ident[0:4, 0:4], reads=[k.lg, k.ident], writes=[psr], tag="masktr")
    p.copy("dve", k.mrow.ap, psr[0:1, 4:8], reads=[psr], writes=[k.mrow])
    p.mm(psr[:, 8:12], k.onesf[0:1, 0:128], k.mrow.ap, True, True,
         reads=[k.onesf, k.mrow], writes=[psr], tag="maskbc")
    p.copy("dve", k.masks.ap, psr[:, 8:12], reads=[psr], writes=[k.masks])


def emit_ln(k, nm, src_aps, src_bufs, outs, ln_i, eps, width):
    p = k.p
    nch = ceil_div(width, 512)
    for ci in range(nch):
        a = ci * 512
        w = min(512, width - a)
        psS, psQ = k.psC, k.psD
        for c in range(8):
            sap = src_aps(c, a, w)
            rb = src_bufs(c)
            xb = (k.ln_xb, k.ln_xb2)[c % 2]
            sq = (k.ln_sq, k.ln_sq2)[c % 2]
            p.copy("dve", xb[:, 0:w], sap, reads=[rb], writes=[xb], tag=f"{nm}cv{c}")
            p.act(sq[:, 0:w], sap, AF.Square, reads=[rb], writes=[sq], tag=f"{nm}sq{c}")
            p.mm(psS[0:1, 0:w], k.onesb[:, 0:1], xb[:, 0:w], c == 0, c == 7,
                 reads=[k.onesb, xb], writes=[psS], tag=f"{nm}mS{c}")
            p.mm(psQ[0:1, 0:w], k.onesb[:, 0:1], sq[:, 0:w], c == 0, c == 7,
                 reads=[k.onesb, sq], writes=[psQ], tag=f"{nm}mQ{c}")
        m_r, v_r, nm_r = k.rows[0:1, 0:w], k.rows[0:1, 512:512 + w], k.rows[0:1, 1024:1024 + w]
        p.ts("dve", m_r, psS[0:1, 0:w], 1.0 / D, None, ALU.mult, ALU.bypass, reads=[psS], writes=[k.rows])
        p.tt("dve", v_r, m_r, m_r, ALU.mult, reads=[k.rows], writes=[k.rows])
        p.stt("dve", v_r, psQ[0:1, 0:w], 1.0 / D, v_r, ALU.mult, ALU.subtract,
              reads=[psQ, k.rows], writes=[k.rows])
        p.ts("dve", v_r, v_r, float(eps), None, ALU.add, ALU.bypass, reads=[k.rows], writes=[k.rows])
        p.act(v_r, v_r, AF.Ln, reads=[k.rows], writes=[k.rows])
        p.act(v_r, v_r, AF.Exp, scale=-0.5, reads=[k.rows], writes=[k.rows])
        p.tt("dve", nm_r, m_r, v_r, ALU.mult, reads=[k.rows], writes=[k.rows])
        p.ts("dve", nm_r, nm_r, -1.0, None, ALU.mult, ALU.bypass, reads=[k.rows], writes=[k.rows])
        psR, psB_ = k.psA, k.psB
        p.mm(psR[:, 0:w], k.onesf[0:1, 0:128], v_r, True, True, reads=[k.onesf, k.rows], writes=[psR], tag=f"{nm}bR")
        p.mm(psB_[:, 0:w], k.onesf[0:1, 0:128], nm_r, True, True, reads=[k.onesf, k.rows], writes=[psB_], tag=f"{nm}bB")
        for c in range(8):
            sap = src_aps(c, a, w)
            rb = src_bufs(c)
            tb = k.ln_t1
            p.tt("dve", tb[:, 0:w], sap, psR[:, 0:w], ALU.mult, reads=[rb, psR], writes=[tb], tag=f"{nm}a1_{c}")
            d0ap, d0buf = outs[0][0](c, a, w), outs[0][1](c)
            if ln_i is not None:
                p.tt("dve", tb[:, 0:w], tb[:, 0:w], psB_[:, 0:w], ALU.add, reads=[tb, psB_], writes=[tb])
                g = k.col(LNG + 8 * ln_i + c)
                bb = k.col(LNB + 8 * ln_i + c)
                p.ts("dve", d0ap, tb[:, 0:w], g, bb, ALU.mult, ALU.add,
                     reads=[tb, k.consts], writes=[d0buf], tag=f"{nm}a2_{c}")
            else:
                p.tt("dve", d0ap, tb[:, 0:w], psB_[:, 0:w], ALU.add, reads=[tb, psB_], writes=[d0buf], tag=f"{nm}a2_{c}")
            for oap, obuf in outs[1:]:
                p.copy("dve", oap(c, a, w), d0ap, reads=[d0buf], writes=[obuf(c)], tag=f"{nm}cp{c}")


def alloc_mamba(k):
    k.MB1 = k.sub("MB1", 0, 0, 16 * B3, F32)
    k.mxc = k.sub("mxc", 1, 0, 16 * SEC, F32)
    k.ma1 = k.sub("ma1", 2, 0, 16 * SEC, F32)
    k.ma4 = k.sub("ma4", 3, 0, 16 * SEC, F32)
    k.MB2 = k.sub("MB2", 4, 0, 16 * SEC, BF16)
    k.mwbf = k.sub("mwbf", 4, 16 * SEC, 16 * SEC, BF16)
    k.meqb = k.sub("meqb", 5, 0, 16 * SEC, BF16)
    k.MB3 = k.sub("MB3", 5, 16 * SEC, 16 * SEC, BF16)


def emit_mamba_section(k, s, pass2):
    p = k.p
    base = s * SEC
    if s == 0:
        p.dma("sp", lambda e: e.dma_start(
            out=k.xh.ap.rearrange("p (cc f) -> p cc f", cc=8),
            in_=k.i_xhalo.ap.rearrange("(cc p) f -> p cc f", p=128)), writes=[k.xh], tag="ldxh")
        emit_ln(k, "lnh", lambda c, a, w: k.xh[:, 4 * c + a: 4 * c + a + w],
                lambda c: k.xh,
                [(lambda c, a, w: k.mlnb[c][:, a:a + w], lambda c: k.mlnb[c])], 0, 1e-6, width=3)
    else:
        for c in range(8):
            p.copy("dve", k.mlnb[c][:, 0:3], k.mlnb[c][:, B3 - 3:B3],
                   reads=[k.mlnb[c]], writes=[k.mlnb[c]], tag=f"lead{c}")
        xzv = blkv(k.MB1, 16)
        p.copy("dve", xzv[:, :, 0:3], xzv[:, :, B3 - 3:B3], reads=[k.MB1], writes=[k.MB1], tag="xzlead")
    emit_ln(k, f"ln0s{s}p{int(pass2)}",
            lambda c, a, w: k.h[c][:, base + a: base + a + w], lambda c: k.h[c],
            [(lambda c, a, w: k.mlnb[c][:, 3 + a:3 + a + w], lambda c: k.mlnb[c])], 0, 1e-6, width=SEC)

    rw0, rww = (0, B3) if s == 0 else (3, SEC)
    fts = list(range(16)) + (list(range(16, 32)) if pass2 else [])
    for fgi in range(0, len(fts), 2):
        wb = k.wst[(fgi // 2) % 2]
        f0 = fts[fgi]
        src = k.i_win.ap[:, 128 * f0:128 * f0 + 256].rearrange("(cc p) f -> p cc f", p=128)
        p.dma("pool", lambda e, wb=wb, src=src: e.dma_start(
            out=wb.ap.rearrange("p (cc f) -> p cc f", cc=8), in_=src), writes=[wb], tag=f"ldwin{f0}")
        for fo in range(2):
            ft = f0 + fo
            pst = (k.psA, k.psB)[(ft // 2) % 2]
            reg = pst[:, (ft % 2) * 512:(ft % 2) * 512 + rww]
            for c in range(8):
                p.mm(reg, wb[:, c * 256 + fo * 128: c * 256 + fo * 128 + 128],
                     k.mlnb[c][:, rw0:rw0 + rww], c == 0, c == 7,
                     reads=[wb, k.mlnb[c]], writes=[pst], tag=f"xz{ft}")
            if ft < 16:
                dst = k.MB1.ap[:, ft * B3 + rw0: ft * B3 + rw0 + rww]
                p.copy("dve", dst, reg, reads=[pst], writes=[k.MB1], tag=f"xzev{ft}")
            else:
                zdst = k.MB2.ap[:, (ft - 16) * SEC:(ft - 16) * SEC + SEC]
                srcp = pst[:, (ft % 2) * 512 + (3 if s == 0 else 0):(ft % 2) * 512 + (3 if s == 0 else 0) + SEC]
                p.act(zdst, srcp, AF.Silu, reads=[pst], writes=[k.MB2], tag=f"silz{ft}")

    for c in range(16):
        st = "dve"
        a4b = k.ma4.ap[:, c * SEC:(c + 1) * SEC]
        p.ts(st, a4b, k.MB1.ap[:, c * B3 + 0: c * B3 + SEC], k.col(CONVC + 0 * 16 + c), None,
             ALU.mult, ALU.bypass, reads=[k.MB1, k.consts], writes=[k.ma4], tag=f"cv0_{c}")
        for i in range(1, 4):
            p.stt(st, a4b, k.MB1.ap[:, c * B3 + i: c * B3 + i + SEC], k.col(CONVC + i * 16 + c), a4b,
                  ALU.mult, ALU.add, reads=[k.MB1, k.consts, k.ma4], writes=[k.ma4], tag=f"cv{i}_{c}")

    p.act(k.mxc.ap, k.ma4.ap, AF.Silu, reads=[k.ma4], writes=[k.mxc], tag="silxc")
    if pass2:
        p.copy("dve", k.mwbf.ap, k.mxc.ap, reads=[k.mxc], writes=[k.mwbf], tag="xcb")
        xcb = k.mwbf
    else:
        p.copy("dve", k.mwbf.ap, k.mxc.ap, reads=[k.mxc], writes=[k.mwbf], tag="xcb")
        xcb = k.mwbf
    for half, dstb, pst in ((0, k.mBb, k.psC), (1, k.mCb, k.psD)):
        if half == 1 and not pass2:
            continue
        wbb = k.wst[half]
        src_ = k.i_wbc.ap[:, 64 * half:64 * half + 64].rearrange("(cc p) q -> p cc q", p=128)
        p.dma("pool", lambda e, wbb=wbb, src_=src_: e.dma_start(
            out=wbb.ap[:, 0:1024].rearrange("p (cc q) -> p cc q", cc=16), in_=src_),
            writes=[wbb], tag=f"ldbc{half}")
        for c in range(16):
            p.mm(pst[0:64, 0:SEC], wbb[:, c * 64:(c + 1) * 64],
                 xcb.ap[:, c * SEC:(c + 1) * SEC], c == 0, c == 15,
                 reads=[wbb, xcb], writes=[pst], tag=f"bc{half}_{c}")
        p.copy("dve", dstb.ap, pst[0:64, 0:SEC], reads=[pst], writes=[dstb], tag=f"bcev{half}")

    a1v = blkv(k.ma1, 16)
    a4v = blkv(k.ma4, 16)
    xcv = blkv(k.mxc, 16)
    p.tt("dve", a1v, xcv, k.cbc(WDTC, 16, SEC), ALU.mult, reads=[k.mxc, k.consts], writes=[k.ma1], tag="dt1")
    p.tt("dve", a1v, a1v, k.cbc(DTBC, 16, SEC), ALU.add, reads=[k.ma1, k.consts], writes=[k.ma1], tag="dt2")
    p.act(k.ma4.ap, k.ma1.ap, AF.Abs, reads=[k.ma1], writes=[k.ma4], tag="sp_abs")
    p.act(k.ma4.ap, k.ma4.ap, AF.Exp, scale=-1.0, reads=[k.ma4], writes=[k.ma4], tag="sp_exp")
    p.act(k.ma4.ap, k.ma4.ap, AF.Ln, bias=k.onesf[:, 0:1], reads=[k.ma4, k.onesf], writes=[k.ma4], tag="sp_ln")
    p.stt("dve", k.ma1.ap, k.ma1.ap, 0.0, k.ma4.ap, ALU.max, ALU.add, reads=[k.ma1, k.ma4], writes=[k.ma1], tag="sp_fin")
    p.tt("dve", a4v, a1v, k.cbc(ALOGC, 16, SEC), ALU.mult, reads=[k.ma1, k.consts], writes=[k.ma4], tag="dtA")
    p.tt("dve", a1v, a1v, xcv, ALU.mult, reads=[k.ma1, k.mxc], writes=[k.ma1], tag="vv")

    for c in range(16):
        p.op("dve", lambda e, c=c: e.tensor_tensor_scan(
            k.MB1.ap[:, c * B3:c * B3 + SEC], k.onesf[:, 0:SEC], k.ma4.ap[:, c * SEC:(c + 1) * SEC],
            k.mqcol[:, c:c + 1], ALU.mult, ALU.add),
            reads=[k.onesf, k.ma4, k.mqcol], writes=[k.MB1], tag=f"qscan{c}")

    qv = blkv(k.MB1, 16)
    ecv = blkv(k.mec, 16)
    pcv = blkv(k.mpc, 16)
    mcv = blkv(k.mmc, 16)
    p.copy("dve", ecv, qv[:, :, CK - 1:SEC:CK], reads=[k.MB1], writes=[k.mec], tag="bnd_e")
    p.copy("dve", pcv[:, :, 0:1], k.mqcol.ap.unsqueeze(2), reads=[k.mqcol], writes=[k.mpc], tag="bnd_p0")
    if NCHS > 1:
        p.copy("dve", pcv[:, :, 1:NCHS], ecv[:, :, 0:NCHS - 1], reads=[k.mec], writes=[k.mpc], tag="bnd_p1")
    p.tt("dve", mcv, ecv, pcv, ALU.add, reads=[k.mec, k.mpc], writes=[k.mmc], tag="bnd_m1")
    p.ts("dve", mcv, mcv, 0.5, None, ALU.mult, ALU.bypass, reads=[k.mmc], writes=[k.mmc], tag="bnd_m2")
    p.tt("dve", blkv(k.mdf, 16), mcv, pcv, ALU.subtract, reads=[k.mmc, k.mpc], writes=[k.mdf], tag="bnd_d1")
    p.act(k.mdh.ap, k.mdf.ap, AF.Exp, reads=[k.mdf], writes=[k.mdh], tag="bnd_dh")
    p.tt("dve", k.mdf.ap, k.mdh.ap, k.mdh.ap, ALU.mult, reads=[k.mdh], writes=[k.mdf], tag="bnd_df")
    if not pass2:
        for j in range(NCHS):
            p.tt("dve", k.mdtot.ap, k.mdtot.ap, blkv(k.mdf, 16)[:, :, j:j + 1].squeeze(2),
                 ALU.mult, reads=[k.mdtot, k.mdf], writes=[k.mdtot], tag=f"dtot{j}")
    p.copy("dve", k.mqcol.ap, qv[:, :, SEC - 1:SEC].squeeze(2), reads=[k.MB1], writes=[k.mqcol], tag="qcol")

    qv4 = blkv(k.MB1, 16)[:, :, 0:SEC].rearrange("p b (j t) -> p b j t", j=NCHS)
    mc4 = blkv(k.mmc, 16).unsqueeze(3).broadcast_to([128, 16, NCHS, CK])
    a44 = k.ma4.ap.rearrange("p (b j t) -> p b j t", b=16, j=NCHS)
    p.tt("dve", a44, mc4, qv4, ALU.subtract, reads=[k.mmc, k.MB1], writes=[k.ma4], tag="exparg")
    if pass2:
        p.act(k.meqb.ap, k.ma4.ap, AF.Exp, scale=-1.0, reads=[k.ma4], writes=[k.meqb], tag="eqb")
    p.act(k.ma4.ap, k.ma4.ap, AF.Exp, reads=[k.ma4], writes=[k.ma4], tag="wexp")
    p.tt("dve", k.mwbf.ap, k.ma4.ap, k.ma1.ap, ALU.mult, reads=[k.ma4, k.ma1], writes=[k.mwbf], tag="wbf")

    for j in range(NCHS):
        p.tr(k.psT[0:64, 1920:1984], k.mBb[:, j * CK:(j + 1) * CK], k.identb[0:64, 0:64],
             reads=[k.mBb, k.identb], writes=[k.psT], tag=f"btr{j}")
        p.copy("dve", k.mbtm.ap, k.psT[0:64, 1920:1984], reads=[k.psT], writes=[k.mbtm], tag=f"btm{j}")
        for c in range(16):
            p.tr(k.psT[0:64, c * 128:(c + 1) * 128],
                 k.mwbf.ap[:, c * SEC + j * CK: c * SEC + j * CK + CK],
                 k.identb.ap, reads=[k.mwbf, k.identb], writes=[k.psT], tag=f"wtr{j}_{c}")
        p.copy("dve", k.mwtm.ap, k.psT[0:64, 0:2048], reads=[k.psT], writes=[k.mwtm], tag=f"wtmev{j}")
        if pass2:
            p.mm(k.psD[0:64, 0:CK], k.mBb[:, j * CK:(j + 1) * CK], k.mCb[:, j * CK:(j + 1) * CK],
                 True, True, reads=[k.mBb, k.mCb], writes=[k.psD], tag=f"g{j}")
            p.tt("dve", k.mgm.ap, k.psD[0:64, 0:CK], k.trib.ap, ALU.mult,
                 reads=[k.psD, k.trib], writes=[k.mgm], tag=f"gm{j}")
            p.tt("dve", blkv(k.mhbf, 16), blkv(k.mhst, 16),
                 blkv(k.mdh, 16)[:, :, j:j + 1].broadcast_to([128, 16, 64]),
                 ALU.mult, reads=[k.mhst, k.mdh], writes=[k.mhbf], tag=f"hbf{j}")
            for c in range(16):
                p.tr(k.psT[0:64, c * 128:(c + 1) * 128], k.mhbf[:, c * 64:(c + 1) * 64],
                     k.identb.ap, reads=[k.mhbf, k.identb], writes=[k.psT], tag=f"htr{j}_{c}")
            p.copy("dve", k.mhtm.ap, k.psT[0:64, 0:2048], reads=[k.psT], writes=[k.mhtm], tag=f"htmev{j}")
            for c in range(16):
                p.mm(k.psA[:, c * 64:(c + 1) * 64], k.mwtm[0:64, c * 128:(c + 1) * 128], k.mgm.ap,
                     c in (0, 8), False, reads=[k.mwtm, k.mgm], writes=[k.psA], tag=f"s1_{j}_{c}")
            for c in range(16):
                p.mm(k.psA[:, c * 64:(c + 1) * 64], k.mhtm[0:64, c * 128:(c + 1) * 128],
                     k.mCb[:, j * CK:(j + 1) * CK], False, True,
                     reads=[k.mhtm, k.mCb], writes=[k.psA], tag=f"s2_{j}_{c}")
            p.tt("dve", k.MB3.ap.rearrange("p (b j t) -> p b j t", b=16, j=NCHS)[:, :, j, :],
                 k.psA.ap.rearrange("p (b t) -> p b t", b=16),
                 k.meqb.ap.rearrange("p (b j t) -> p b j t", b=16, j=NCHS)[:, :, j, :],
                 ALU.mult, reads=[k.psA, k.meqb], writes=[k.MB3], tag=f"ylo{j}")
        for c in range(16):
            p.mm(k.psB[:, c * 64:(c + 1) * 64], k.mwtm[0:64, c * 128:(c + 1) * 128], k.mbtm.ap,
                 c in (0, 8), True, reads=[k.mwtm, k.mbtm], writes=[k.psB], tag=f"upd{j}_{c}")
        p.tt("dve", k.psB.ap.rearrange("p (b t) -> p b t", b=16), k.psB.ap.rearrange("p (b t) -> p b t", b=16),
             blkv(k.mdh, 16)[:, :, j:j + 1].broadcast_to([128, 16, 64]), ALU.mult,
             reads=[k.psB, k.mdh], writes=[k.psB], tag=f"hup1_{j}")
        p.tt("dve", blkv(k.mhst, 16), blkv(k.mhst, 16),
             blkv(k.mdf, 16)[:, :, j:j + 1].broadcast_to([128, 16, 64]), ALU.mult,
             reads=[k.mhst, k.mdf], writes=[k.mhst], tag=f"hup2_{j}")
        p.tt("dve", k.mhst.ap, k.mhst.ap, k.psB.ap, ALU.add,
             reads=[k.mhst, k.psB], writes=[k.mhst], tag=f"hup3_{j}")

    if pass2:
        p.tt("dve", a4v, xcv, k.cbc(DSKC, 16, SEC), ALU.mult, reads=[k.mxc, k.consts], writes=[k.ma4], tag="dsk1")
        p.tt("dve", k.MB3.ap, k.MB3.ap, k.ma4.ap, ALU.add, reads=[k.MB3, k.ma4], writes=[k.MB3], tag="dsk2")
        p.tt("dve", k.MB3.ap, k.MB3.ap, k.MB2.ap, ALU.mult, reads=[k.MB3, k.MB2], writes=[k.MB3], tag="gate")
        for f in range(8):
            wb = k.wst[f % 2]
            src = k.i_wout.ap[:, 128 * f:128 * (f + 1)].rearrange("(cc p) q -> p cc q", p=128)
            p.dma("pool", lambda e, wb=wb, src=src: e.dma_start(
                out=wb.ap.rearrange("p (cc q) -> p cc q", cc=16), in_=src), writes=[wb], tag=f"ldwo{f}")
            pst = (k.psC, k.psD)[f % 2]
            for c in range(16):
                p.mm(pst[:, 0:SEC], wb[:, c * 128:(c + 1) * 128], k.MB3.ap[:, c * SEC:(c + 1) * SEC],
                     c == 0, c == 15, reads=[wb, k.MB3], writes=[pst], tag=f"wo{f}_{c}")
            p.stt("dve", k.h[f][:, base:base + SEC], pst[:, 0:SEC], k.masks[:, 0:1],
                  k.h[f][:, base:base + SEC], ALU.mult, ALU.add,
                  reads=[pst, k.masks, k.h[f]], writes=[k.h[f]], tag=f"hout{f}")


def emit_mamba_exchange(k):
    p = k.p
    gi, go = k.g_mst_in, k.g_mst_out
    p.dma("sp", lambda e: e.dma_start(
        out=gi.ap.rearrange("(cc p) n -> p cc n", p=128),
        in_=k.mhst.ap.rearrange("p (cc n) -> p cc n", cc=16)),
        reads=[k.mhst], writes=[gi], tag="mst_st1")
    p.dma("sp", lambda e: e.dma_start(out=k.g_mp_in.ap, in_=k.mdtot.ap),
          reads=[k.mdtot], writes=[k.g_mp_in], tag="mst_st2")
    p.coll(lambda e: e.collective_compute("AllGather", ALU.bypass, replica_groups=[list(range(NCORES))],
                                          ins=[gi.ap.opt()], outs=[go.ap.opt()]),
           reads=[gi], writes=[go], tag="AG_mst")
    p.coll(lambda e: e.collective_compute("AllGather", ALU.bypass, replica_groups=[list(range(NCORES))],
                                          ins=[k.g_mp_in.ap.opt()], outs=[k.g_mp_out.ap.opt()]),
           reads=[k.g_mp_in], writes=[k.g_mp_out], tag="AG_mp")
    macc = k.sub("macc", 2, 0, 16 * 64, F32)
    p.op("dve", lambda e: e.memset(macc.ap, 0.0), writes=[macc])
    p.op("dve", lambda e: e.memset(k.mhst.ap, 0.0), writes=[k.mhst])
    mrb = [k.sub(f"mrb{i}", i, 0, 16 * 64, F32) for i in range(2)]
    mrp = [k.sub(f"mrp{i}", i, 16 * 64, 16, F32) for i in range(2)]
    for j in range(NCORES):
        rb, rp = mrb[j % 2], mrp[j % 2]
        src = go.ap[j * DI:(j + 1) * DI, :].rearrange("(cc p) n -> p cc n", p=128)
        p.dma("sp", lambda e, rb=rb, src=src: e.dma_start(
            out=rb.ap.rearrange("p (cc n) -> p cc n", cc=16), in_=src), reads=[go], writes=[rb], tag=f"mrb{j}")
        srcp = k.g_mp_out.ap[j * 128:(j + 1) * 128, :]
        p.dma("sp", lambda e, rp=rp, srcp=srcp: e.dma_start(out=rp.ap, in_=srcp),
              reads=[k.g_mp_out], writes=[rp], tag=f"mrp{j}")
        p.stt("dve", k.mhst.ap, macc.ap, k.col(SELR + j), k.mhst.ap, ALU.mult, ALU.add,
              reads=[macc, k.consts, k.mhst], writes=[k.mhst], tag=f"pick{j}")
        if j < NCORES - 1:
            p.tt("dve", blkv(macc, 16), blkv(macc, 16),
                 rp.ap.unsqueeze(2).broadcast_to([128, 16, 64]), ALU.mult,
                 reads=[macc, rp], writes=[macc], tag=f"accP{j}")
            p.tt("dve", blkv(macc, 16), blkv(macc, 16), blkv(rb, 16), ALU.add,
                 reads=[macc, rb], writes=[macc], tag=f"accH{j}")


def emit_mamba(k):
    p = k.p
    alloc_mamba(k)
    p.op("dve", lambda e: e.memset(k.mhst.ap, 0.0), writes=[k.mhst])
    p.op("dve", lambda e: e.memset(k.mqcol.ap, 0.0), writes=[k.mqcol])
    p.op("dve", lambda e: e.memset(k.mdtot.ap, 1.0), writes=[k.mdtot])
    for s in range(NSEC):
        emit_mamba_section(k, s, pass2=False)
    emit_mamba_exchange(k)
    p.op("dve", lambda e: e.memset(k.mqcol.ap, 0.0), writes=[k.mqcol])
    for s in range(NSEC):
        emit_mamba_section(k, s, pass2=True)


def emit_tc(k):
    p = k.p
    x1b = [k.sub(f"x1b{c}", 4 + c // 4, (c % 4) * 2048, 2048, BF16) for c in range(8)]
    x1f = [k.sub(f"x1f{c}", 2, c * 512, 512, F32) for c in range(8)]
    tcw = k.sub("tcw", 3, 0, 8192, BF16)
    p.dma("pool", lambda e: e.dma_start(
        out=tcw.ap.rearrange("p (cc f) -> p cc f", cc=8),
        in_=k.i_tcw.ap.rearrange("(cc p) f -> p cc f", p=128)), writes=[tcw], tag="ldtcw")
    tcc = k.sbuf("tcc", [128, 4], F32)
    thc = k.col(THC)
    p.act(tcc[:, 0:1], thc, AF.Sin, scale=0.5, reads=[k.consts], writes=[tcc], tag="tcsin")
    p.tt("dve", tcc[:, 1:2], tcc[:, 0:1], tcc[:, 0:1], ALU.mult, reads=[tcc], writes=[tcc])
    p.ts("dve", tcc[:, 2:3], tcc[:, 1:2], -1.0, 1.0, ALU.mult, ALU.add, reads=[tcc], writes=[tcc])
    p.act(tcc[:, 2:3], tcc[:, 2:3], AF.Ln, reads=[tcc], writes=[tcc])
    p.act(tcc[:, 2:3], tcc[:, 2:3], AF.Exp, scale=0.5, reads=[tcc], writes=[tcc])
    p.tt("dve", tcc[:, 3:4], tcc[:, 0:1], tcc[:, 2:3], ALU.mult, reads=[tcc], writes=[tcc])
    p.ts("dve", tcc[:, 3:4], tcc[:, 3:4], 2.0, None, ALU.mult, ALU.bypass, reads=[tcc], writes=[tcc])
    p.ts("dve", tcc[:, 1:2], tcc[:, 1:2], -2.0, 1.0, ALU.mult, ALU.add, reads=[tcc], writes=[tcc])
    p.tt("dve", tcc[:, 1:2], tcc[:, 1:2], k.masks[:, 1:2], ALU.mult, reads=[tcc, k.masks], writes=[tcc])
    p.tt("dve", tcc[:, 3:4], tcc[:, 3:4], k.masks[:, 1:2], ALU.mult, reads=[tcc, k.masks], writes=[tcc])
    for ci in range(4):
        a = ci * 512
        emit_ln(k, f"ln1c{ci}",
                lambda c, aa, w, a=a: k.h[c][:, a + aa: a + aa + w], lambda c: k.h[c],
                [(lambda c, aa, w: x1f[c].ap[:, aa:aa + w], lambda c: x1f[c]),
                 (lambda c, aa, w, a=a: x1b[c].ap[:, a + aa:a + aa + w], lambda c: x1b[c])],
                1, 1e-6, width=512)
        if ci == 0 and "tc" in DBG:
            d2 = p.dram_out("dbg_x1f0", [128, 512], F32)
            d3 = p.dram_out("dbg_tcc", [128, 4], F32)
            d4 = p.dram_out("dbg_masks", [128, 4], F32)
            p.dma("sp", lambda e: e.dma_start(out=d2.ap, in_=x1f[0].ap), reads=[x1f[0]], writes=[d2])
            p.dma("sp", lambda e: e.dma_start(out=d3.ap, in_=tcc.ap), reads=[tcc], writes=[d3])
            p.dma("sp", lambda e: e.dma_start(out=d4.ap, in_=k.masks.ap), reads=[k.masks], writes=[d4])
        for f in range(8):
            pst = (k.psC, k.psD)[f % 2]
            for c in range(8):
                p.mm(pst[:, 0:512], tcw[:, c * 1024 + f * 128: c * 1024 + f * 128 + 128],
                     x1b[c].ap[:, a:a + 512], c == 0, c == 7,
                     reads=[tcw, x1b[c]], writes=[pst], tag=f"tcmm{f}")
            p.act(k.ln_t1.ap, pst[:, 0:512], AF.Tanh, bias=k.col(TCBC + f),
                  reads=[pst, k.consts], writes=[k.ln_t1], tag=f"tctanh{f}")
            p.stt("dve", k.h[f][:, a:a + 512], x1f[f].ap, tcc[:, 1:2], k.h[f][:, a:a + 512],
                  ALU.mult, ALU.add, reads=[x1f[f], tcc, k.h[f]], writes=[k.h[f]], tag=f"tchx{f}")
            p.stt("dve", k.h[f][:, a:a + 512], k.ln_t1.ap, tcc[:, 3:4], k.h[f][:, a:a + 512],
                  ALU.mult, ALU.add, reads=[k.ln_t1, tcc, k.h[f]], writes=[k.h[f]], tag=f"tcht{f}")


def build_kmax(k, tiles, dstbuf, nm, dcol):
    p = k.p
    p.op("dve", lambda e: e.memset(k.ln_t1.ap, 0.0), writes=[k.ln_t1])
    for cc in range(8):
        for tch in range(2):
            p.act(k.ln_xb.ap, tiles[cc].ap[:, tch * 512:tch * 512 + 512], AF.Square,
                  reads=[tiles[cc]], writes=[k.ln_xb], tag=f"{nm}sq{cc}")
            for h2 in range(2):
                pst = k.psC
                p.mm(pst[h2 * 64:h2 * 64 + 1, 0:512], k.onesb[h2 * 64:h2 * 64 + 64, 0:1],
                     k.ln_xb[h2 * 64:h2 * 64 + 64, 0:512], True, True,
                     reads=[k.onesb, k.ln_xb], writes=[pst], tag=f"{nm}mm{cc}{h2}")
                col = 16 + 2 * cc + h2
                p.op("dve", lambda e, pst=pst, h2=h2, col=col: e.tensor_reduce(
                    k.ln_t1[h2 * 64:h2 * 64 + 1, col:col + 1], pst[h2 * 64:h2 * 64 + 1, 0:512], AX.X, ALU.max),
                    reads=[pst], writes=[k.ln_t1], tag=f"{nm}rd{cc}{h2}")
                p.tt("dve", k.ln_t1[h2 * 64:h2 * 64 + 1, 2 * cc + h2:2 * cc + h2 + 1],
                     k.ln_t1[h2 * 64:h2 * 64 + 1, 2 * cc + h2:2 * cc + h2 + 1],
                     k.ln_t1[h2 * 64:h2 * 64 + 1, col:col + 1], ALU.max,
                     reads=[k.ln_t1], writes=[k.ln_t1], tag=f"{nm}mx{cc}{h2}")
    pst = k.psD
    p.tr(pst[0:16, 0:128], k.ln_t1[:, 0:16], k.ident.ap, reads=[k.ln_t1, k.ident], writes=[pst], tag=f"{nm}tr")
    fv = k.ln_xb.ap.bitcast(F32)
    p.copy("dve", fv[0:16, 0:2], pst[0:16, 0:128:64], reads=[pst], writes=[k.ln_xb], tag=f"{nm}cp")
    p.op("dve", lambda e: e.tensor_reduce(fv[0:16, 2:3], fv[0:16, 0:2], AX.X, ALU.max),
         reads=[k.ln_xb], writes=[k.ln_xb], tag=f"{nm}rd2")
    p.tr(pst[0:1, 0:16], fv[0:16, 2:3], k.ident[0:16, 0:16], reads=[k.ln_xb, k.ident], writes=[pst], tag=f"{nm}tr2")
    p.copy("dve", dstbuf[0:1, dcol:dcol + 16], pst[0:1, 0:16], reads=[pst], writes=[dstbuf], tag=f"{nm}fin")


def emit_wlam(k):
    p = k.p
    INV = float(np.float32(0.7071067811865476))
    abf = [k.sub(f"abf{c}", 4, c * 1024, 1024, BF16) for c in range(8)]
    qbf = [k.sub(f"qbf{c}", 5, c * 1024, 1024, BF16) for c in range(8)]
    ds = [k.sub(f"wds{c}", c // 4, (c % 4) * 1024, 1024, F32) for c in range(8)]
    x2f = [k.sub(f"x2f{c}", 2, c * 512, 512, F32) for c in range(8)]
    kfm = [k.sub(f"kfm{c}", 3, c * 1024, 1024, BF16) for c in range(8)]
    for ci in range(4):
        a = ci * 512
        pa = ci * 256
        emit_ln(k, f"ln2c{ci}",
                lambda c, aa, w, a=a: k.h[c][:, a + aa: a + aa + w], lambda c: k.h[c],
                [(lambda c, aa, w: x2f[c].ap[:, aa:aa + w], lambda c: x2f[c])],
                2, 1e-6, width=512)
        for c in range(8):
            ev = x2f[c].ap[:, 0:512:2]
            od = x2f[c].ap[:, 1:512:2]
            p.tt("dve", k.ln_t1[:, 0:256], ev, od, ALU.add, reads=[x2f[c]], writes=[k.ln_t1], tag=f"pa{c}")
            p.ts("dve", abf[c].ap[:, pa:pa + 256], k.ln_t1[:, 0:256], INV, None, ALU.mult, ALU.bypass,
                 reads=[k.ln_t1], writes=[abf[c]], tag=f"pab{c}")
            p.tt("dve", k.ln_t1[:, 0:256], ev, od, ALU.subtract, reads=[x2f[c]], writes=[k.ln_t1], tag=f"pd{c}")
            p.ts("dve", ds[c].ap[:, pa:pa + 256], k.ln_t1[:, 0:256], 0.5, None, ALU.mult, ALU.bypass,
                 reads=[k.ln_t1], writes=[ds[c]], tag=f"pdb{c}")

    def prj(widram, dsts, tagp):
        for f in range(8):
            wb = k.wst[f % 2]
            src = widram.ap[:, 128 * f:128 * (f + 1)].rearrange("(cc p) q -> p cc q", p=128)
            p.dma("pool", lambda e, wb=wb, src=src: e.dma_start(
                out=wb.ap[:, 0:1024].rearrange("p (cc q) -> p cc q", cc=8), in_=src),
                writes=[wb], tag=f"ld{tagp}{f}")
            for tch in range(2):
                pst = (k.psC, k.psD)[tch]
                for c in range(8):
                    p.mm(pst[:, 0:512], wb[:, c * 128:(c + 1) * 128], abf[c].ap[:, tch * 512:tch * 512 + 512],
                         c == 0, c == 7, reads=[wb, abf[c]], writes=[pst], tag=f"{tagp}mm{f}")
                p.copy("dve", dsts[f].ap[:, tch * 512:tch * 512 + 512], pst[:, 0:512],
                       reads=[pst], writes=[dsts[f]], tag=f"{tagp}ev{f}")

    prj(k.i_wq, qbf, "wq")
    prj(k.i_wk, kfm, "wk")
    kmax = None
    build_kmax(k, kfm, k.rows, "kn", 272)
    for c in range(8):
        p.dma("sp", lambda e, c=c: e.dma_start(
            out=k.g_kv_in.ap[c * 128:(c + 1) * 128, 0:1024], in_=kfm[c].ap),
            reads=[kfm[c]], writes=[k.g_kv_in], tag=f"stk{c}")
    p.dma("pool", lambda e: e.dma_start(out=k.g_kv_in.ap[2 * D:2 * D + 1, 0:16], in_=k.rows[0:1, 272:288]),
          reads=[k.rows], writes=[k.g_kv_in], tag="stkn")
    prj(k.i_wv, kfm, "wv")
    vtm = [k.sub(f"vtm{c}", 4, c * 1040, 1040, BF16) for c in range(8)]
    for i in range(8):
        for c in range(8):
            p.tr(k.psT[:, c * 128:(c + 1) * 128], kfm[c].ap[:, i * 128:(i + 1) * 128], k.identb.ap,
                 reads=[kfm[c], k.identb], writes=[k.psT], tag=f"vtr{i}_{c}")
        p.op("dve", lambda e, i=i: e.memset(vtm[i].ap.rearrange("p (hh u) -> p hh u", u=65)[:, :, 64:65], 1.0),
             writes=[vtm[i]], tag=f"vone{i}")
        p.copy("dve", vtm[i].ap.rearrange("p (hh u) -> p hh u", u=65)[:, :, 0:64],
               k.psT[:, 0:1024].rearrange("p (hh d) -> p hh d", d=64),
               reads=[k.psT], writes=[vtm[i]], tag=f"vte{i}")
        p.dma("sp", lambda e, i=i: e.dma_start(
            out=k.g_kv_in.ap[D + i * 128: D + (i + 1) * 128, :], in_=vtm[i].ap),
            reads=[vtm[i]], writes=[k.g_kv_in], tag=f"stv{i}")
    gi, go = k.g_kv_in, k.g_kv_out
    p.coll(lambda e: e.collective_compute("AllGather", ALU.bypass, replica_groups=[list(range(NCORES))],
                                          ins=[gi.ap.opt()], outs=[go.ap.opt()]),
           reads=[gi], writes=[go], tag="AG_kv")
    build_kmax(k, qbf, k.rows, "qn", 288)
    qmax = None
    kng = None
    KST = 2 * D + 16
    for r in range(NCORES):
        p.dma("pool", lambda e, r=r: e.dma_start(
            out=k.rows[0:1, r * 16:(r + 1) * 16], in_=go.ap[r * KST + 2 * D: r * KST + 2 * D + 1, 0:16]),
            reads=[go], writes=[k.rows], tag=f"ldkn{r}")
    kngm = None
    kngm_ap = k.rows[0:1, 256:272]
    p.op("dve", lambda e: e.tensor_reduce(
        kngm_ap, k.rows[0:1, 0:128].rearrange("one (r hh) -> one hh r", r=NCORES), AX.X, ALU.max),
        reads=[k.rows], writes=[k.rows], tag="knmax")
    p.tt("dve", kngm_ap, kngm_ap, k.rows[0:1, 288:304], ALU.mult, reads=[k.rows], writes=[k.rows])
    p.act(kngm_ap, kngm_ap, AF.Ln, reads=[k.rows], writes=[k.rows])
    p.act(kngm_ap, kngm_ap, AF.Exp, scale=0.5, reads=[k.rows], writes=[k.rows])
    p.ts("dve", kngm_ap, kngm_ap, 1.01 / 8.0, -40.0, ALU.mult, ALU.add, reads=[k.rows], writes=[k.rows])
    p.ts("dve", kngm_ap, kngm_ap, 0.0, -1.0, ALU.max, ALU.mult, reads=[k.rows], writes=[k.rows])
    biasr = k.sbuf("biasr", [128, 16], F32)
    p.mm(k.psD[:, 0:16], k.onesf[0:1, 0:128], kngm_ap, True, True, reads=[k.onesf, k.rows], writes=[k.psD], tag="biasbc")
    p.copy("dve", biasr.ap, k.psD[:, 0:16], reads=[k.psD], writes=[biasr])
    ksb = k.sub("ksb", 3, 0, 8192, BF16)
    pt = [k.sub(f"pt{i}", 2, i * 2048, 2048, BF16) for i in range(2)]
    vt = [k.sub(f"vt{i}", 2, 4096 + i * 1040, 1040, BF16) for i in range(2)]
    ofm = vtm
    for cc in range(8):
        for r in range(NCORES):
            p.dma("sp", lambda e, cc=cc, r=r: e.dma_start(
                out=ksb.ap[:, r * 1024:(r + 1) * 1024],
                in_=go.ap[r * KST + cc * 128: r * KST + (cc + 1) * 128, 0:1024]),
                reads=[go], writes=[ksb], tag=f"ldK{cc}_{r}")
        for tt2 in range(64):
            vti = vt[tt2 % 2]
            r, loc = tt2 // 8, tt2 % 8
            p.dma("sp", lambda e, vti=vti, r=r, loc=loc: e.dma_start(
                out=vti.ap, in_=go.ap[r * KST + D + loc * 128: r * KST + D + (loc + 1) * 128, :]),
                reads=[go], writes=[vti], tag=f"ldV{cc}_{tt2}")
            for h2 in range(2):
                h = 2 * cc + h2
                pss = k.psB
                ptb = pt[h2]
                for tch in range(2):
                    p.mm(pss[:, tch * 512:tch * 512 + 512],
                         ksb.ap[64 * h2:64 * h2 + 64, tt2 * 128:(tt2 + 1) * 128],
                         qbf[cc].ap[64 * h2:64 * h2 + 64, tch * 512:tch * 512 + 512], True, True,
                         reads=[ksb, qbf[cc]], writes=[pss], tag=f"st{h}_{tt2}_{tch}")
                p.act(ptb.ap[:, 0:1024], pss.ap, AF.Exp, scale=0.125, bias=biasr[:, h:h + 1],
                      reads=[pss, biasr], writes=[ptb], tag=f"exp{h}_{tt2}")
                for tch in range(2):
                    acc = (k.psC, k.psD)[tch] if h2 == 0 else k.psA
                    reg = acc[0:65, 0:512] if h2 == 0 else acc[0:65, tch * 512:tch * 512 + 512]
                    p.mm(reg, vti.ap[:, h * 65:(h + 1) * 65],
                         ptb.ap[:, tch * 512:tch * 512 + 512], tt2 == 0, tt2 == 63,
                         reads=[vti, ptb], writes=[acc], tag=f"av{h}_{tt2}_{tch}")
        for h2 in range(2):
            h = 2 * cc + h2
            for tch in range(2):
                accp = (k.psC, k.psD)[tch] if h2 == 0 else k.psA
                reg = accp[0:65, 0:512] if h2 == 0 else accp[0:65, tch * 512:tch * 512 + 512]
                rrow = accp[64:65, 0:512] if h2 == 0 else accp[64:65, tch * 512:tch * 512 + 512]
                p.op("dve", lambda e, rrow=rrow: e.reciprocal(k.ln_t1[64:65, 0:512], rrow),
                     reads=[accp], writes=[k.ln_t1], tag=f"rd{h}{tch}")
                p.mm(k.psB[0:64, 0:512], k.onesf[64:65, 0:64], k.ln_t1[64:65, 0:512], True, True,
                     reads=[k.onesf, k.ln_t1], writes=[k.psB], tag=f"rdb{h}{tch}")
                p.copy("dve", k.ln_t1[0:64, 0:512], k.psB[0:64, 0:512], reads=[k.psB], writes=[k.ln_t1],
                       tag=f"rde{h}{tch}")
                rr = accp[0:64, 0:512] if h2 == 0 else accp[0:64, tch * 512:tch * 512 + 512]
                p.tt("dve", ofm[cc].ap[h2 * 64:h2 * 64 + 64, tch * 512:tch * 512 + 512],
                     rr, k.ln_t1[0:64, 0:512], ALU.mult,
                     reads=[accp, k.ln_t1], writes=[ofm[cc]], tag=f"onorm{h}{tch}")
    ywloc = [(2, 0), (2, 2048), (3, 0), (3, 2048), (0, 0), (0, 2048), (1, 0), (1, 2048)]
    yw = [k.sub(f"yw{c}", ywloc[c][0], ywloc[c][1], 2048, F32) for c in range(8)]
    ds7v = SubBuf("ds7v", k.wst[1], 0, 1024, F32)
    p.copy("dve", ds7v.ap, ds[7].ap, reads=[ds[7]], writes=[k.wst[1]], tag="ds7cp")
    dsrc = ds[:7] + [ds7v]
    for f in range(8):
        wb = k.wst[0]
        src = k.i_wo.ap[:, 128 * f:128 * (f + 1)].rearrange("(cc p) q -> p cc q", p=128)
        p.dma("pool", lambda e, wb=wb, src=src: e.dma_start(
            out=wb.ap[:, 0:1024].rearrange("p (cc q) -> p cc q", cc=8), in_=src),
            writes=[wb], tag=f"ldwo2{f}")
        for tch in range(2):
            pst = (k.psC, k.psD)[tch]
            for c in range(8):
                p.mm(pst[:, 0:512], wb[:, c * 128:(c + 1) * 128], ofm[c].ap[:, tch * 512:tch * 512 + 512],
                     c == 0, c == 7, reads=[wb, ofm[c]], writes=[pst], tag=f"womm{f}")
            p.ts("dve", k.ln_t1.ap, pst[:, 0:512], INV, None, ALU.mult, ALU.bypass,
                 reads=[pst], writes=[k.ln_t1], tag=f"o2i{f}")
            p.tt("dve", yw[f].ap[:, tch * 1024:tch * 1024 + 1024:2], k.ln_t1.ap,
                 dsrc[f].ap[:, tch * 512:tch * 512 + 512], ALU.add,
                 reads=[k.ln_t1, dsrc[f]], writes=[yw[f]], tag=f"ywe{f}")
            p.tt("dve", yw[f].ap[:, tch * 1024 + 1:tch * 1024 + 1024:2], k.ln_t1.ap,
                 dsrc[f].ap[:, tch * 512:tch * 512 + 512], ALU.subtract,
                 reads=[k.ln_t1, dsrc[f]], writes=[yw[f]], tag=f"ywo{f}")
    for c in range(8):
        p.copy("dve", k.ln_t1[:, 0:1], yw[c].ap[:, 0:1], reads=[yw[c]], writes=[k.ln_t1])
        p.copy("dve", k.ln_t1[:, 1:3], yw[c].ap[:, 2046:2048], reads=[yw[c]], writes=[k.ln_t1])
        p.dma("sp", lambda e, c=c: e.dma_start(out=k.g_yw_in.ap[c * 128:(c + 1) * 128, 0:3],
                                               in_=k.ln_t1[:, 0:3]),
              reads=[k.ln_t1], writes=[k.g_yw_in], tag=f"styw{c}")
    p.coll(lambda e: e.collective_compute("AllGather", ALU.bypass, replica_groups=[list(range(NCORES))],
                                          ins=[k.g_yw_in.ap.opt()], outs=[k.g_yw_out.ap.opt()]),
           reads=[k.g_yw_in], writes=[k.g_yw_out], tag="AG_yw")
    ywh = k.sub("ywh", 4, 0, 256, F32)
    p.dma("sp", lambda e: e.dma_start(
        out=ywh.ap.rearrange("p (r cc f) -> p r cc f", r=8, cc=8),
        in_=k.g_yw_out.ap.rearrange("(r cc p) f -> p r cc f", r=8, p=128)),
        reads=[k.g_yw_out], writes=[ywh], tag="ldywh")
    hal = k.sub("hal", 4, 256, 24, F32)
    p.op("dve", lambda e: e.memset(hal.ap, 0.0), writes=[hal])
    hv = hal.ap.rearrange("p (cc u) -> p cc u", cc=8)
    yv = ywh.ap.rearrange("p (r cc f) -> p r cc f", r=8, cc=8)
    for r in range(NCORES):
        p.stt("dve", hv[:, :, 0:2], yv[:, r, :, 1:3], k.col(SELP + r), hv[:, :, 0:2],
              ALU.mult, ALU.add, reads=[ywh, k.consts, hal], writes=[hal], tag=f"halL{r}")
        p.stt("dve", hv[:, :, 2:3], yv[:, r, :, 0:1], k.col(SELN + r), hv[:, :, 2:3],
              ALU.mult, ALU.add, reads=[ywh, k.consts, hal], writes=[hal], tag=f"halR{r}")
    wmc = k.sbuf("wmc", [128, 32], F32)
    p.tt("dve", wmc.ap, k.consts[:, WCVC:WCVC + 32], k.masks[:, 2:3].broadcast_to([128, 32]),
         ALU.mult, reads=[k.consts, k.masks], writes=[wmc], tag="wmc")
    for c in range(8):
        for i in range(4):
            sh = i - 2
            lo = max(0, -sh)
            hi = min(T, T - sh)
            p.stt("dve", k.h[c][:, lo:hi], yw[c].ap[:, lo + sh:hi + sh], wmc[:, i * 8 + c:i * 8 + c + 1],
                  k.h[c][:, lo:hi], ALU.mult, ALU.add, reads=[yw[c], wmc, k.h[c]], writes=[k.h[c]],
                  tag=f"cvw{i}_{c}")
        p.stt("dve", k.h[c][:, 0:1], hv[:, c, 0:1], wmc[:, 0 * 8 + c:0 * 8 + c + 1], k.h[c][:, 0:1],
              ALU.mult, ALU.add, reads=[hal, wmc, k.h[c]], writes=[k.h[c]], tag=f"fxa{c}")
        p.stt("dve", k.h[c][:, 0:1], hv[:, c, 1:2], wmc[:, 1 * 8 + c:1 * 8 + c + 1], k.h[c][:, 0:1],
              ALU.mult, ALU.add, reads=[hal, wmc, k.h[c]], writes=[k.h[c]], tag=f"fxb{c}")
        p.stt("dve", k.h[c][:, 1:2], hv[:, c, 1:2], wmc[:, 0 * 8 + c:0 * 8 + c + 1], k.h[c][:, 1:2],
              ALU.mult, ALU.add, reads=[hal, wmc, k.h[c]], writes=[k.h[c]], tag=f"fxc{c}")
        p.stt("dve", k.h[c][:, T - 1:T], hv[:, c, 2:3], wmc[:, 3 * 8 + c:3 * 8 + c + 1], k.h[c][:, T - 1:T],
              ALU.mult, ALU.add, reads=[hal, wmc, k.h[c]], writes=[k.h[c]], tag=f"fxd{c}")


def emit_moe(k):
    p = k.p
    x3b = [k.sub(f"x3b{c}", 4 + c // 4, (c % 4) * 2048, 2048, BF16) for c in range(8)]
    x3f = [k.sub(f"x3f{c}", 2, c * 512, 512, F32) for c in range(8)]
    gtm = k.sub("gtm", 3, 1664, 128, F32)
    wgsb = k.sub("wgsb", 3, 1536, 64, F32)
    gsc = k.sub("gsc", 3, 1600, 8, F32)
    p.dma("sp", lambda e: e.dma_start(
        out=wgsb.ap.rearrange("p (cc e) -> p cc e", cc=8),
        in_=k.i_wg.ap.rearrange("(cc p) e -> p cc e", p=128)), writes=[wgsb], tag="ldwg")
    for ci in range(4):
        a = ci * 512
        emit_ln(k, f"ln3c{ci}",
                lambda c, aa, w, a=a: k.h[c][:, a + aa: a + aa + w], lambda c: k.h[c],
                [(lambda c, aa, w: x3f[c].ap[:, aa:aa + w], lambda c: x3f[c]),
                 (lambda c, aa, w, a=a: x3b[c].ap[:, a + aa:a + aa + w], lambda c: x3b[c])],
                3, 1e-6, width=512)
        for c in range(8):
            p.mm(k.psD[0:8, 0:512], wgsb[:, c * 8:(c + 1) * 8], x3f[c].ap, c == 0, c == 7,
                 reads=[wgsb, x3f[c]], writes=[k.psD], tag=f"lgmm{ci}")
        p.copy("dve", k.ln_t1[0:8, 0:512], k.psD[0:8, 0:512], reads=[k.psD], writes=[k.ln_t1], tag=f"lgev{ci}")
        for i in range(4):
            tl = ci * 4 + i
            p.tr(k.psC[0:128, 0:8], k.ln_t1[0:8, i * 128:(i + 1) * 128], k.ident[0:8, 0:8],
                 reads=[k.ln_t1, k.ident], writes=[k.psC], tag=f"lgtr{tl}")
            p.copy("dve", gtm[:, tl * 8:(tl + 1) * 8], k.psC[0:128, 0:8], reads=[k.psC], writes=[gtm],
                   tag=f"lgtm{tl}")
    fv = k.ln_xb.ap.bitcast(F32)
    for tl in range(16):
        g8 = gtm[:, tl * 8:(tl + 1) * 8]
        p.op("dve", lambda e, g8=g8: e.tensor_reduce(gsc[:, 0:1], g8, AX.X, ALU.max),
             reads=[gtm], writes=[gsc], tag=f"g1_{tl}")
        p.op("dve", lambda e, g8=g8: e.tensor_scalar(fv[:, 0:8], g8, gsc[:, 0:1], None,
                                                     ALU.is_equal, ALU.bypass),
             reads=[gtm, gsc], writes=[k.ln_xb], tag=f"geq{tl}")
        p.stt("dve", fv[:, 8:16], fv[:, 0:8], -1e30, g8, ALU.mult, ALU.add,
              reads=[k.ln_xb, gtm], writes=[k.ln_xb], tag=f"glm{tl}")
        p.op("dve", lambda e: e.tensor_reduce(gsc[:, 1:2], fv[:, 8:16], AX.X, ALU.max),
             reads=[k.ln_xb], writes=[gsc], tag=f"g2_{tl}")
        p.op("dve", lambda e, g8=g8: e.tensor_scalar(fv[:, 0:8], g8, gsc[:, 1:2], None,
                                                     ALU.is_ge, ALU.bypass),
             reads=[gtm, gsc], writes=[k.ln_xb], tag=f"gge{tl}")
        p.op("dve", lambda e, g8=g8: e.tensor_scalar(fv[:, 8:16], g8, gsc[:, 0:1], None,
                                                     ALU.subtract, ALU.bypass),
             reads=[gtm, gsc], writes=[k.ln_xb], tag=f"gsh{tl}")
        p.act(fv[:, 8:16], fv[:, 8:16], AF.Exp, reads=[k.ln_xb], writes=[k.ln_xb], tag=f"gexp{tl}")
        p.tt("dve", gsc[:, 2:3], gsc[:, 1:2], gsc[:, 0:1], ALU.subtract, reads=[gsc], writes=[gsc])
        p.act(gsc[:, 2:3], gsc[:, 2:3], AF.Exp, reads=[gsc], writes=[gsc], tag=f"gden{tl}")
        p.ts("dve", gsc[:, 2:3], gsc[:, 2:3], 1.0, None, ALU.add, ALU.bypass, reads=[gsc], writes=[gsc])
        p.op("dve", lambda e: e.reciprocal(gsc[:, 2:3], gsc[:, 2:3]), reads=[gsc], writes=[gsc])
        p.tt("dve", gsc[:, 2:3], gsc[:, 2:3], k.masks[:, 3:4], ALU.mult, reads=[gsc, k.masks], writes=[gsc])
        p.tt("dve", fv[:, 0:8], fv[:, 0:8], fv[:, 8:16], ALU.mult, reads=[k.ln_xb], writes=[k.ln_xb],
             tag=f"gm1_{tl}")
        p.op("dve", lambda e, g8=g8: e.tensor_scalar(g8, fv[:, 0:8], gsc[:, 2:3], None,
                                                     ALU.mult, ALU.bypass),
             reads=[k.ln_xb, gsc], writes=[gtm], tag=f"gfin{tl}")
    g8r = k.sub("g8r", 2, 0, 2048, BF16)
    for tl in range(16):
        p.tr(k.psC[0:8, 0:128], gtm[:, tl * 8:(tl + 1) * 8], k.ident.ap,
             reads=[gtm, k.ident], writes=[k.psC], tag=f"g8tr{tl}")
        p.copy("dve", g8r.ap[0:8, tl * 128:(tl + 1) * 128], k.psC[0:8, 0:128],
               reads=[k.psC], writes=[g8r], tag=f"g8ev{tl}")
    grep = k.sub("grep", 2, 1024, 2048, F32)
    h1g = [k.sub("h1gA", 0, 0, 8192, BF16), k.sub("h1gB", 1, 0, 8192, BF16)]
    gelsets = [[k.sub(f"gel{s}_{i}", 3, (s * 3 + i) * 512, 512, F32) for i in range(3)] for s in range(2)]
    b2sb = k.sub("b2sb", 3, 7168, 1024, BF16)
    k.ohot8 = k.sub("ohot8", 3, 6144, 1024, BF16)
    p.dma("pool", lambda e: e.dma_start(out=k.ohot8.ap[0:8, :], in_=k.i_ohot.ap), writes=[k.ohot8], tag="ldoh")
    p.dma("pool", lambda e: e.dma_start(out=b2sb.ap[0:8, :], in_=k.i_b2r.ap), writes=[b2sb], tag="ldb2")
    for e_ in range(E):
        for ch in range(4):
            pst = (k.psC, k.psD)[ch % 2]
            p.mm(pst[:, 0:512], k.ohot8.ap[0:8, e_ * 128:(e_ + 1) * 128],
                 g8r.ap[0:8, ch * 512:(ch + 1) * 512], True, True,
                 reads=[k.ohot8, g8r], writes=[pst], tag=f"grbc{e_}_{ch}")
            p.copy("dve", grep.ap[:, ch * 512:(ch + 1) * 512], pst[:, 0:512], reads=[pst], writes=[grep],
                   tag=f"grev{e_}_{ch}")
        for fq in range(4):
            for fg in range(4):
                wb = k.wst[fg % 2]
                c0 = fq * 1024 + fg * 256
                src = k.i_w1.ap[e_ * D:(e_ + 1) * D, c0:c0 + 256].rearrange("(cc p) q -> p cc q", p=128)
                p.dma("pool", lambda e, wb=wb, src=src: e.dma_start(
                    out=wb.ap.rearrange("p (cc q) -> p cc q", cc=8), in_=src),
                    writes=[wb], tag=f"ldw1_{e_}_{fq}{fg}")
                for fo in range(2):
                    ftq = fg * 2 + fo
                    b1c = k.col(MB1C + e_ * 32 + fq * 8 + ftq)
                    for tch in range(4):
                        gel = gelsets[(ftq * 4 + tch) % 2]
                        pst = (k.psA, k.psB)[ftq % 2]
                        reg = pst[:, (tch % 2) * 512:(tch % 2) * 512 + 512]
                        for c in range(8):
                            p.mm(reg, wb[:, c * 256 + fo * 128:c * 256 + fo * 128 + 128],
                                 x3b[c].ap[:, tch * 512:(tch + 1) * 512], c == 0, c == 7,
                                 reads=[wb, x3b[c]], writes=[pst], tag=f"o1_{e_}_{fq}{ftq}_{tch}")
                        p.act(gel[0].ap, reg, AF.Identity, bias=b1c,
                              reads=[pst, k.consts], writes=[gel[0]], tag=f"gx{e_}_{fq}{ftq}_{tch}")
                        p.act(gel[1].ap, reg, AF.Square, bias=b1c, reads=[pst, k.consts], writes=[gel[1]],
                              tag=f"ga{e_}_{fq}{ftq}_{tch}")
                        p.ts("dve", gel[1].ap, gel[1].ap, GELU_C, 1.0, ALU.mult, ALU.add,
                             reads=[gel[1]], writes=[gel[1]], tag=f"gb{e_}")
                        p.tt("dve", gel[2].ap, gel[1].ap, gel[0].ap, ALU.mult,
                             reads=[gel[1], gel[0]], writes=[gel[2]], tag=f"gc{e_}")
                        p.act(gel[2].ap, gel[2].ap, AF.Sigmoid, scale=GELU_S,
                              reads=[gel[2]], writes=[gel[2]], tag=f"gs{e_}")
                        p.tt("dve", gel[0].ap, gel[2].ap, gel[0].ap, ALU.mult,
                             reads=[gel[2], gel[0]], writes=[gel[0]], tag=f"gg{e_}")
                        h1gd = h1g[ftq // 4].ap[:, (ftq % 4) * 2048 + tch * 512:(ftq % 4) * 2048 + tch * 512 + 512]
                        p.tt("dve", h1gd, gel[0].ap, grep.ap[:, tch * 512:(tch + 1) * 512], ALU.mult,
                             reads=[gel[0], grep], writes=[h1g[ftq // 4]], tag=f"hg{e_}_{fq}{ftq}_{tch}")
            for f2 in range(8):
                wb = k.wst[f2 % 2]
                src = k.i_w2.ap[e_ * FF + fq * 1024: e_ * FF + (fq + 1) * 1024,
                                128 * f2:128 * (f2 + 1)].rearrange("(cc p) q -> p cc q", p=128)
                p.dma("pool", lambda e, wb=wb, src=src: e.dma_start(
                    out=wb.ap[:, 0:1024].rearrange("p (cc q) -> p cc q", cc=8), in_=src),
                    writes=[wb], tag=f"ldw2_{e_}_{fq}{f2}")
                for tch in range(4):
                    pst = (k.psC, k.psD)[f2 % 2]
                    for cq in range(8):
                        h1s = h1g[cq // 4].ap[:, (cq % 4) * 2048 + tch * 512:(cq % 4) * 2048 + tch * 512 + 512]
                        p.mm(pst[:, 0:512], wb[:, cq * 128:(cq + 1) * 128], h1s,
                             cq == 0, cq == 7 and fq < 3, reads=[wb, h1g[cq // 4]], writes=[pst],
                             tag=f"o2_{e_}_{fq}{f2}_{tch}")
                    if fq == 3:
                        p.mm(pst[:, 0:512], b2sb[0:8, f2 * 128:(f2 + 1) * 128],
                             g8r.ap[0:8, tch * 512:(tch + 1) * 512], False, True,
                             reads=[b2sb, g8r], writes=[pst], tag=f"b2_{e_}_{f2}_{tch}")
                    p.tt("dve", k.h[f2][:, tch * 512:(tch + 1) * 512], k.h[f2][:, tch * 512:(tch + 1) * 512],
                         pst[:, 0:512], ALU.add, reads=[k.h[f2], pst], writes=[k.h[f2]],
                         tag=f"hadd{e_}_{fq}{f2}_{tch}")


def emit_outer(k):
    p = k.p
    xtm = [k.sub(f"otm{i}", i, 0, 1024, F32) for i in range(2)]
    for i in range(16):
        b = xtm[i % 2]
        src = k.i_x[3 + 128 * i: 3 + 128 * (i + 1), :]
        p.dma("sp", lambda e, b=b, src=src: e.dma_start(out=b.ap, in_=src), writes=[b], tag=f"rlx{i}")
        ps = (k.psA, k.psB)[i % 2]
        for c in range(8):
            p.tr(ps[:, 128 * c:128 * (c + 1)], b.ap[:, 128 * c:128 * (c + 1)], k.ident.ap,
                 reads=[b, k.ident], writes=[ps], tag=f"rtr{i}_{c}")
        for c in range(8):
            p.tt("dve", k.h[c][:, 128 * i:128 * (i + 1)], k.h[c][:, 128 * i:128 * (i + 1)],
                 ps[:, 128 * c:128 * (c + 1)], ALU.add, reads=[k.h[c], ps], writes=[k.h[c]],
                 tag=f"radd{i}_{c}")
    for ci in range(4):
        a = ci * 512
        emit_ln(k, f"ln4c{ci}",
                lambda c, aa, w, a=a: k.h[c][:, a + aa: a + aa + w], lambda c: k.h[c],
                [(lambda c, aa, w, a=a: k.h[c][:, a + aa: a + aa + w], lambda c: k.h[c])],
                None, 1e-5, width=512)


def emit_ffn(k):
    p = k.p
    xpl = k.sbuf("xpl", [128, 8], F32)
    for c in range(8):
        p.copy("dve", xpl[:, c:c + 1], k.h[c][:, T - 1:T], reads=[k.h[c]], writes=[xpl], tag=f"xpl{c}")
    p.dma("sp", lambda e: e.dma_start(out=k.g_xp_in.ap, in_=xpl.ap), reads=[xpl], writes=[k.g_xp_in], tag="stxp")
    p.coll(lambda e: e.collective_compute("AllGather", ALU.bypass, replica_groups=[list(range(NCORES))],
                                          ins=[k.g_xp_in.ap.opt()], outs=[k.g_xp_out.ap.opt()]),
           reads=[k.g_xp_in], writes=[k.g_xp_out], tag="AG_xp")
    xpg = k.sbuf("xpg", [128, 64], F32)
    p.dma("sp", lambda e: e.dma_start(
        out=xpg.ap.rearrange("p (r cc) -> p r cc", r=8),
        in_=k.g_xp_out.ap.rearrange("(r p) cc -> p r cc", p=128)),
        reads=[k.g_xp_out], writes=[xpg], tag="ldxp")
    xpb = k.sbuf("xpb", [128, 8], F32)
    p.op("dve", lambda e: e.memset(xpb.ap, 0.0), writes=[xpb])
    for r in range(NCORES):
        p.stt("dve", xpb.ap, xpg.ap.rearrange("p (r cc) -> p r cc", r=8)[:, r, :], k.col(SELP + r),
              xpb.ap, ALU.mult, ALU.add, reads=[xpg, k.consts, xpb], writes=[xpb], tag=f"xpk{r}")
    xrb = [k.sub(f"xrb{c}", 4 + c // 4, (c % 4) * 2048, 2048, BF16) for c in range(8)]
    xkb = [k.sub(f"xkb{c}", c // 4, (c % 4) * 2048, 2048, BF16) for c in range(8)]
    dsc = k.sub("ffd", 2, 0, 2048, F32)
    for c in range(8):
        p.tt("dve", dsc.ap[:, 1:2048], k.h[c][:, 1:2048], k.h[c][:, 0:2047], ALU.subtract,
             reads=[k.h[c]], writes=[dsc], tag=f"fd{c}")
        p.tt("dve", dsc.ap[:, 0:1], k.h[c][:, 0:1], xpb[:, c:c + 1], ALU.subtract,
             reads=[k.h[c], xpb], writes=[dsc], tag=f"fd0{c}")
        p.stt("dve", xrb[c].ap[:, 1:2048], dsc.ap[:, 1:2048], k.col(MURC + c), k.h[c][:, 0:2047],
              ALU.mult, ALU.add, reads=[dsc, k.consts, k.h[c]], writes=[xrb[c]], tag=f"fr{c}")
        p.stt("dve", xkb[c].ap[:, 1:2048], dsc.ap[:, 1:2048], k.col(MUKC + c), k.h[c][:, 0:2047],
              ALU.mult, ALU.add, reads=[dsc, k.consts, k.h[c]], writes=[xkb[c]], tag=f"fk{c}")
        p.stt("dve", xrb[c].ap[:, 0:1], dsc.ap[:, 0:1], k.col(MURC + c), xpb[:, c:c + 1],
              ALU.mult, ALU.add, reads=[dsc, k.consts, xpb], writes=[xrb[c]], tag=f"fr0{c}")
        p.stt("dve", xkb[c].ap[:, 0:1], dsc.ap[:, 0:1], k.col(MUKC + c), xpb[:, c:c + 1],
              ALU.mult, ALU.add, reads=[dsc, k.consts, xpb], writes=[xkb[c]], tag=f"fk0{c}")
    wT = k.sub("ffwT", 3, 0, 8192, BF16)

    def build_wT(widram, nm):
        for di in range(8):
            wb = k.wst[di % 2]
            src = widram.ap[128 * di:128 * (di + 1), :].rearrange("p (cc q) -> p cc q", cc=8)
            p.dma("pool", lambda e, wb=wb, src=src: e.dma_start(
                out=wb.ap[:, 0:1024].rearrange("p (cc q) -> p cc q", cc=8), in_=src),
                writes=[wb], tag=f"ld{nm}{di}")
            for cj in range(8):
                p.tr(k.psT[:, cj * 128:(cj + 1) * 128], wb[:, cj * 128:(cj + 1) * 128], k.identb.ap,
                     reads=[wb, k.identb], writes=[k.psT], tag=f"{nm}tr{di}_{cj}")
            p.copy("dve", wT.ap.rearrange("p (cj di q) -> p cj di q", cj=8, di=8)[:, :, di, :],
                   k.psT[:, 0:1024].rearrange("p (cj q) -> p cj q", cj=8),
                   reads=[k.psT], writes=[wT], tag=f"{nm}ev{di}")

    def ffn_mm(xin, di, tch, pst):
        for cj in range(8):
            p.mm(pst[:, 0:512], wT.ap[:, (cj * 8 + di) * 128:(cj * 8 + di) * 128 + 128],
                 xin[cj].ap[:, tch * 512:(tch + 1) * 512], cj == 0, cj == 7,
                 reads=[wT, xin[cj]], writes=[pst], tag=f"fmm{di}_{tch}")

    rq = [k.sub(f"ffrq{d}", 2, d * 512, 512, BF16) for d in range(8)]
    kq = [k.sub(f"ffkq{d}", 2, 4096 + d * 512, 512, BF16) for d in range(8)]
    for tch in range(4):
        build_wT(k.i_wr, f"wr{tch}")
        for di in range(8):
            pst = (k.psC, k.psD)[di % 2]
            ffn_mm(xrb, di, tch, pst)
            p.act(rq[di].ap, pst[:, 0:512], AF.Sigmoid, reads=[pst], writes=[rq[di]], tag=f"sg{tch}{di}")
        build_wT(k.i_wkf, f"wk{tch}")
        for di in range(8):
            pst = (k.psC, k.psD)[di % 2]
            ffn_mm(xkb, di, tch, pst)
            p.ts("dve", k.ln_t1.ap, pst[:, 0:512], 0.0, None, ALU.max, ALU.bypass,
                 reads=[pst], writes=[k.ln_t1], tag=f"rel{tch}{di}")
            p.act(kq[di].ap, k.ln_t1.ap, AF.Square, reads=[k.ln_t1], writes=[kq[di]], tag=f"ksq{tch}{di}")
        build_wT(k.i_wvf, f"wv{tch}")
        for di in range(8):
            pst = (k.psC, k.psD)[di % 2]
            for cj in range(8):
                p.mm(pst[:, 0:512], wT.ap[:, (cj * 8 + di) * 128:(cj * 8 + di) * 128 + 128],
                     kq[cj].ap, cj == 0, cj == 7, reads=[wT, kq[cj]], writes=[pst], tag=f"vmm{tch}{di}")
            p.tt("dve", k.ln_t1.ap, pst[:, 0:512], rq[di].ap, ALU.mult,
                 reads=[pst, rq[di]], writes=[k.ln_t1], tag=f"fmul{tch}{di}")
            p.tt("dve", k.h[di][:, tch * 512:(tch + 1) * 512], k.h[di][:, tch * 512:(tch + 1) * 512],
                 k.ln_t1.ap, ALU.add, reads=[k.h[di], k.ln_t1], writes=[k.h[di]], tag=f"fadd{tch}{di}")
    for ci in range(4):
        a = ci * 512
        emit_ln(k, f"ln5c{ci}",
                lambda c, aa, w, a=a: k.h[c][:, a + aa: a + aa + w], lambda c: k.h[c],
                [(lambda c, aa, w, a=a: k.h[c][:, a + aa: a + aa + w], lambda c: k.h[c])],
                None, 1e-5, width=512)


def build_program(tap=None, debug=False, upto=None):
    nc = bass.Bass("TRN2", target_bir_lowering=False)
    k = Kern(nc, tap)
    emit_setup(k)
    emit_input(k)
    if tap:
        tap("input", k)
    stages = [("mamba", emit_mamba), ("tc", emit_tc), ("wlam", emit_wlam),
              ("moe", emit_moe), ("outer", emit_outer), ("ffn", emit_ffn)]
    for name, fn in stages:
        if upto is not None and name not in upto:
            continue
        fn(k)
        if tap:
            tap(name, k)
    emit_output_stub(k)
    k.p.finalize(debug=debug)
    return nc, k


def emit_output_stub(k):
    p = k.p
    otm = [k.sub(f"otm{i}", i, 2048, 1024, F32) for i in range(2)]
    for i in range(16):
        ps = (k.psA, k.psB)[i % 2]
        ob = otm[i % 2]
        for c in range(8):
            p.tr(ps[:, 128 * c:128 * (c + 1)], k.h[c][:, 128 * i:128 * (i + 1)], k.ident.ap,
                 reads=[k.h[c], k.ident], writes=[ps], tag=f"otr{i}_{c}")
        p.copy("dve", ob.ap, ps[:, 0:1024], reads=[ps], writes=[ob], tag=f"oev{i}")
        p.dma("sp", lambda e, ob=ob, i=i: e.dma_start(out=k.out.ap[128 * i:128 * (i + 1), :], in_=ob.ap),
              reads=[ob], writes=[k.out], tag=f"ost{i}")


def pack_cols(I):
    cols = np.zeros((128, COLS_W), np.float32)
    for i in range(4):
        for kk in range(8):
            cols[:, LNG + 8 * i + kk] = I["L0_attn_ln_g"][i][128 * kk:128 * (kk + 1)]
            cols[:, LNB + 8 * i + kk] = I["L0_attn_ln_b"][i][128 * kk:128 * (kk + 1)]
    for kk in range(16):
        cols[:, ALOGC + kk] = I["L0_attn_m_alog"][128 * kk:128 * (kk + 1)]
        cols[:, WDTC + kk] = I["L0_attn_m_wdt"][128 * kk:128 * (kk + 1)]
        cols[:, DTBC + kk] = I["L0_attn_m_dtb"][128 * kk:128 * (kk + 1)]
        cols[:, DSKC + kk] = I["L0_attn_m_dskip"][128 * kk:128 * (kk + 1)]
        for i in range(4):
            cols[:, CONVC + i * 16 + kk] = I["L0_attn_m_conv"][i][128 * kk:128 * (kk + 1)]
    for kk in range(8):
        cols[:, TCBC + kk] = I["L0_attn_tc_b"][128 * kk:128 * (kk + 1)]
        cols[:, MURC + kk] = I["L0_ffn_mu_r"][128 * kk:128 * (kk + 1)]
        cols[:, MUKC + kk] = I["L0_ffn_mu_k"][128 * kk:128 * (kk + 1)]
        for i in range(4):
            cols[:, WCVC + i * 8 + kk] = I["L0_attn_wlam_conv"][i][128 * kk:128 * (kk + 1)]
    cols[0:4, RBC] = I["L0_attn_router_b"]
    cols[:, THC] = float(I["L0_attn_tc_theta"])
    for e in range(8):
        for kf in range(32):
            cols[:, MB1C + e * 32 + kf] = I["L0_attn_moe_b1"][e][128 * kf:128 * (kf + 1)]
        for kk in range(8):
            cols[:, MB2C + e * 8 + kk] = I["L0_attn_moe_b2"][e][128 * kk:128 * (kk + 1)]
    return cols


def make_inmaps(inputs):
    I = {kk: np.asarray(v) for kk, v in inputs.items()}
    x = np.ascontiguousarray(I["L0_attn_hidden_states"][0])
    cols_base = pack_cols(I)
    tri = np.asarray(np.triu(np.ones((CK, CK))), np.float32)
    ident = np.eye(128, dtype=np.float32)
    shared = {
        "m_win": I["L0_attn_m_win"], "m_wbc": I["L0_attn_m_wbc"], "m_wout": I["L0_attn_m_wout"],
        "tc_w": I["L0_attn_tc_w"], "w_q": I["L0_attn_w_q"], "w_k": I["L0_attn_w_k"],
        "w_v": I["L0_attn_w_v"], "w_o": I["L0_attn_w_o"], "moe_wg": I["L0_attn_moe_wg"],
        "moe_w1": I["L0_attn_moe_w1"].reshape(E * D, FF),
        "moe_w2": I["L0_attn_moe_w2"].reshape(E * FF, D),
        "ffn_wr": I["L0_ffn_W_r"], "ffn_wk": I["L0_ffn_W_k"], "ffn_wv": I["L0_ffn_W_v"],
        "router_w": I["L0_attn_router_w"], "identity": ident, "tri": tri,
        "moe_b2r": I["L0_attn_moe_b2"],
        "ohot": np.repeat(np.eye(8, dtype=np.float32), 128, axis=1).reshape(8, 8, 128).transpose(0, 1, 2).reshape(8, 1024) if False else np.kron(np.eye(8, dtype=np.float32), np.ones((1, 128), np.float32)),
    }
    shared = {kk: np.ascontiguousarray(np.asarray(v, np.float32)) for kk, v in shared.items()}
    in_maps = []
    for r in range(NCORES):
        t0 = r * T
        xp = np.zeros((T + 3, D), np.float32)
        xp[3:] = x[t0:t0 + T]
        if r > 0:
            xp[0:3] = x[t0 - 3:t0]
        halo = np.zeros((D, 4), np.float32)
        halo[:, 0:3] = xp[0:3].T
        cols = cols_base.copy()
        for j in range(8):
            cols[:, SELR + j] = 1.0 if j == r else 0.0
            cols[:, SELP + j] = 1.0 if j == r - 1 else 0.0
            cols[:, SELN + j] = 1.0 if j == r + 1 else 0.0
        m = dict(shared)
        m["x_pad"] = xp
        m["x_halo_fm"] = np.ascontiguousarray(halo)
        m["cols"] = np.ascontiguousarray(cols)
        in_maps.append(m)
    return in_maps


def kernel(**inputs):
    nc, _ = build_program()
    in_maps = make_inmaps(inputs)
    res = run_bass_kernel_spmd(nc, in_maps, core_ids=list(range(NCORES)))
    return np.concatenate([r["y"] for r in res.results], axis=0)[None]


# version 08 (final)
# speedup vs baseline: 1.2097x; 1.0641x over previous
import contextlib
import numpy as np
import concourse.bass as bass
import concourse.mybir as mybir
from concourse.bass_utils import run_bass_kernel_spmd

F32 = mybir.dt.float32
BF16 = mybir.dt.bfloat16
AF = mybir.ActivationFunctionType
ALU = mybir.AluOpType
AX = mybir.AxisListType

STREAMS = ("pe", "dve", "act", "pool", "sp")

NCORES = 8
T = 2048
D = 1024
DI = 2048
NST = 64
CK = 64
SEC = 256
NSEC = T // SEC
NCHS = SEC // CK
B3 = SEC + 3
NH = 16
HD = 64
TP = T // 2
E = 8
FF = 4096
LOG_THR = float(np.log(np.float32(0.3) / np.float32(0.7)))
GELU_S = float(2.0 * 0.7978845608028654)
GELU_C = 0.044715
DBG = []


class Buf:
    __slots__ = ("name", "t", "lw", "readers")

    def __init__(self, name, t):
        self.name, self.t = name, t
        self.lw, self.readers = None, []

    @property
    def root(self):
        return self

    def __getitem__(self, idx):
        return self.t[idx]

    @property
    def ap(self):
        return self.t[:]


class SubBuf:
    __slots__ = ("name", "parent", "off", "width", "dtype")

    def __init__(self, name, parent, off, width, dtype):
        self.name, self.parent, self.off, self.width, self.dtype = name, parent, off, width, dtype

    def __getitem__(self, idx):
        return self.ap[idx]

    @property
    def root(self):
        return self.parent

    @property
    def ap(self):
        a = self.parent.t[:].bitcast(self.dtype)
        return a[:, self.off:self.off + self.width]


class Op:
    __slots__ = ("stream", "sidx", "fn", "dma", "coll", "inc_marked",
                 "milestone", "waits", "drain_before", "tag")

    def __init__(self, stream, sidx, fn, dma, coll, tag):
        self.stream, self.sidx, self.fn = stream, sidx, fn
        self.dma, self.coll = dma, coll
        self.inc_marked = dma or coll
        self.milestone = None
        self.waits = {}
        self.drain_before = False
        self.tag = tag


class Prog:
    def __init__(self, nc):
        self.nc = nc
        self.es = contextlib.ExitStack()
        self.sops = {s: [] for s in STREAMS}
        self.nops = 0
        self._names = set()
        self.csem = {s: self.es.enter_context(nc.semaphore("c_" + s)) for s in STREAMS}
        self.NQL = 8
        self.qsem = {s: [self.es.enter_context(nc.semaphore(f"q_{s}{i}")) for i in range(self.NQL)]
                     for s in ("sp", "act", "pool")}
        self.ccsem = self.es.enter_context(nc.semaphore("ccq"))
        self._drain_cover = {s: -1 for s in STREAMS}
        self._qcnt = {s: 0 for s in ("sp", "act", "pool")}
        self._cccnt = 0

    def _nm(self, name):
        assert name not in self._names, name
        self._names.add(name)
        return name

    def sbuf(self, name, shape, dtype):
        return Buf(name, self.es.enter_context(self.nc.sbuf_tensor(self._nm(name), list(shape), dtype)))

    def psum(self, name, shape, dtype):
        return Buf(name, self.es.enter_context(self.nc.psum_tensor(self._nm(name), list(shape), dtype)))

    def dram_in(self, name, shape, dtype):
        return Buf(name, self.nc.dram_tensor(self._nm(name), list(shape), dtype, kind="ExternalInput").ap())

    def dram_out(self, name, shape, dtype):
        return Buf(name, self.nc.dram_tensor(self._nm(name), list(shape), dtype, kind="ExternalOutput").ap())

    def dram_tmp(self, name, shape, dtype):
        return Buf(name, self.nc.dram_tensor(self._nm(name), list(shape), dtype).ap())

    def op(self, stream, fn, reads=(), writes=(), dma=False, coll=False, tag=""):
        o = Op(stream, len(self.sops[stream]), fn, dma, coll, tag)
        self.nops += 1
        self.sops[stream].append(o)
        if dma:
            qn = self._qcnt[stream]
            self._qcnt[stream] = qn + 1
            o.milestone = (f"q_{stream}{qn % self.NQL}", 16 * (qn // self.NQL + 1))
        elif coll:
            self._cccnt += 1
            o.milestone = ("cc", self._cccnt)
        deps = []
        for b in reads:
            b = b.root
            if b.lw is not None:
                deps.append(b.lw)
        wroots = [b.root for b in writes]
        for b in wroots:
            if b.lw is not None:
                deps.append(b.lw)
            deps.extend(b.readers)
        best = {}
        for d in deps:
            if d is o:
                continue
            kk = d.milestone[0] if (d.dma or d.coll) else (d.stream,)
            prev = best.get(kk)
            if prev is None or d.sidx > prev.sidx:
                best[kk] = d
        need_drain = False
        for d in best.values():
            if d.stream == stream and not d.dma and not d.coll:
                if stream == "pe":
                    continue
                if d.sidx > self._drain_cover[stream]:
                    need_drain = True
                continue
            d.inc_marked = True
            o.waits[d] = True
        if need_drain:
            o.drain_before = True
            self._drain_cover[stream] = o.sidx - 1
        for b in reads:
            b.root.readers.append(o)
        for b in wroots:
            b.lw = o
            b.readers = []
        return o

    def dma(self, stream, fn, reads=(), writes=(), tag=""):
        assert stream in ("sp", "act", "pool")
        return self.op(stream, fn, reads, writes, dma=True, tag=tag)

    def coll(self, fn, reads=(), writes=(), tag=""):
        return self.op("pool", fn, reads, writes, coll=True, tag=tag)

    def tt(self, st, out, a, b, alu, reads, writes, tag=""):
        self.op(st, lambda e: e.tensor_tensor(out, a, b, alu), reads, writes, tag=tag)

    def ts(self, st, out, a, s1, s2, op0, op1, reads, writes, tag=""):
        self.op(st, lambda e: e.tensor_scalar(out, a, s1, s2, op0, op1), reads, writes, tag=tag)

    def stt(self, st, out, a, s, b, op0, op1, reads, writes, tag=""):
        self.op(st, lambda e: e.scalar_tensor_tensor(out, a, s, b, op0, op1), reads, writes, tag=tag)

    def act(self, out, a, fn, reads, writes, scale=None, bias=None, accum=None, tag=""):
        kw = {}
        if scale is not None:
            kw["scale"] = scale
        if bias is not None:
            kw["bias"] = bias
        if accum is not None:
            kw["accum_out"] = accum
        self.op("act", lambda e: e.activation(out, a, fn, **kw), reads, writes, tag=tag)

    def copy(self, st, out, a, reads, writes, tag=""):
        self.op(st, lambda e: e.tensor_copy(out, a), reads, writes, tag=tag)

    def mm(self, out, lhsT, rhs, start, stop, reads, writes, tag=""):
        self.op("pe", lambda e: e.matmul(out, lhsT, rhs, start=start, stop=stop), reads, writes, tag=tag)

    def tr(self, out, in_, ident, reads, writes, tag=""):
        self.op("pe", lambda e: e.transpose(out, in_, ident), reads, writes, tag=tag)

    def finalize(self, debug=False):
        totals = {}
        qlane_tot = {}
        for s in STREAMS:
            cc = 0
            for o in self.sops[s]:
                if o.dma or o.coll or not o.inc_marked:
                    continue
                cc += 1
                o.milestone = ("c_" + s, cc)
            totals[s] = (self._qcnt.get(s, 0) * 16, self._cccnt if s == "pool" else 0, cc)
        for s in ("sp", "act", "pool"):
            qn = self._qcnt[s]
            qlane_tot[s] = [16 * ((qn - 1 - i) // self.NQL + 1) if i < min(qn, self.NQL) else 0
                            for i in range(self.NQL)]
        sem_of = {("c_" + s): self.csem[s] for s in STREAMS}
        for s in ("sp", "act", "pool"):
            for i in range(self.NQL):
                sem_of[f"q_{s}{i}"] = self.qsem[s][i]
        sem_of["cc"] = self.ccsem
        nwaits = [0]

        def emit(eng, s):
            hw = {}
            for o in self.sops[s]:
                if o.drain_before:
                    eng.drain()
                byname = {}
                for d in o.waits:
                    nm, v = d.milestone
                    if byname.get(nm, 0) < v:
                        byname[nm] = v
                for nm in sorted(byname):
                    v = byname[nm]
                    if hw.get(nm, 0) >= v:
                        continue
                    eng.wait_ge(sem_of[nm], v)
                    hw[nm] = v
                    nwaits[0] += 1
                inst = o.fn(eng)
                if o.inc_marked:
                    nm, _ = o.milestone
                    inst.then_inc(sem_of[nm], 16 if o.dma else 1)

        block = self.es.enter_context(self.nc.Block())

        @block.tensor
        def _(e):
            emit(e, "pe")

        @block.vector
        def _(e):
            emit(e, "dve")

        @block.scalar
        def _(e):
            emit(e, "act")

        @block.gpsimd
        def _(e):
            emit(e, "pool")

        @block.sync
        def _(e):
            emit(e, "sp")
            for s in ("sp", "act", "pool"):
                for i in range(self.NQL):
                    if qlane_tot[s][i]:
                        e.wait_ge(self.qsem[s][i], qlane_tot[s][i])
            if totals["pool"][1]:
                e.wait_ge(self.ccsem, totals["pool"][1])

        if debug:
            print(f"Prog: {self.nops} ops, {nwaits[0]} waits; totals={totals}")
        return totals


def ceil_div(a, b):
    return -(-a // b)


LNG, LNB = 0, 32
ALOGC, WDTC, DTBC, DSKC = 64, 80, 96, 112
TCBC, MURC, MUKC = 128, 136, 144
SELR, SELP, SELN = 152, 160, 168
CONVC = 176
RBC, THC = 240, 241
MB1C, MB2C = 256, 512
WCVC = 576
COLS_W = 640
KVW = 1040


class Kern:
    def __init__(self, nc, tap=None):
        self.nc = nc
        self.p = Prog(nc)
        self.tap = tap
        p = self.p
        s = p.sbuf

        self.i_x = p.dram_in("x_pad", [T + 3, D], F32)
        self.i_xhalo = p.dram_in("x_halo_fm", [D, 4], F32)
        self.i_win = p.dram_in("m_win", [D, 2 * DI], F32)
        self.i_wbc = p.dram_in("m_wbc", [DI, 2 * NST], F32)
        self.i_wout = p.dram_in("m_wout", [DI, D], F32)
        self.i_tcw = p.dram_in("tc_w", [D, D], F32)
        self.i_wq = p.dram_in("w_q", [D, D], F32)
        self.i_wk = p.dram_in("w_k", [D, D], F32)
        self.i_wv = p.dram_in("w_v", [D, D], F32)
        self.i_wo = p.dram_in("w_o", [D, D], F32)
        self.i_wg = p.dram_in("moe_wg", [D, E], F32)
        self.i_w1 = p.dram_in("moe_w1", [E * D, FF], F32)
        self.i_w2 = p.dram_in("moe_w2", [E * FF, D], F32)
        self.i_wr = p.dram_in("ffn_wr", [D, D], F32)
        self.i_wkf = p.dram_in("ffn_wk", [D, D], F32)
        self.i_wvf = p.dram_in("ffn_wv", [D, D], F32)
        self.i_rw = p.dram_in("router_w", [D, 4], F32)
        self.i_b2r = p.dram_in("moe_b2r", [8, 1024], F32)
        self.i_ohot = p.dram_in("ohot", [8, 1024], F32)
        self.i_cols = p.dram_in("cols", [128, COLS_W], F32)
        self.i_ident = p.dram_in("identity", [128, 128], F32)
        self.i_tri = p.dram_in("tri", [CK, CK], F32)
        self.out = p.dram_out("y", [T, D], F32)

        self.g_pool_in = p.dram_tmp("cc_pool_in", [128, 8], F32)
        self.g_pool_out = p.dram_tmp("cc_pool_out", [128, 8], F32)
        self.g_mst_in = p.dram_tmp("cc_mst_in", [DI, NST], F32)
        self.g_mp_in = p.dram_tmp("cc_mp_in", [128, 16], F32)
        self.g_mp_out = p.dram_tmp("cc_mp_out", [NCORES * 128, 16], F32)
        self.g_mst_out = p.dram_tmp("cc_mst_out", [NCORES * DI, NST], F32)
        self.g_kv_in = p.dram_tmp("cc_kv_in", [2 * D + 16, KVW], BF16)
        self.g_kv_out = p.dram_tmp("cc_kv_out", [NCORES * (2 * D + 16), KVW], BF16)
        self.g_yw_in = p.dram_tmp("cc_yw_in", [D, 4], F32)
        self.g_yw_out = p.dram_tmp("cc_yw_out", [NCORES * D, 4], F32)
        self.g_xp_in = p.dram_tmp("cc_xp_in", [128, 8], F32)
        self.g_xp_out = p.dram_tmp("cc_xp_out", [NCORES * 128, 8], F32)

        self.h = [s(f"h{c}", [128, T], F32) for c in range(8)]
        self.consts = s("consts", [128, COLS_W], F32)
        self.ident = s("identf", [128, 128], F32)
        self.identb = s("identb", [128, 128], BF16)
        self.onesb = s("onesb", [128, 128], BF16)
        self.onesf = s("onesf", [128, 256], F32)
        self.trib = s("trib", [CK, CK], F32)
        self.masks = s("masksrep", [128, 4], F32)
        self.rows = s("rows", [1, 1536], F32)

        _arw = [4144, 4096, 4096, 4096, 4160, 4096]
        self.ar = [s(f"ar{i}", [128, _arw[i]], F32) for i in range(6)]

        self.psA = p.psum("psA", [128, 1024], F32)
        self.psB = p.psum("psB", [128, 1024], F32)
        self.psC = p.psum("psC", [128, 512], F32)
        self.psD = p.psum("psD", [128, 512], F32)
        self.psT = p.psum("psT", [128, 2048], BF16)
        self._psi = 0

        self.ln_xb = s("ln_xb", [128, 512], BF16)
        self.ln_xb2 = s("ln_xb2", [128, 512], BF16)
        self.ln_sq = s("ln_sq", [128, 512], BF16)
        self.ln_sq2 = s("ln_sq2", [128, 512], BF16)
        self.ln_t1 = s("ln_t1", [128, 512], F32)

        self.wst = [s(f"wst{i}", [128, 2048], BF16) for i in range(2)]

        self.xh = s("xh", [128, 32], F32)
        self.mlnb = [s(f"mlnb{c}", [128, B3], BF16) for c in range(8)]
        self.mBb = s("mBb", [64, SEC], BF16)
        self.mCb = s("mCb", [64, SEC], BF16)
        self.mwtm = s("mwtm", [64, 2048], BF16)
        self.mhtm = s("mhtm", [64, 2048], BF16)
        self.mbtm = s("mbtm", [64, 64], BF16)
        self.mhbf = s("mhbf", [128, 16 * 64], BF16)
        self.mhst = s("mhst", [128, 16 * 64], F32)
        self.mqcol = s("mqcol", [128, 16], F32)
        self.mdtot = s("mdtot", [128, 16], F32)
        self.mec = s("mec", [128, 16 * NCHS], F32)
        self.mpc = s("mpc", [128, 16 * NCHS], F32)
        self.mmc = s("mmc", [128, 16 * NCHS], F32)
        self.mdh = s("mdh", [128, 16 * NCHS], F32)
        self.mdf = s("mdf", [128, 16 * NCHS], F32)
        self.mgm = s("mgm", [64, CK], BF16)
        self.pooled = s("pooled", [128, 8], F32)
        self.poolg = s("poolg", [128, 8], F32)
        self.rwsb = s("rwsb", [128, 32], F32)
        self.lg = s("lg", [4, 1], F32)
        self.mrow = s("mrow", [1, 4], F32)

    def sbuf(self, name, shape, dtype):
        return self.p.sbuf(name, shape, dtype)

    def next_ps(self):
        b = (self.psA, self.psB)[self._psi % 2]
        self._psi += 1
        return b

    def sub(self, name, ai, off, width, dtype):
        return SubBuf(name, self.ar[ai], off, width, dtype)

    def col(self, idx, n=1):
        return self.consts[:, idx:idx + n]

    def cbc(self, idx, nblk, w):
        return self.consts[:, idx:idx + nblk].unsqueeze(2).broadcast_to([128, nblk, w])


def blkv(buf, nblk, off=0, w=None):
    a = buf.ap
    q = a.shape[1] // nblk
    a = a.rearrange("p (b q) -> p b q", b=nblk)
    if w is None:
        return a
    return a[:, :, off:off + w]


def emit_setup(k):
    p = k.p
    p.dma("sp", lambda e: e.dma_start(out=k.consts.ap, in_=k.i_cols.ap), writes=[k.consts], tag="ld_cols")
    p.dma("sp", lambda e: e.dma_start(out=k.ident.ap, in_=k.i_ident.ap), writes=[k.ident])
    p.dma("pool", lambda e: e.dma_start(out=k.identb.ap, in_=k.i_ident.ap), writes=[k.identb])
    p.dma("sp", lambda e: e.dma_start(out=k.trib.ap, in_=k.i_tri.ap), writes=[k.trib])
    p.op("dve", lambda e: e.memset(k.onesf.ap, 1.0), writes=[k.onesf])
    p.op("dve", lambda e: e.memset(k.onesb.ap, 1.0), writes=[k.onesb])
    acol = k.consts[:, ALOGC:ALOGC + 16]
    p.act(acol, acol, AF.Exp, reads=[k.consts], writes=[k.consts], tag="A_exp")
    p.ts("dve", acol, acol, -1.0, None, ALU.mult, ALU.bypass, reads=[k.consts], writes=[k.consts], tag="A_neg")


def emit_input(k):
    p = k.p
    xtm = [k.sub(f"xtm{i}", i, 0, 1024, F32) for i in range(2)]
    for i in range(16):
        b = xtm[i % 2]
        src = k.i_x[3 + 128 * i: 3 + 128 * (i + 1), :]
        p.dma("sp", lambda e, b=b, src=src: e.dma_start(out=b.ap, in_=src), writes=[b], tag=f"ldx{i}")
        ps = (k.psA, k.psB)[i % 2]
        for c in range(8):
            p.tr(ps[:, 128 * c:128 * (c + 1)], b.ap[:, 128 * c:128 * (c + 1)], k.ident.ap,
                 reads=[b, k.ident], writes=[ps], tag=f"trx{i}_{c}")
        for c in range(8):
            p.copy("dve", k.h[c][:, 128 * i:128 * (i + 1)], ps[:, 128 * c:128 * (c + 1)],
                   reads=[ps], writes=[k.h[c]], tag=f"evx{i}_{c}")
    for c in range(8):
        p.op("dve", lambda e, c=c: e.tensor_reduce(k.pooled[:, c:c + 1], k.h[c].ap, AX.X, ALU.add),
             reads=[k.h[c]], writes=[k.pooled], tag=f"pool{c}")
    gi, go = k.g_pool_in, k.g_pool_out
    p.dma("sp", lambda e: e.dma_start(out=gi.ap, in_=k.pooled.ap), reads=[k.pooled], writes=[gi])
    p.coll(lambda e: e.collective_compute("AllReduce", ALU.add, replica_groups=[list(range(NCORES))],
                                          ins=[gi.ap.opt()], outs=[go.ap.opt()]),
           reads=[gi], writes=[go], tag="AR_pool")
    p.dma("sp", lambda e: e.dma_start(out=k.poolg.ap, in_=go.ap), reads=[go], writes=[k.poolg])
    p.ts("dve", k.poolg.ap, k.poolg.ap, 1.0 / (NCORES * T), None, ALU.mult, ALU.bypass,
         reads=[k.poolg], writes=[k.poolg], tag="poolmean")
    p.dma("sp", lambda e: e.dma_start(out=k.rwsb.ap.rearrange("p (kk f) -> p kk f", kk=8),
                                      in_=k.i_rw.ap.rearrange("(kk p) f -> p kk f", p=128)),
          writes=[k.rwsb])
    psr = k.psC
    for c in range(8):
        p.mm(psr[0:4, 0:1], k.rwsb[:, 4 * c:4 * c + 4], k.poolg[:, c:c + 1], c == 0, c == 7,
             reads=[k.rwsb, k.poolg], writes=[psr], tag=f"rlog{c}")
    p.tt("dve", k.lg.ap, psr[0:4, 0:1], k.consts[0:4, RBC:RBC + 1], ALU.add,
         reads=[psr, k.consts], writes=[k.lg])
    p.ts("dve", k.lg.ap, k.lg.ap, LOG_THR, None, ALU.is_gt, ALU.bypass, reads=[k.lg], writes=[k.lg])
    if "rt" in DBG:
        dp = p.dram_out("dbg_pooled", [128, 8], F32)
        p.dma("sp", lambda e: e.dma_start(out=dp.ap, in_=k.pooled.ap), reads=[k.pooled], writes=[dp])
        d5 = p.dram_out("dbg_lg", [4, 1], F32)
        d6 = p.dram_out("dbg_poolg", [128, 8], F32)
        p.dma("sp", lambda e: e.dma_start(out=d5.ap, in_=k.lg.ap), reads=[k.lg], writes=[d5])
        p.dma("sp", lambda e: e.dma_start(out=d6.ap, in_=k.poolg.ap), reads=[k.poolg], writes=[d6])
    p.tr(psr[0:1, 4:8], k.lg.ap, k.ident[0:4, 0:4], reads=[k.lg, k.ident], writes=[psr], tag="masktr")
    p.copy("dve", k.mrow.ap, psr[0:1, 4:8], reads=[psr], writes=[k.mrow])
    p.mm(psr[:, 8:12], k.onesf[0:1, 0:128], k.mrow.ap, True, True,
         reads=[k.onesf, k.mrow], writes=[psr], tag="maskbc")
    p.copy("dve", k.masks.ap, psr[:, 8:12], reads=[psr], writes=[k.masks])


def emit_ln(k, nm, src_aps, src_bufs, outs, ln_i, eps, width):
    p = k.p
    nch = ceil_div(width, 512)
    for ci in range(nch):
        a = ci * 512
        w = min(512, width - a)
        psS, psQ = k.psC, k.psD
        for c in range(8):
            sap = src_aps(c, a, w)
            rb = src_bufs(c)
            xb = (k.ln_xb, k.ln_xb2)[c % 2]
            sq = (k.ln_sq, k.ln_sq2)[c % 2]
            p.copy("dve", xb[:, 0:w], sap, reads=[rb], writes=[xb], tag=f"{nm}cv{c}")
            p.act(sq[:, 0:w], sap, AF.Square, reads=[rb], writes=[sq], tag=f"{nm}sq{c}")
            p.mm(psS[0:1, 0:w], k.onesb[:, 0:1], xb[:, 0:w], c == 0, c == 7,
                 reads=[k.onesb, xb], writes=[psS], tag=f"{nm}mS{c}")
            p.mm(psQ[0:1, 0:w], k.onesb[:, 0:1], sq[:, 0:w], c == 0, c == 7,
                 reads=[k.onesb, sq], writes=[psQ], tag=f"{nm}mQ{c}")
        m_r, v_r, nm_r = k.rows[0:1, 0:w], k.rows[0:1, 512:512 + w], k.rows[0:1, 1024:1024 + w]
        p.ts("dve", m_r, psS[0:1, 0:w], 1.0 / D, None, ALU.mult, ALU.bypass, reads=[psS], writes=[k.rows])
        p.tt("dve", v_r, m_r, m_r, ALU.mult, reads=[k.rows], writes=[k.rows])
        p.stt("dve", v_r, psQ[0:1, 0:w], 1.0 / D, v_r, ALU.mult, ALU.subtract,
              reads=[psQ, k.rows], writes=[k.rows])
        p.ts("dve", v_r, v_r, float(eps), None, ALU.add, ALU.bypass, reads=[k.rows], writes=[k.rows])
        p.act(v_r, v_r, AF.Ln, reads=[k.rows], writes=[k.rows])
        p.act(v_r, v_r, AF.Exp, scale=-0.5, reads=[k.rows], writes=[k.rows])
        p.tt("dve", nm_r, m_r, v_r, ALU.mult, reads=[k.rows], writes=[k.rows])
        p.ts("dve", nm_r, nm_r, -1.0, None, ALU.mult, ALU.bypass, reads=[k.rows], writes=[k.rows])
        psR, psB_ = k.psA, k.psB
        p.mm(psR[:, 0:w], k.onesf[0:1, 0:128], v_r, True, True, reads=[k.onesf, k.rows], writes=[psR], tag=f"{nm}bR")
        p.mm(psB_[:, 0:w], k.onesf[0:1, 0:128], nm_r, True, True, reads=[k.onesf, k.rows], writes=[psB_], tag=f"{nm}bB")
        for c in range(8):
            sap = src_aps(c, a, w)
            rb = src_bufs(c)
            tb = k.ln_t1
            p.tt("dve", tb[:, 0:w], sap, psR[:, 0:w], ALU.mult, reads=[rb, psR], writes=[tb], tag=f"{nm}a1_{c}")
            d0ap, d0buf = outs[0][0](c, a, w), outs[0][1](c)
            if ln_i is not None:
                p.tt("dve", tb[:, 0:w], tb[:, 0:w], psB_[:, 0:w], ALU.add, reads=[tb, psB_], writes=[tb])
                g = k.col(LNG + 8 * ln_i + c)
                bb = k.col(LNB + 8 * ln_i + c)
                p.ts("dve", d0ap, tb[:, 0:w], g, bb, ALU.mult, ALU.add,
                     reads=[tb, k.consts], writes=[d0buf], tag=f"{nm}a2_{c}")
            else:
                p.tt("dve", d0ap, tb[:, 0:w], psB_[:, 0:w], ALU.add, reads=[tb, psB_], writes=[d0buf], tag=f"{nm}a2_{c}")
            for oap, obuf in outs[1:]:
                p.copy("dve", oap(c, a, w), d0ap, reads=[d0buf], writes=[obuf(c)], tag=f"{nm}cp{c}")


def alloc_mamba(k):
    k.MB1 = k.sub("MB1", 0, 0, 16 * B3, F32)
    k.mxc = k.sub("mxc", 1, 0, 16 * SEC, F32)
    k.ma1 = k.sub("ma1", 2, 0, 16 * SEC, F32)
    k.ma4 = k.sub("ma4", 3, 0, 16 * SEC, F32)
    k.MB2 = k.sub("MB2", 4, 0, 16 * SEC, BF16)
    k.mwbf = k.sub("mwbf", 4, 16 * SEC, 16 * SEC, BF16)
    k.meqb = k.sub("meqb", 5, 0, 16 * SEC, BF16)
    k.MB3 = k.sub("MB3", 5, 16 * SEC, 16 * SEC, BF16)


def emit_mamba_section(k, s, pass2):
    p = k.p
    base = s * SEC
    if s == 0:
        p.dma("sp", lambda e: e.dma_start(
            out=k.xh.ap.rearrange("p (cc f) -> p cc f", cc=8),
            in_=k.i_xhalo.ap.rearrange("(cc p) f -> p cc f", p=128)), writes=[k.xh], tag="ldxh")
        emit_ln(k, "lnh", lambda c, a, w: k.xh[:, 4 * c + a: 4 * c + a + w],
                lambda c: k.xh,
                [(lambda c, a, w: k.mlnb[c][:, a:a + w], lambda c: k.mlnb[c])], 0, 1e-6, width=3)
    else:
        for c in range(8):
            p.copy("dve", k.mlnb[c][:, 0:3], k.mlnb[c][:, B3 - 3:B3],
                   reads=[k.mlnb[c]], writes=[k.mlnb[c]], tag=f"lead{c}")
        xzv = blkv(k.MB1, 16)
        p.copy("dve", xzv[:, :, 0:3], xzv[:, :, B3 - 3:B3], reads=[k.MB1], writes=[k.MB1], tag="xzlead")
    emit_ln(k, f"ln0s{s}p{int(pass2)}",
            lambda c, a, w: k.h[c][:, base + a: base + a + w], lambda c: k.h[c],
            [(lambda c, a, w: k.mlnb[c][:, 3 + a:3 + a + w], lambda c: k.mlnb[c])], 0, 1e-6, width=SEC)

    rw0, rww = (0, B3) if s == 0 else (3, SEC)
    fts = list(range(16)) + (list(range(16, 32)) if pass2 else [])
    for fgi in range(0, len(fts), 2):
        wb = k.wst[(fgi // 2) % 2]
        f0 = fts[fgi]
        src = k.i_win.ap[:, 128 * f0:128 * f0 + 256].rearrange("(cc p) f -> p cc f", p=128)
        p.dma("pool", lambda e, wb=wb, src=src: e.dma_start(
            out=wb.ap.rearrange("p (cc f) -> p cc f", cc=8), in_=src), writes=[wb], tag=f"ldwin{f0}")
        for fo in range(2):
            ft = f0 + fo
            pst = (k.psA, k.psB)[(ft // 2) % 2]
            reg = pst[:, (ft % 2) * 512:(ft % 2) * 512 + rww]
            for c in range(8):
                p.mm(reg, wb[:, c * 256 + fo * 128: c * 256 + fo * 128 + 128],
                     k.mlnb[c][:, rw0:rw0 + rww], c == 0, c == 7,
                     reads=[wb, k.mlnb[c]], writes=[pst], tag=f"xz{ft}")
            if ft < 16:
                dst = k.MB1.ap[:, ft * B3 + rw0: ft * B3 + rw0 + rww]
                p.copy("dve", dst, reg, reads=[pst], writes=[k.MB1], tag=f"xzev{ft}")
            else:
                zdst = k.MB2.ap[:, (ft - 16) * SEC:(ft - 16) * SEC + SEC]
                srcp = pst[:, (ft % 2) * 512 + (3 if s == 0 else 0):(ft % 2) * 512 + (3 if s == 0 else 0) + SEC]
                p.act(zdst, srcp, AF.Silu, reads=[pst], writes=[k.MB2], tag=f"silz{ft}")

    for c in range(16):
        st = "dve"
        a4b = k.ma4.ap[:, c * SEC:(c + 1) * SEC]
        p.ts(st, a4b, k.MB1.ap[:, c * B3 + 0: c * B3 + SEC], k.col(CONVC + 0 * 16 + c), None,
             ALU.mult, ALU.bypass, reads=[k.MB1, k.consts], writes=[k.ma4], tag=f"cv0_{c}")
        for i in range(1, 4):
            p.stt(st, a4b, k.MB1.ap[:, c * B3 + i: c * B3 + i + SEC], k.col(CONVC + i * 16 + c), a4b,
                  ALU.mult, ALU.add, reads=[k.MB1, k.consts, k.ma4], writes=[k.ma4], tag=f"cv{i}_{c}")

    p.act(k.mxc.ap, k.ma4.ap, AF.Silu, reads=[k.ma4], writes=[k.mxc], tag="silxc")
    if pass2:
        p.copy("dve", k.mwbf.ap, k.mxc.ap, reads=[k.mxc], writes=[k.mwbf], tag="xcb")
        xcb = k.mwbf
    else:
        p.copy("dve", k.mwbf.ap, k.mxc.ap, reads=[k.mxc], writes=[k.mwbf], tag="xcb")
        xcb = k.mwbf
    for half, dstb, pst in ((0, k.mBb, k.psC), (1, k.mCb, k.psD)):
        if half == 1 and not pass2:
            continue
        wbb = k.wst[half]
        src_ = k.i_wbc.ap[:, 64 * half:64 * half + 64].rearrange("(cc p) q -> p cc q", p=128)
        p.dma("pool", lambda e, wbb=wbb, src_=src_: e.dma_start(
            out=wbb.ap[:, 0:1024].rearrange("p (cc q) -> p cc q", cc=16), in_=src_),
            writes=[wbb], tag=f"ldbc{half}")
        for c in range(16):
            p.mm(pst[0:64, 0:SEC], wbb[:, c * 64:(c + 1) * 64],
                 xcb.ap[:, c * SEC:(c + 1) * SEC], c == 0, c == 15,
                 reads=[wbb, xcb], writes=[pst], tag=f"bc{half}_{c}")
        p.copy("dve", dstb.ap, pst[0:64, 0:SEC], reads=[pst], writes=[dstb], tag=f"bcev{half}")

    a1v = blkv(k.ma1, 16)
    a4v = blkv(k.ma4, 16)
    xcv = blkv(k.mxc, 16)
    p.tt("dve", a1v, xcv, k.cbc(WDTC, 16, SEC), ALU.mult, reads=[k.mxc, k.consts], writes=[k.ma1], tag="dt1")
    p.tt("dve", a1v, a1v, k.cbc(DTBC, 16, SEC), ALU.add, reads=[k.ma1, k.consts], writes=[k.ma1], tag="dt2")
    p.act(k.ma4.ap, k.ma1.ap, AF.Abs, reads=[k.ma1], writes=[k.ma4], tag="sp_abs")
    p.act(k.ma4.ap, k.ma4.ap, AF.Exp, scale=-1.0, reads=[k.ma4], writes=[k.ma4], tag="sp_exp")
    p.act(k.ma4.ap, k.ma4.ap, AF.Ln, bias=k.onesf[:, 0:1], reads=[k.ma4, k.onesf], writes=[k.ma4], tag="sp_ln")
    p.stt("dve", k.ma1.ap, k.ma1.ap, 0.0, k.ma4.ap, ALU.max, ALU.add, reads=[k.ma1, k.ma4], writes=[k.ma1], tag="sp_fin")
    p.tt("dve", a4v, a1v, k.cbc(ALOGC, 16, SEC), ALU.mult, reads=[k.ma1, k.consts], writes=[k.ma4], tag="dtA")
    p.tt("dve", a1v, a1v, xcv, ALU.mult, reads=[k.ma1, k.mxc], writes=[k.ma1], tag="vv")

    for c in range(16):
        p.op("dve", lambda e, c=c: e.tensor_tensor_scan(
            k.MB1.ap[:, c * B3:c * B3 + SEC], k.onesf[:, 0:SEC], k.ma4.ap[:, c * SEC:(c + 1) * SEC],
            k.mqcol[:, c:c + 1], ALU.mult, ALU.add),
            reads=[k.onesf, k.ma4, k.mqcol], writes=[k.MB1], tag=f"qscan{c}")

    qv = blkv(k.MB1, 16)
    ecv = blkv(k.mec, 16)
    pcv = blkv(k.mpc, 16)
    mcv = blkv(k.mmc, 16)
    p.copy("dve", ecv, qv[:, :, CK - 1:SEC:CK], reads=[k.MB1], writes=[k.mec], tag="bnd_e")
    p.copy("dve", pcv[:, :, 0:1], k.mqcol.ap.unsqueeze(2), reads=[k.mqcol], writes=[k.mpc], tag="bnd_p0")
    if NCHS > 1:
        p.copy("dve", pcv[:, :, 1:NCHS], ecv[:, :, 0:NCHS - 1], reads=[k.mec], writes=[k.mpc], tag="bnd_p1")
    p.tt("dve", mcv, ecv, pcv, ALU.add, reads=[k.mec, k.mpc], writes=[k.mmc], tag="bnd_m1")
    p.ts("dve", mcv, mcv, 0.5, None, ALU.mult, ALU.bypass, reads=[k.mmc], writes=[k.mmc], tag="bnd_m2")
    p.tt("dve", blkv(k.mdf, 16), mcv, pcv, ALU.subtract, reads=[k.mmc, k.mpc], writes=[k.mdf], tag="bnd_d1")
    p.act(k.mdh.ap, k.mdf.ap, AF.Exp, reads=[k.mdf], writes=[k.mdh], tag="bnd_dh")
    p.tt("dve", k.mdf.ap, k.mdh.ap, k.mdh.ap, ALU.mult, reads=[k.mdh], writes=[k.mdf], tag="bnd_df")
    if not pass2:
        for j in range(NCHS):
            p.tt("dve", k.mdtot.ap, k.mdtot.ap, blkv(k.mdf, 16)[:, :, j:j + 1].squeeze(2),
                 ALU.mult, reads=[k.mdtot, k.mdf], writes=[k.mdtot], tag=f"dtot{j}")
    p.copy("dve", k.mqcol.ap, qv[:, :, SEC - 1:SEC].squeeze(2), reads=[k.MB1], writes=[k.mqcol], tag="qcol")

    qv4 = blkv(k.MB1, 16)[:, :, 0:SEC].rearrange("p b (j t) -> p b j t", j=NCHS)
    mc4 = blkv(k.mmc, 16).unsqueeze(3).broadcast_to([128, 16, NCHS, CK])
    a44 = k.ma4.ap.rearrange("p (b j t) -> p b j t", b=16, j=NCHS)
    p.tt("dve", a44, mc4, qv4, ALU.subtract, reads=[k.mmc, k.MB1], writes=[k.ma4], tag="exparg")
    if pass2:
        p.act(k.meqb.ap, k.ma4.ap, AF.Exp, scale=-1.0, reads=[k.ma4], writes=[k.meqb], tag="eqb")
    p.act(k.ma4.ap, k.ma4.ap, AF.Exp, reads=[k.ma4], writes=[k.ma4], tag="wexp")
    p.tt("dve", k.mwbf.ap, k.ma4.ap, k.ma1.ap, ALU.mult, reads=[k.ma4, k.ma1], writes=[k.mwbf], tag="wbf")

    for j in range(NCHS):
        p.tr(k.psT[0:64, 1920:1984], k.mBb[:, j * CK:(j + 1) * CK], k.identb[0:64, 0:64],
             reads=[k.mBb, k.identb], writes=[k.psT], tag=f"btr{j}")
        p.copy("dve", k.mbtm.ap, k.psT[0:64, 1920:1984], reads=[k.psT], writes=[k.mbtm], tag=f"btm{j}")
        for c in range(16):
            p.tr(k.psT[0:64, c * 128:(c + 1) * 128],
                 k.mwbf.ap[:, c * SEC + j * CK: c * SEC + j * CK + CK],
                 k.identb.ap, reads=[k.mwbf, k.identb], writes=[k.psT], tag=f"wtr{j}_{c}")
        p.copy("dve", k.mwtm.ap, k.psT[0:64, 0:2048], reads=[k.psT], writes=[k.mwtm], tag=f"wtmev{j}")
        if pass2:
            p.mm(k.psD[0:64, 0:CK], k.mBb[:, j * CK:(j + 1) * CK], k.mCb[:, j * CK:(j + 1) * CK],
                 True, True, reads=[k.mBb, k.mCb], writes=[k.psD], tag=f"g{j}")
            p.tt("dve", k.mgm.ap, k.psD[0:64, 0:CK], k.trib.ap, ALU.mult,
                 reads=[k.psD, k.trib], writes=[k.mgm], tag=f"gm{j}")
            p.tt("dve", blkv(k.mhbf, 16), blkv(k.mhst, 16),
                 blkv(k.mdh, 16)[:, :, j:j + 1].broadcast_to([128, 16, 64]),
                 ALU.mult, reads=[k.mhst, k.mdh], writes=[k.mhbf], tag=f"hbf{j}")
            for c in range(16):
                p.tr(k.psT[0:64, c * 128:(c + 1) * 128], k.mhbf[:, c * 64:(c + 1) * 64],
                     k.identb.ap, reads=[k.mhbf, k.identb], writes=[k.psT], tag=f"htr{j}_{c}")
            p.copy("dve", k.mhtm.ap, k.psT[0:64, 0:2048], reads=[k.psT], writes=[k.mhtm], tag=f"htmev{j}")
            for c in range(16):
                p.mm(k.psA[:, c * 64:(c + 1) * 64], k.mwtm[0:64, c * 128:(c + 1) * 128], k.mgm.ap,
                     c in (0, 8), False, reads=[k.mwtm, k.mgm], writes=[k.psA], tag=f"s1_{j}_{c}")
            for c in range(16):
                p.mm(k.psA[:, c * 64:(c + 1) * 64], k.mhtm[0:64, c * 128:(c + 1) * 128],
                     k.mCb[:, j * CK:(j + 1) * CK], False, True,
                     reads=[k.mhtm, k.mCb], writes=[k.psA], tag=f"s2_{j}_{c}")
            p.tt("dve", k.MB3.ap.rearrange("p (b j t) -> p b j t", b=16, j=NCHS)[:, :, j, :],
                 k.psA.ap.rearrange("p (b t) -> p b t", b=16),
                 k.meqb.ap.rearrange("p (b j t) -> p b j t", b=16, j=NCHS)[:, :, j, :],
                 ALU.mult, reads=[k.psA, k.meqb], writes=[k.MB3], tag=f"ylo{j}")
        for c in range(16):
            p.mm(k.psB[:, c * 64:(c + 1) * 64], k.mwtm[0:64, c * 128:(c + 1) * 128], k.mbtm.ap,
                 c in (0, 8), True, reads=[k.mwtm, k.mbtm], writes=[k.psB], tag=f"upd{j}_{c}")
        p.tt("dve", k.psB.ap.rearrange("p (b t) -> p b t", b=16), k.psB.ap.rearrange("p (b t) -> p b t", b=16),
             blkv(k.mdh, 16)[:, :, j:j + 1].broadcast_to([128, 16, 64]), ALU.mult,
             reads=[k.psB, k.mdh], writes=[k.psB], tag=f"hup1_{j}")
        p.tt("dve", blkv(k.mhst, 16), blkv(k.mhst, 16),
             blkv(k.mdf, 16)[:, :, j:j + 1].broadcast_to([128, 16, 64]), ALU.mult,
             reads=[k.mhst, k.mdf], writes=[k.mhst], tag=f"hup2_{j}")
        p.tt("dve", k.mhst.ap, k.mhst.ap, k.psB.ap, ALU.add,
             reads=[k.mhst, k.psB], writes=[k.mhst], tag=f"hup3_{j}")

    if pass2:
        p.tt("dve", a4v, xcv, k.cbc(DSKC, 16, SEC), ALU.mult, reads=[k.mxc, k.consts], writes=[k.ma4], tag="dsk1")
        p.tt("dve", k.MB3.ap, k.MB3.ap, k.ma4.ap, ALU.add, reads=[k.MB3, k.ma4], writes=[k.MB3], tag="dsk2")
        p.tt("dve", k.MB3.ap, k.MB3.ap, k.MB2.ap, ALU.mult, reads=[k.MB3, k.MB2], writes=[k.MB3], tag="gate")
        for f in range(8):
            wb = k.wst[f % 2]
            src = k.i_wout.ap[:, 128 * f:128 * (f + 1)].rearrange("(cc p) q -> p cc q", p=128)
            p.dma("pool", lambda e, wb=wb, src=src: e.dma_start(
                out=wb.ap.rearrange("p (cc q) -> p cc q", cc=16), in_=src), writes=[wb], tag=f"ldwo{f}")
            pst = (k.psC, k.psD)[f % 2]
            for c in range(16):
                p.mm(pst[:, 0:SEC], wb[:, c * 128:(c + 1) * 128], k.MB3.ap[:, c * SEC:(c + 1) * SEC],
                     c == 0, c == 15, reads=[wb, k.MB3], writes=[pst], tag=f"wo{f}_{c}")
            p.stt("dve", k.h[f][:, base:base + SEC], pst[:, 0:SEC], k.masks[:, 0:1],
                  k.h[f][:, base:base + SEC], ALU.mult, ALU.add,
                  reads=[pst, k.masks, k.h[f]], writes=[k.h[f]], tag=f"hout{f}")


def emit_mamba_exchange(k):
    p = k.p
    gi, go = k.g_mst_in, k.g_mst_out
    p.dma("sp", lambda e: e.dma_start(
        out=gi.ap.rearrange("(cc p) n -> p cc n", p=128),
        in_=k.mhst.ap.rearrange("p (cc n) -> p cc n", cc=16)),
        reads=[k.mhst], writes=[gi], tag="mst_st1")
    p.dma("sp", lambda e: e.dma_start(out=k.g_mp_in.ap, in_=k.mdtot.ap),
          reads=[k.mdtot], writes=[k.g_mp_in], tag="mst_st2")
    p.coll(lambda e: e.collective_compute("AllGather", ALU.bypass, replica_groups=[list(range(NCORES))],
                                          ins=[gi.ap.opt()], outs=[go.ap.opt()]),
           reads=[gi], writes=[go], tag="AG_mst")
    p.coll(lambda e: e.collective_compute("AllGather", ALU.bypass, replica_groups=[list(range(NCORES))],
                                          ins=[k.g_mp_in.ap.opt()], outs=[k.g_mp_out.ap.opt()]),
           reads=[k.g_mp_in], writes=[k.g_mp_out], tag="AG_mp")
    macc = k.sub("macc", 2, 0, 16 * 64, F32)
    p.op("dve", lambda e: e.memset(macc.ap, 0.0), writes=[macc])
    p.op("dve", lambda e: e.memset(k.mhst.ap, 0.0), writes=[k.mhst])
    mrb = [k.sub(f"mrb{i}", i, 0, 16 * 64, F32) for i in range(2)]
    mrp = [k.sub(f"mrp{i}", i, 16 * 64, 16, F32) for i in range(2)]
    for j in range(NCORES):
        rb, rp = mrb[j % 2], mrp[j % 2]
        src = go.ap[j * DI:(j + 1) * DI, :].rearrange("(cc p) n -> p cc n", p=128)
        p.dma("sp", lambda e, rb=rb, src=src: e.dma_start(
            out=rb.ap.rearrange("p (cc n) -> p cc n", cc=16), in_=src), reads=[go], writes=[rb], tag=f"mrb{j}")
        srcp = k.g_mp_out.ap[j * 128:(j + 1) * 128, :]
        p.dma("sp", lambda e, rp=rp, srcp=srcp: e.dma_start(out=rp.ap, in_=srcp),
              reads=[k.g_mp_out], writes=[rp], tag=f"mrp{j}")
        p.stt("dve", k.mhst.ap, macc.ap, k.col(SELR + j), k.mhst.ap, ALU.mult, ALU.add,
              reads=[macc, k.consts, k.mhst], writes=[k.mhst], tag=f"pick{j}")
        if j < NCORES - 1:
            p.tt("dve", blkv(macc, 16), blkv(macc, 16),
                 rp.ap.unsqueeze(2).broadcast_to([128, 16, 64]), ALU.mult,
                 reads=[macc, rp], writes=[macc], tag=f"accP{j}")
            p.tt("dve", blkv(macc, 16), blkv(macc, 16), blkv(rb, 16), ALU.add,
                 reads=[macc, rb], writes=[macc], tag=f"accH{j}")


def emit_mamba(k):
    p = k.p
    alloc_mamba(k)
    p.op("dve", lambda e: e.memset(k.mhst.ap, 0.0), writes=[k.mhst])
    p.op("dve", lambda e: e.memset(k.mqcol.ap, 0.0), writes=[k.mqcol])
    p.op("dve", lambda e: e.memset(k.mdtot.ap, 1.0), writes=[k.mdtot])
    for s in range(NSEC):
        emit_mamba_section(k, s, pass2=False)
    emit_mamba_exchange(k)
    p.op("dve", lambda e: e.memset(k.mqcol.ap, 0.0), writes=[k.mqcol])
    for s in range(NSEC):
        emit_mamba_section(k, s, pass2=True)


def emit_tc(k):
    p = k.p
    x1b = [k.sub(f"x1b{c}", 4 + c // 4, (c % 4) * 2048, 2048, BF16) for c in range(8)]
    x1f = [k.sub(f"x1f{c}", 2, c * 512, 512, F32) for c in range(8)]
    tcw = k.sub("tcw", 3, 0, 8192, BF16)
    p.dma("pool", lambda e: e.dma_start(
        out=tcw.ap.rearrange("p (cc f) -> p cc f", cc=8),
        in_=k.i_tcw.ap.rearrange("(cc p) f -> p cc f", p=128)), writes=[tcw], tag="ldtcw")
    tcc = k.sbuf("tcc", [128, 4], F32)
    thc = k.col(THC)
    p.act(tcc[:, 0:1], thc, AF.Sin, scale=0.5, reads=[k.consts], writes=[tcc], tag="tcsin")
    p.tt("dve", tcc[:, 1:2], tcc[:, 0:1], tcc[:, 0:1], ALU.mult, reads=[tcc], writes=[tcc])
    p.ts("dve", tcc[:, 2:3], tcc[:, 1:2], -1.0, 1.0, ALU.mult, ALU.add, reads=[tcc], writes=[tcc])
    p.act(tcc[:, 2:3], tcc[:, 2:3], AF.Ln, reads=[tcc], writes=[tcc])
    p.act(tcc[:, 2:3], tcc[:, 2:3], AF.Exp, scale=0.5, reads=[tcc], writes=[tcc])
    p.tt("dve", tcc[:, 3:4], tcc[:, 0:1], tcc[:, 2:3], ALU.mult, reads=[tcc], writes=[tcc])
    p.ts("dve", tcc[:, 3:4], tcc[:, 3:4], 2.0, None, ALU.mult, ALU.bypass, reads=[tcc], writes=[tcc])
    p.ts("dve", tcc[:, 1:2], tcc[:, 1:2], -2.0, 1.0, ALU.mult, ALU.add, reads=[tcc], writes=[tcc])
    p.tt("dve", tcc[:, 1:2], tcc[:, 1:2], k.masks[:, 1:2], ALU.mult, reads=[tcc, k.masks], writes=[tcc])
    p.tt("dve", tcc[:, 3:4], tcc[:, 3:4], k.masks[:, 1:2], ALU.mult, reads=[tcc, k.masks], writes=[tcc])
    for ci in range(4):
        a = ci * 512
        emit_ln(k, f"ln1c{ci}",
                lambda c, aa, w, a=a: k.h[c][:, a + aa: a + aa + w], lambda c: k.h[c],
                [(lambda c, aa, w: x1f[c].ap[:, aa:aa + w], lambda c: x1f[c]),
                 (lambda c, aa, w, a=a: x1b[c].ap[:, a + aa:a + aa + w], lambda c: x1b[c])],
                1, 1e-6, width=512)
        if ci == 0 and "tc" in DBG:
            d2 = p.dram_out("dbg_x1f0", [128, 512], F32)
            d3 = p.dram_out("dbg_tcc", [128, 4], F32)
            d4 = p.dram_out("dbg_masks", [128, 4], F32)
            p.dma("sp", lambda e: e.dma_start(out=d2.ap, in_=x1f[0].ap), reads=[x1f[0]], writes=[d2])
            p.dma("sp", lambda e: e.dma_start(out=d3.ap, in_=tcc.ap), reads=[tcc], writes=[d3])
            p.dma("sp", lambda e: e.dma_start(out=d4.ap, in_=k.masks.ap), reads=[k.masks], writes=[d4])
        for f in range(8):
            pst = (k.psC, k.psD)[f % 2]
            for c in range(8):
                p.mm(pst[:, 0:512], tcw[:, c * 1024 + f * 128: c * 1024 + f * 128 + 128],
                     x1b[c].ap[:, a:a + 512], c == 0, c == 7,
                     reads=[tcw, x1b[c]], writes=[pst], tag=f"tcmm{f}")
            p.act(k.ln_t1.ap, pst[:, 0:512], AF.Tanh, bias=k.col(TCBC + f),
                  reads=[pst, k.consts], writes=[k.ln_t1], tag=f"tctanh{f}")
            p.stt("dve", k.h[f][:, a:a + 512], x1f[f].ap, tcc[:, 1:2], k.h[f][:, a:a + 512],
                  ALU.mult, ALU.add, reads=[x1f[f], tcc, k.h[f]], writes=[k.h[f]], tag=f"tchx{f}")
            p.stt("dve", k.h[f][:, a:a + 512], k.ln_t1.ap, tcc[:, 3:4], k.h[f][:, a:a + 512],
                  ALU.mult, ALU.add, reads=[k.ln_t1, tcc, k.h[f]], writes=[k.h[f]], tag=f"tcht{f}")


def build_kmax(k, tiles, dstbuf, nm, dcol):
    p = k.p
    p.op("dve", lambda e: e.memset(k.ln_t1.ap, 0.0), writes=[k.ln_t1])
    for cc in range(8):
        for tch in range(2):
            p.act(k.ln_xb.ap, tiles[cc].ap[:, tch * 512:tch * 512 + 512], AF.Square,
                  reads=[tiles[cc]], writes=[k.ln_xb], tag=f"{nm}sq{cc}")
            for h2 in range(2):
                pst = k.psC
                p.mm(pst[h2 * 64:h2 * 64 + 1, 0:512], k.onesb[h2 * 64:h2 * 64 + 64, 0:1],
                     k.ln_xb[h2 * 64:h2 * 64 + 64, 0:512], True, True,
                     reads=[k.onesb, k.ln_xb], writes=[pst], tag=f"{nm}mm{cc}{h2}")
                col = 16 + 2 * cc + h2
                p.op("dve", lambda e, pst=pst, h2=h2, col=col: e.tensor_reduce(
                    k.ln_t1[h2 * 64:h2 * 64 + 1, col:col + 1], pst[h2 * 64:h2 * 64 + 1, 0:512], AX.X, ALU.max),
                    reads=[pst], writes=[k.ln_t1], tag=f"{nm}rd{cc}{h2}")
                p.tt("dve", k.ln_t1[h2 * 64:h2 * 64 + 1, 2 * cc + h2:2 * cc + h2 + 1],
                     k.ln_t1[h2 * 64:h2 * 64 + 1, 2 * cc + h2:2 * cc + h2 + 1],
                     k.ln_t1[h2 * 64:h2 * 64 + 1, col:col + 1], ALU.max,
                     reads=[k.ln_t1], writes=[k.ln_t1], tag=f"{nm}mx{cc}{h2}")
    pst = k.psD
    p.tr(pst[0:16, 0:128], k.ln_t1[:, 0:16], k.ident.ap, reads=[k.ln_t1, k.ident], writes=[pst], tag=f"{nm}tr")
    fv = k.ln_xb.ap.bitcast(F32)
    p.copy("dve", fv[0:16, 0:2], pst[0:16, 0:128:64], reads=[pst], writes=[k.ln_xb], tag=f"{nm}cp")
    p.op("dve", lambda e: e.tensor_reduce(fv[0:16, 2:3], fv[0:16, 0:2], AX.X, ALU.max),
         reads=[k.ln_xb], writes=[k.ln_xb], tag=f"{nm}rd2")
    p.tr(pst[0:1, 0:16], fv[0:16, 2:3], k.ident[0:16, 0:16], reads=[k.ln_xb, k.ident], writes=[pst], tag=f"{nm}tr2")
    p.copy("dve", dstbuf[0:1, dcol:dcol + 16], pst[0:1, 0:16], reads=[pst], writes=[dstbuf], tag=f"{nm}fin")


def emit_wlam(k):
    p = k.p
    INV = float(np.float32(0.7071067811865476))
    abf = [k.sub(f"abf{c}", 4, c * 1024, 1024, BF16) for c in range(8)]
    qbf = [k.sub(f"qbf{c}", 5, c * 1024, 1024, BF16) for c in range(8)]
    ds = [k.sub(f"wds{c}", c // 4, (c % 4) * 1024, 1024, F32) for c in range(8)]
    x2f = [k.sub(f"x2f{c}", 2, c * 512, 512, F32) for c in range(8)]
    kfm = [k.sub(f"kfm{c}", 3, c * 1024, 1024, BF16) for c in range(8)]
    for ci in range(4):
        a = ci * 512
        pa = ci * 256
        emit_ln(k, f"ln2c{ci}",
                lambda c, aa, w, a=a: k.h[c][:, a + aa: a + aa + w], lambda c: k.h[c],
                [(lambda c, aa, w: x2f[c].ap[:, aa:aa + w], lambda c: x2f[c])],
                2, 1e-6, width=512)
        for c in range(8):
            ev = x2f[c].ap[:, 0:512:2]
            od = x2f[c].ap[:, 1:512:2]
            p.tt("dve", k.ln_t1[:, 0:256], ev, od, ALU.add, reads=[x2f[c]], writes=[k.ln_t1], tag=f"pa{c}")
            p.ts("dve", abf[c].ap[:, pa:pa + 256], k.ln_t1[:, 0:256], INV, None, ALU.mult, ALU.bypass,
                 reads=[k.ln_t1], writes=[abf[c]], tag=f"pab{c}")
            p.tt("dve", k.ln_t1[:, 0:256], ev, od, ALU.subtract, reads=[x2f[c]], writes=[k.ln_t1], tag=f"pd{c}")
            p.ts("dve", ds[c].ap[:, pa:pa + 256], k.ln_t1[:, 0:256], 0.5, None, ALU.mult, ALU.bypass,
                 reads=[k.ln_t1], writes=[ds[c]], tag=f"pdb{c}")

    def prj(widram, dsts, tagp):
        for f in range(8):
            wb = k.wst[f % 2]
            src = widram.ap[:, 128 * f:128 * (f + 1)].rearrange("(cc p) q -> p cc q", p=128)
            p.dma("pool", lambda e, wb=wb, src=src: e.dma_start(
                out=wb.ap[:, 0:1024].rearrange("p (cc q) -> p cc q", cc=8), in_=src),
                writes=[wb], tag=f"ld{tagp}{f}")
            for tch in range(2):
                pst = (k.psC, k.psD)[tch]
                for c in range(8):
                    p.mm(pst[:, 0:512], wb[:, c * 128:(c + 1) * 128], abf[c].ap[:, tch * 512:tch * 512 + 512],
                         c == 0, c == 7, reads=[wb, abf[c]], writes=[pst], tag=f"{tagp}mm{f}")
                p.copy("dve", dsts[f].ap[:, tch * 512:tch * 512 + 512], pst[:, 0:512],
                       reads=[pst], writes=[dsts[f]], tag=f"{tagp}ev{f}")

    prj(k.i_wq, qbf, "wq")
    prj(k.i_wk, kfm, "wk")
    kmax = None
    build_kmax(k, kfm, k.rows, "kn", 272)
    for c in range(8):
        p.dma("sp", lambda e, c=c: e.dma_start(
            out=k.g_kv_in.ap[c * 128:(c + 1) * 128, 0:1024], in_=kfm[c].ap),
            reads=[kfm[c]], writes=[k.g_kv_in], tag=f"stk{c}")
    p.dma("pool", lambda e: e.dma_start(out=k.g_kv_in.ap[2 * D:2 * D + 1, 0:16], in_=k.rows[0:1, 272:288]),
          reads=[k.rows], writes=[k.g_kv_in], tag="stkn")
    prj(k.i_wv, kfm, "wv")
    vtm = [k.sub(f"vtm{c}", 4, c * 1040, 1040, BF16) for c in range(8)]
    for i in range(8):
        for c in range(8):
            p.tr(k.psT[:, c * 128:(c + 1) * 128], kfm[c].ap[:, i * 128:(i + 1) * 128], k.identb.ap,
                 reads=[kfm[c], k.identb], writes=[k.psT], tag=f"vtr{i}_{c}")
        p.op("dve", lambda e, i=i: e.memset(vtm[i].ap.rearrange("p (hh u) -> p hh u", u=65)[:, :, 64:65], 1.0),
             writes=[vtm[i]], tag=f"vone{i}")
        p.copy("dve", vtm[i].ap.rearrange("p (hh u) -> p hh u", u=65)[:, :, 0:64],
               k.psT[:, 0:1024].rearrange("p (hh d) -> p hh d", d=64),
               reads=[k.psT], writes=[vtm[i]], tag=f"vte{i}")
        p.dma("sp", lambda e, i=i: e.dma_start(
            out=k.g_kv_in.ap[D + i * 128: D + (i + 1) * 128, :], in_=vtm[i].ap),
            reads=[vtm[i]], writes=[k.g_kv_in], tag=f"stv{i}")
    gi, go = k.g_kv_in, k.g_kv_out
    p.coll(lambda e: e.collective_compute("AllGather", ALU.bypass, replica_groups=[list(range(NCORES))],
                                          ins=[gi.ap.opt()], outs=[go.ap.opt()]),
           reads=[gi], writes=[go], tag="AG_kv")
    build_kmax(k, qbf, k.rows, "qn", 288)
    qmax = None
    kng = None
    KST = 2 * D + 16
    for r in range(NCORES):
        p.dma("pool", lambda e, r=r: e.dma_start(
            out=k.rows[0:1, r * 16:(r + 1) * 16], in_=go.ap[r * KST + 2 * D: r * KST + 2 * D + 1, 0:16]),
            reads=[go], writes=[k.rows], tag=f"ldkn{r}")
    kngm = None
    kngm_ap = k.rows[0:1, 256:272]
    p.op("dve", lambda e: e.tensor_reduce(
        kngm_ap, k.rows[0:1, 0:128].rearrange("one (r hh) -> one hh r", r=NCORES), AX.X, ALU.max),
        reads=[k.rows], writes=[k.rows], tag="knmax")
    p.tt("dve", kngm_ap, kngm_ap, k.rows[0:1, 288:304], ALU.mult, reads=[k.rows], writes=[k.rows])
    p.act(kngm_ap, kngm_ap, AF.Ln, reads=[k.rows], writes=[k.rows])
    p.act(kngm_ap, kngm_ap, AF.Exp, scale=0.5, reads=[k.rows], writes=[k.rows])
    p.ts("dve", kngm_ap, kngm_ap, 1.01 / 8.0, -40.0, ALU.mult, ALU.add, reads=[k.rows], writes=[k.rows])
    p.ts("dve", kngm_ap, kngm_ap, 0.0, -1.0, ALU.max, ALU.mult, reads=[k.rows], writes=[k.rows])
    biasr = k.sbuf("biasr", [128, 16], F32)
    p.mm(k.psD[:, 0:16], k.onesf[0:1, 0:128], kngm_ap, True, True, reads=[k.onesf, k.rows], writes=[k.psD], tag="biasbc")
    p.copy("dve", biasr.ap, k.psD[:, 0:16], reads=[k.psD], writes=[biasr])
    ksb = k.sub("ksb", 3, 0, 8192, BF16)
    pt = [k.sub(f"pt{i}", 2, i * 2048, 2048, BF16) for i in range(2)]
    vt = [k.sub(f"vt{i}", 2, 4096 + i * 1040, 1040, BF16) for i in range(2)]
    ofm = vtm
    for cc in range(8):
        for r in range(NCORES):
            p.dma("sp", lambda e, cc=cc, r=r: e.dma_start(
                out=ksb.ap[:, r * 1024:(r + 1) * 1024],
                in_=go.ap[r * KST + cc * 128: r * KST + (cc + 1) * 128, 0:1024]),
                reads=[go], writes=[ksb], tag=f"ldK{cc}_{r}")
        for tt2 in range(64):
            vti = vt[tt2 % 2]
            r, loc = tt2 // 8, tt2 % 8
            p.dma("sp", lambda e, vti=vti, r=r, loc=loc: e.dma_start(
                out=vti.ap, in_=go.ap[r * KST + D + loc * 128: r * KST + D + (loc + 1) * 128, :]),
                reads=[go], writes=[vti], tag=f"ldV{cc}_{tt2}")
            for h2 in range(2):
                h = 2 * cc + h2
                pss = k.psB
                ptb = pt[h2]
                for tch in range(2):
                    p.mm(pss[:, tch * 512:tch * 512 + 512],
                         ksb.ap[64 * h2:64 * h2 + 64, tt2 * 128:(tt2 + 1) * 128],
                         qbf[cc].ap[64 * h2:64 * h2 + 64, tch * 512:tch * 512 + 512], True, True,
                         reads=[ksb, qbf[cc]], writes=[pss], tag=f"st{h}_{tt2}_{tch}")
                p.act(ptb.ap[:, 0:1024], pss.ap, AF.Exp, scale=0.125, bias=biasr[:, h:h + 1],
                      reads=[pss, biasr], writes=[ptb], tag=f"exp{h}_{tt2}")
                for tch in range(2):
                    acc = (k.psC, k.psD)[tch] if h2 == 0 else k.psA
                    reg = acc[0:65, 0:512] if h2 == 0 else acc[0:65, tch * 512:tch * 512 + 512]
                    p.mm(reg, vti.ap[:, h * 65:(h + 1) * 65],
                         ptb.ap[:, tch * 512:tch * 512 + 512], tt2 == 0, tt2 == 63,
                         reads=[vti, ptb], writes=[acc], tag=f"av{h}_{tt2}_{tch}")
        for h2 in range(2):
            h = 2 * cc + h2
            for tch in range(2):
                accp = (k.psC, k.psD)[tch] if h2 == 0 else k.psA
                reg = accp[0:65, 0:512] if h2 == 0 else accp[0:65, tch * 512:tch * 512 + 512]
                rrow = accp[64:65, 0:512] if h2 == 0 else accp[64:65, tch * 512:tch * 512 + 512]
                p.op("dve", lambda e, rrow=rrow: e.reciprocal(k.ln_t1[64:65, 0:512], rrow),
                     reads=[accp], writes=[k.ln_t1], tag=f"rd{h}{tch}")
                p.mm(k.psB[0:64, 0:512], k.onesf[64:65, 0:64], k.ln_t1[64:65, 0:512], True, True,
                     reads=[k.onesf, k.ln_t1], writes=[k.psB], tag=f"rdb{h}{tch}")
                p.copy("dve", k.ln_t1[0:64, 0:512], k.psB[0:64, 0:512], reads=[k.psB], writes=[k.ln_t1],
                       tag=f"rde{h}{tch}")
                rr = accp[0:64, 0:512] if h2 == 0 else accp[0:64, tch * 512:tch * 512 + 512]
                p.tt("dve", ofm[cc].ap[h2 * 64:h2 * 64 + 64, tch * 512:tch * 512 + 512],
                     rr, k.ln_t1[0:64, 0:512], ALU.mult,
                     reads=[accp, k.ln_t1], writes=[ofm[cc]], tag=f"onorm{h}{tch}")
    ywloc = [(2, 0), (2, 2048), (3, 0), (3, 2048), (0, 0), (0, 2048), (1, 0), (1, 2048)]
    yw = [k.sub(f"yw{c}", ywloc[c][0], ywloc[c][1], 2048, F32) for c in range(8)]
    ds7v = SubBuf("ds7v", k.wst[1], 0, 1024, F32)
    p.copy("dve", ds7v.ap, ds[7].ap, reads=[ds[7]], writes=[k.wst[1]], tag="ds7cp")
    dsrc = ds[:7] + [ds7v]
    for f in range(8):
        wb = k.wst[0]
        src = k.i_wo.ap[:, 128 * f:128 * (f + 1)].rearrange("(cc p) q -> p cc q", p=128)
        p.dma("pool", lambda e, wb=wb, src=src: e.dma_start(
            out=wb.ap[:, 0:1024].rearrange("p (cc q) -> p cc q", cc=8), in_=src),
            writes=[wb], tag=f"ldwo2{f}")
        for tch in range(2):
            pst = (k.psC, k.psD)[tch]
            for c in range(8):
                p.mm(pst[:, 0:512], wb[:, c * 128:(c + 1) * 128], ofm[c].ap[:, tch * 512:tch * 512 + 512],
                     c == 0, c == 7, reads=[wb, ofm[c]], writes=[pst], tag=f"womm{f}")
            p.ts("dve", k.ln_t1.ap, pst[:, 0:512], INV, None, ALU.mult, ALU.bypass,
                 reads=[pst], writes=[k.ln_t1], tag=f"o2i{f}")
            p.tt("dve", yw[f].ap[:, tch * 1024:tch * 1024 + 1024:2], k.ln_t1.ap,
                 dsrc[f].ap[:, tch * 512:tch * 512 + 512], ALU.add,
                 reads=[k.ln_t1, dsrc[f]], writes=[yw[f]], tag=f"ywe{f}")
            p.tt("dve", yw[f].ap[:, tch * 1024 + 1:tch * 1024 + 1024:2], k.ln_t1.ap,
                 dsrc[f].ap[:, tch * 512:tch * 512 + 512], ALU.subtract,
                 reads=[k.ln_t1, dsrc[f]], writes=[yw[f]], tag=f"ywo{f}")
    for c in range(8):
        p.copy("dve", k.ln_t1[:, 0:1], yw[c].ap[:, 0:1], reads=[yw[c]], writes=[k.ln_t1])
        p.copy("dve", k.ln_t1[:, 1:3], yw[c].ap[:, 2046:2048], reads=[yw[c]], writes=[k.ln_t1])
        p.dma("sp", lambda e, c=c: e.dma_start(out=k.g_yw_in.ap[c * 128:(c + 1) * 128, 0:3],
                                               in_=k.ln_t1[:, 0:3]),
              reads=[k.ln_t1], writes=[k.g_yw_in], tag=f"styw{c}")
    p.coll(lambda e: e.collective_compute("AllGather", ALU.bypass, replica_groups=[list(range(NCORES))],
                                          ins=[k.g_yw_in.ap.opt()], outs=[k.g_yw_out.ap.opt()]),
           reads=[k.g_yw_in], writes=[k.g_yw_out], tag="AG_yw")
    ywh = k.sub("ywh", 4, 0, 256, F32)
    p.dma("sp", lambda e: e.dma_start(
        out=ywh.ap.rearrange("p (r cc f) -> p r cc f", r=8, cc=8),
        in_=k.g_yw_out.ap.rearrange("(r cc p) f -> p r cc f", r=8, p=128)),
        reads=[k.g_yw_out], writes=[ywh], tag="ldywh")
    hal = k.sub("hal", 4, 256, 24, F32)
    p.op("dve", lambda e: e.memset(hal.ap, 0.0), writes=[hal])
    hv = hal.ap.rearrange("p (cc u) -> p cc u", cc=8)
    yv = ywh.ap.rearrange("p (r cc f) -> p r cc f", r=8, cc=8)
    for r in range(NCORES):
        p.stt("dve", hv[:, :, 0:2], yv[:, r, :, 1:3], k.col(SELP + r), hv[:, :, 0:2],
              ALU.mult, ALU.add, reads=[ywh, k.consts, hal], writes=[hal], tag=f"halL{r}")
        p.stt("dve", hv[:, :, 2:3], yv[:, r, :, 0:1], k.col(SELN + r), hv[:, :, 2:3],
              ALU.mult, ALU.add, reads=[ywh, k.consts, hal], writes=[hal], tag=f"halR{r}")
    wmc = k.sbuf("wmc", [128, 32], F32)
    p.tt("dve", wmc.ap, k.consts[:, WCVC:WCVC + 32], k.masks[:, 2:3].broadcast_to([128, 32]),
         ALU.mult, reads=[k.consts, k.masks], writes=[wmc], tag="wmc")
    for c in range(8):
        for i in range(4):
            sh = i - 2
            lo = max(0, -sh)
            hi = min(T, T - sh)
            p.stt("dve", k.h[c][:, lo:hi], yw[c].ap[:, lo + sh:hi + sh], wmc[:, i * 8 + c:i * 8 + c + 1],
                  k.h[c][:, lo:hi], ALU.mult, ALU.add, reads=[yw[c], wmc, k.h[c]], writes=[k.h[c]],
                  tag=f"cvw{i}_{c}")
        p.stt("dve", k.h[c][:, 0:1], hv[:, c, 0:1], wmc[:, 0 * 8 + c:0 * 8 + c + 1], k.h[c][:, 0:1],
              ALU.mult, ALU.add, reads=[hal, wmc, k.h[c]], writes=[k.h[c]], tag=f"fxa{c}")
        p.stt("dve", k.h[c][:, 0:1], hv[:, c, 1:2], wmc[:, 1 * 8 + c:1 * 8 + c + 1], k.h[c][:, 0:1],
              ALU.mult, ALU.add, reads=[hal, wmc, k.h[c]], writes=[k.h[c]], tag=f"fxb{c}")
        p.stt("dve", k.h[c][:, 1:2], hv[:, c, 1:2], wmc[:, 0 * 8 + c:0 * 8 + c + 1], k.h[c][:, 1:2],
              ALU.mult, ALU.add, reads=[hal, wmc, k.h[c]], writes=[k.h[c]], tag=f"fxc{c}")
        p.stt("dve", k.h[c][:, T - 1:T], hv[:, c, 2:3], wmc[:, 3 * 8 + c:3 * 8 + c + 1], k.h[c][:, T - 1:T],
              ALU.mult, ALU.add, reads=[hal, wmc, k.h[c]], writes=[k.h[c]], tag=f"fxd{c}")


def emit_moe(k):
    p = k.p
    x3b = [k.sub(f"x3b{c}", 4 + c // 4, (c % 4) * 2048, 2048, BF16) for c in range(8)]
    x3f = [k.sub(f"x3f{c}", 2, c * 512, 512, F32) for c in range(8)]
    gtm = k.sub("gtm", 3, 1664, 128, F32)
    wgsb = k.sub("wgsb", 3, 1536, 64, F32)
    gsc = k.sub("gsc", 3, 1600, 8, F32)
    p.dma("sp", lambda e: e.dma_start(
        out=wgsb.ap.rearrange("p (cc e) -> p cc e", cc=8),
        in_=k.i_wg.ap.rearrange("(cc p) e -> p cc e", p=128)), writes=[wgsb], tag="ldwg")
    for ci in range(4):
        a = ci * 512
        emit_ln(k, f"ln3c{ci}",
                lambda c, aa, w, a=a: k.h[c][:, a + aa: a + aa + w], lambda c: k.h[c],
                [(lambda c, aa, w: x3f[c].ap[:, aa:aa + w], lambda c: x3f[c]),
                 (lambda c, aa, w, a=a: x3b[c].ap[:, a + aa:a + aa + w], lambda c: x3b[c])],
                3, 1e-6, width=512)
        for c in range(8):
            p.mm(k.psD[0:8, 0:512], wgsb[:, c * 8:(c + 1) * 8], x3f[c].ap, c == 0, c == 7,
                 reads=[wgsb, x3f[c]], writes=[k.psD], tag=f"lgmm{ci}")
        p.copy("dve", k.ln_t1[0:8, 0:512], k.psD[0:8, 0:512], reads=[k.psD], writes=[k.ln_t1], tag=f"lgev{ci}")
        for i in range(4):
            tl = ci * 4 + i
            p.tr(k.psC[0:128, 0:8], k.ln_t1[0:8, i * 128:(i + 1) * 128], k.ident[0:8, 0:8],
                 reads=[k.ln_t1, k.ident], writes=[k.psC], tag=f"lgtr{tl}")
            p.copy("dve", gtm[:, tl * 8:(tl + 1) * 8], k.psC[0:128, 0:8], reads=[k.psC], writes=[gtm],
                   tag=f"lgtm{tl}")
    fv = k.ln_xb.ap.bitcast(F32)
    for tl in range(16):
        g8 = gtm[:, tl * 8:(tl + 1) * 8]
        p.op("dve", lambda e, g8=g8: e.tensor_reduce(gsc[:, 0:1], g8, AX.X, ALU.max),
             reads=[gtm], writes=[gsc], tag=f"g1_{tl}")
        p.op("dve", lambda e, g8=g8: e.tensor_scalar(fv[:, 0:8], g8, gsc[:, 0:1], None,
                                                     ALU.is_equal, ALU.bypass),
             reads=[gtm, gsc], writes=[k.ln_xb], tag=f"geq{tl}")
        p.stt("dve", fv[:, 8:16], fv[:, 0:8], -1e30, g8, ALU.mult, ALU.add,
              reads=[k.ln_xb, gtm], writes=[k.ln_xb], tag=f"glm{tl}")
        p.op("dve", lambda e: e.tensor_reduce(gsc[:, 1:2], fv[:, 8:16], AX.X, ALU.max),
             reads=[k.ln_xb], writes=[gsc], tag=f"g2_{tl}")
        p.op("dve", lambda e, g8=g8: e.tensor_scalar(fv[:, 0:8], g8, gsc[:, 1:2], None,
                                                     ALU.is_ge, ALU.bypass),
             reads=[gtm, gsc], writes=[k.ln_xb], tag=f"gge{tl}")
        p.op("dve", lambda e, g8=g8: e.tensor_scalar(fv[:, 8:16], g8, gsc[:, 0:1], None,
                                                     ALU.subtract, ALU.bypass),
             reads=[gtm, gsc], writes=[k.ln_xb], tag=f"gsh{tl}")
        p.act(fv[:, 8:16], fv[:, 8:16], AF.Exp, reads=[k.ln_xb], writes=[k.ln_xb], tag=f"gexp{tl}")
        p.tt("dve", gsc[:, 2:3], gsc[:, 1:2], gsc[:, 0:1], ALU.subtract, reads=[gsc], writes=[gsc])
        p.act(gsc[:, 2:3], gsc[:, 2:3], AF.Exp, reads=[gsc], writes=[gsc], tag=f"gden{tl}")
        p.ts("dve", gsc[:, 2:3], gsc[:, 2:3], 1.0, None, ALU.add, ALU.bypass, reads=[gsc], writes=[gsc])
        p.op("dve", lambda e: e.reciprocal(gsc[:, 2:3], gsc[:, 2:3]), reads=[gsc], writes=[gsc])
        p.tt("dve", gsc[:, 2:3], gsc[:, 2:3], k.masks[:, 3:4], ALU.mult, reads=[gsc, k.masks], writes=[gsc])
        p.tt("dve", fv[:, 0:8], fv[:, 0:8], fv[:, 8:16], ALU.mult, reads=[k.ln_xb], writes=[k.ln_xb],
             tag=f"gm1_{tl}")
        p.op("dve", lambda e, g8=g8: e.tensor_scalar(g8, fv[:, 0:8], gsc[:, 2:3], None,
                                                     ALU.mult, ALU.bypass),
             reads=[k.ln_xb, gsc], writes=[gtm], tag=f"gfin{tl}")
    g8r = k.sub("g8r", 2, 0, 2048, BF16)
    for tl in range(16):
        p.tr(k.psC[0:8, 0:128], gtm[:, tl * 8:(tl + 1) * 8], k.ident.ap,
             reads=[gtm, k.ident], writes=[k.psC], tag=f"g8tr{tl}")
        p.copy("dve", g8r.ap[0:8, tl * 128:(tl + 1) * 128], k.psC[0:8, 0:128],
               reads=[k.psC], writes=[g8r], tag=f"g8ev{tl}")
    grep = k.sub("grep", 2, 1024, 2048, F32)
    h1g = [k.sub("h1gA", 0, 0, 8192, BF16), k.sub("h1gB", 1, 0, 8192, BF16)]
    gelw = [k.sub(f"gelw{i}", 3, i * 1024, 1024, F32) for i in range(3)]
    b2sb = k.sub("b2sb", 3, 7168, 1024, BF16)
    k.ohot8 = k.sub("ohot8", 3, 6144, 1024, BF16)
    p.dma("pool", lambda e: e.dma_start(out=k.ohot8.ap[0:8, :], in_=k.i_ohot.ap), writes=[k.ohot8], tag="ldoh")
    p.dma("pool", lambda e: e.dma_start(out=b2sb.ap[0:8, :], in_=k.i_b2r.ap), writes=[b2sb], tag="ldb2")
    for e_ in range(E):
        for ch in range(4):
            pst = (k.psC, k.psD)[ch % 2]
            p.mm(pst[:, 0:512], k.ohot8.ap[0:8, e_ * 128:(e_ + 1) * 128],
                 g8r.ap[0:8, ch * 512:(ch + 1) * 512], True, True,
                 reads=[k.ohot8, g8r], writes=[pst], tag=f"grbc{e_}_{ch}")
            p.copy("dve", grep.ap[:, ch * 512:(ch + 1) * 512], pst[:, 0:512], reads=[pst], writes=[grep],
                   tag=f"grev{e_}_{ch}")
        for fq in range(4):
            for fg in range(4):
                wb = k.wst[fg % 2]
                c0 = fq * 1024 + fg * 256
                src = k.i_w1.ap[e_ * D:(e_ + 1) * D, c0:c0 + 256].rearrange("(cc p) q -> p cc q", p=128)
                p.dma("pool", lambda e, wb=wb, src=src: e.dma_start(
                    out=wb.ap.rearrange("p (cc q) -> p cc q", cc=8), in_=src),
                    writes=[wb], tag=f"ldw1_{e_}_{fq}{fg}")
                for fo in range(2):
                    ftq = fg * 2 + fo
                    b1c = k.col(MB1C + e_ * 32 + fq * 8 + ftq)
                    for tph in range(2):
                        pst = (k.psA, k.psB)[(2 * ftq + tph) % 2]
                        for tchh in range(2):
                            tch = tph * 2 + tchh
                            reg = pst[:, tchh * 512:tchh * 512 + 512]
                            for c in range(8):
                                p.mm(reg, wb[:, c * 256 + fo * 128:c * 256 + fo * 128 + 128],
                                     x3b[c].ap[:, tch * 512:(tch + 1) * 512], c == 0, c == 7,
                                     reads=[wb, x3b[c]], writes=[pst], tag=f"o1_{e_}_{fq}{ftq}_{tch}")
                        wide = pst[:, 0:1024]
                        p.act(gelw[0].ap, wide, AF.Identity, bias=b1c,
                              reads=[pst, k.consts], writes=[gelw[0]], tag=f"gx{e_}_{fq}{ftq}_{tph}")
                        p.act(gelw[1].ap, wide, AF.Square, bias=b1c, reads=[pst, k.consts], writes=[gelw[1]],
                              tag=f"ga{e_}_{fq}{ftq}_{tph}")
                        p.ts("dve", gelw[1].ap, gelw[1].ap, GELU_C, 1.0, ALU.mult, ALU.add,
                             reads=[gelw[1]], writes=[gelw[1]], tag=f"gb{e_}")
                        p.tt("dve", gelw[2].ap, gelw[1].ap, gelw[0].ap, ALU.mult,
                             reads=[gelw[1], gelw[0]], writes=[gelw[2]], tag=f"gc{e_}")
                        p.act(gelw[2].ap, gelw[2].ap, AF.Sigmoid, scale=GELU_S,
                              reads=[gelw[2]], writes=[gelw[2]], tag=f"gs{e_}")
                        p.tt("dve", gelw[0].ap, gelw[2].ap, gelw[0].ap, ALU.mult,
                             reads=[gelw[2], gelw[0]], writes=[gelw[0]], tag=f"gg{e_}")
                        h1gd = h1g[ftq // 4].ap[:, (ftq % 4) * 2048 + tph * 1024:(ftq % 4) * 2048 + tph * 1024 + 1024]
                        p.tt("dve", h1gd, gelw[0].ap, grep.ap[:, tph * 1024:(tph + 1) * 1024], ALU.mult,
                             reads=[gelw[0], grep], writes=[h1g[ftq // 4]], tag=f"hg{e_}_{fq}{ftq}_{tph}")
            for f2 in range(8):
                wb = k.wst[f2 % 2]
                src = k.i_w2.ap[e_ * FF + fq * 1024: e_ * FF + (fq + 1) * 1024,
                                128 * f2:128 * (f2 + 1)].rearrange("(cc p) q -> p cc q", p=128)
                p.dma("pool", lambda e, wb=wb, src=src: e.dma_start(
                    out=wb.ap[:, 0:1024].rearrange("p (cc q) -> p cc q", cc=8), in_=src),
                    writes=[wb], tag=f"ldw2_{e_}_{fq}{f2}")
                for tch in range(4):
                    pst = (k.psC, k.psD)[f2 % 2]
                    for cq in range(8):
                        h1s = h1g[cq // 4].ap[:, (cq % 4) * 2048 + tch * 512:(cq % 4) * 2048 + tch * 512 + 512]
                        p.mm(pst[:, 0:512], wb[:, cq * 128:(cq + 1) * 128], h1s,
                             cq == 0, cq == 7 and fq < 3, reads=[wb, h1g[cq // 4]], writes=[pst],
                             tag=f"o2_{e_}_{fq}{f2}_{tch}")
                    if fq == 3:
                        p.mm(pst[:, 0:512], b2sb[0:8, f2 * 128:(f2 + 1) * 128],
                             g8r.ap[0:8, tch * 512:(tch + 1) * 512], False, True,
                             reads=[b2sb, g8r], writes=[pst], tag=f"b2_{e_}_{f2}_{tch}")
                    p.tt("dve", k.h[f2][:, tch * 512:(tch + 1) * 512], k.h[f2][:, tch * 512:(tch + 1) * 512],
                         pst[:, 0:512], ALU.add, reads=[k.h[f2], pst], writes=[k.h[f2]],
                         tag=f"hadd{e_}_{fq}{f2}_{tch}")


def emit_outer(k):
    p = k.p
    xtm = [k.sub(f"otm{i}", i, 0, 1024, F32) for i in range(2)]
    for i in range(16):
        b = xtm[i % 2]
        src = k.i_x[3 + 128 * i: 3 + 128 * (i + 1), :]
        p.dma("sp", lambda e, b=b, src=src: e.dma_start(out=b.ap, in_=src), writes=[b], tag=f"rlx{i}")
        ps = (k.psA, k.psB)[i % 2]
        for c in range(8):
            p.tr(ps[:, 128 * c:128 * (c + 1)], b.ap[:, 128 * c:128 * (c + 1)], k.ident.ap,
                 reads=[b, k.ident], writes=[ps], tag=f"rtr{i}_{c}")
        for c in range(8):
            p.tt("dve", k.h[c][:, 128 * i:128 * (i + 1)], k.h[c][:, 128 * i:128 * (i + 1)],
                 ps[:, 128 * c:128 * (c + 1)], ALU.add, reads=[k.h[c], ps], writes=[k.h[c]],
                 tag=f"radd{i}_{c}")
    for ci in range(4):
        a = ci * 512
        emit_ln(k, f"ln4c{ci}",
                lambda c, aa, w, a=a: k.h[c][:, a + aa: a + aa + w], lambda c: k.h[c],
                [(lambda c, aa, w, a=a: k.h[c][:, a + aa: a + aa + w], lambda c: k.h[c])],
                None, 1e-5, width=512)


def emit_ffn(k):
    p = k.p
    xpl = k.sbuf("xpl", [128, 8], F32)
    for c in range(8):
        p.copy("dve", xpl[:, c:c + 1], k.h[c][:, T - 1:T], reads=[k.h[c]], writes=[xpl], tag=f"xpl{c}")
    p.dma("sp", lambda e: e.dma_start(out=k.g_xp_in.ap, in_=xpl.ap), reads=[xpl], writes=[k.g_xp_in], tag="stxp")
    p.coll(lambda e: e.collective_compute("AllGather", ALU.bypass, replica_groups=[list(range(NCORES))],
                                          ins=[k.g_xp_in.ap.opt()], outs=[k.g_xp_out.ap.opt()]),
           reads=[k.g_xp_in], writes=[k.g_xp_out], tag="AG_xp")
    xpg = k.sbuf("xpg", [128, 64], F32)
    p.dma("sp", lambda e: e.dma_start(
        out=xpg.ap.rearrange("p (r cc) -> p r cc", r=8),
        in_=k.g_xp_out.ap.rearrange("(r p) cc -> p r cc", p=128)),
        reads=[k.g_xp_out], writes=[xpg], tag="ldxp")
    xpb = k.sbuf("xpb", [128, 8], F32)
    p.op("dve", lambda e: e.memset(xpb.ap, 0.0), writes=[xpb])
    for r in range(NCORES):
        p.stt("dve", xpb.ap, xpg.ap.rearrange("p (r cc) -> p r cc", r=8)[:, r, :], k.col(SELP + r),
              xpb.ap, ALU.mult, ALU.add, reads=[xpg, k.consts, xpb], writes=[xpb], tag=f"xpk{r}")
    xrb = [k.sub(f"xrb{c}", 4 + c // 4, (c % 4) * 2048, 2048, BF16) for c in range(8)]
    xkb = [k.sub(f"xkb{c}", c // 4, (c % 4) * 2048, 2048, BF16) for c in range(8)]
    dsc = k.sub("ffd", 2, 0, 2048, F32)
    for c in range(8):
        p.tt("dve", dsc.ap[:, 1:2048], k.h[c][:, 1:2048], k.h[c][:, 0:2047], ALU.subtract,
             reads=[k.h[c]], writes=[dsc], tag=f"fd{c}")
        p.tt("dve", dsc.ap[:, 0:1], k.h[c][:, 0:1], xpb[:, c:c + 1], ALU.subtract,
             reads=[k.h[c], xpb], writes=[dsc], tag=f"fd0{c}")
        p.stt("dve", xrb[c].ap[:, 1:2048], dsc.ap[:, 1:2048], k.col(MURC + c), k.h[c][:, 0:2047],
              ALU.mult, ALU.add, reads=[dsc, k.consts, k.h[c]], writes=[xrb[c]], tag=f"fr{c}")
        p.stt("dve", xkb[c].ap[:, 1:2048], dsc.ap[:, 1:2048], k.col(MUKC + c), k.h[c][:, 0:2047],
              ALU.mult, ALU.add, reads=[dsc, k.consts, k.h[c]], writes=[xkb[c]], tag=f"fk{c}")
        p.stt("dve", xrb[c].ap[:, 0:1], dsc.ap[:, 0:1], k.col(MURC + c), xpb[:, c:c + 1],
              ALU.mult, ALU.add, reads=[dsc, k.consts, xpb], writes=[xrb[c]], tag=f"fr0{c}")
        p.stt("dve", xkb[c].ap[:, 0:1], dsc.ap[:, 0:1], k.col(MUKC + c), xpb[:, c:c + 1],
              ALU.mult, ALU.add, reads=[dsc, k.consts, xpb], writes=[xkb[c]], tag=f"fk0{c}")
    wT = k.sub("ffwT", 3, 0, 8192, BF16)

    def build_wT(widram, nm):
        for di in range(8):
            wb = k.wst[di % 2]
            src = widram.ap[128 * di:128 * (di + 1), :].rearrange("p (cc q) -> p cc q", cc=8)
            p.dma("pool", lambda e, wb=wb, src=src: e.dma_start(
                out=wb.ap[:, 0:1024].rearrange("p (cc q) -> p cc q", cc=8), in_=src),
                writes=[wb], tag=f"ld{nm}{di}")
            for cj in range(8):
                p.tr(k.psT[:, cj * 128:(cj + 1) * 128], wb[:, cj * 128:(cj + 1) * 128], k.identb.ap,
                     reads=[wb, k.identb], writes=[k.psT], tag=f"{nm}tr{di}_{cj}")
            p.copy("dve", wT.ap.rearrange("p (cj di q) -> p cj di q", cj=8, di=8)[:, :, di, :],
                   k.psT[:, 0:1024].rearrange("p (cj q) -> p cj q", cj=8),
                   reads=[k.psT], writes=[wT], tag=f"{nm}ev{di}")

    def ffn_mm(xin, di, tch, pst):
        for cj in range(8):
            p.mm(pst[:, 0:512], wT.ap[:, (cj * 8 + di) * 128:(cj * 8 + di) * 128 + 128],
                 xin[cj].ap[:, tch * 512:(tch + 1) * 512], cj == 0, cj == 7,
                 reads=[wT, xin[cj]], writes=[pst], tag=f"fmm{di}_{tch}")

    rq = [k.sub(f"ffrq{d}", 2, d * 512, 512, BF16) for d in range(8)]
    kq = [k.sub(f"ffkq{d}", 2, 4096 + d * 512, 512, BF16) for d in range(8)]
    for tch in range(4):
        build_wT(k.i_wr, f"wr{tch}")
        for di in range(8):
            pst = (k.psC, k.psD)[di % 2]
            ffn_mm(xrb, di, tch, pst)
            p.act(rq[di].ap, pst[:, 0:512], AF.Sigmoid, reads=[pst], writes=[rq[di]], tag=f"sg{tch}{di}")
        build_wT(k.i_wkf, f"wk{tch}")
        for di in range(8):
            pst = (k.psC, k.psD)[di % 2]
            ffn_mm(xkb, di, tch, pst)
            p.ts("dve", k.ln_t1.ap, pst[:, 0:512], 0.0, None, ALU.max, ALU.bypass,
                 reads=[pst], writes=[k.ln_t1], tag=f"rel{tch}{di}")
            p.act(kq[di].ap, k.ln_t1.ap, AF.Square, reads=[k.ln_t1], writes=[kq[di]], tag=f"ksq{tch}{di}")
        build_wT(k.i_wvf, f"wv{tch}")
        for di in range(8):
            pst = (k.psC, k.psD)[di % 2]
            for cj in range(8):
                p.mm(pst[:, 0:512], wT.ap[:, (cj * 8 + di) * 128:(cj * 8 + di) * 128 + 128],
                     kq[cj].ap, cj == 0, cj == 7, reads=[wT, kq[cj]], writes=[pst], tag=f"vmm{tch}{di}")
            p.tt("dve", k.ln_t1.ap, pst[:, 0:512], rq[di].ap, ALU.mult,
                 reads=[pst, rq[di]], writes=[k.ln_t1], tag=f"fmul{tch}{di}")
            p.tt("dve", k.h[di][:, tch * 512:(tch + 1) * 512], k.h[di][:, tch * 512:(tch + 1) * 512],
                 k.ln_t1.ap, ALU.add, reads=[k.h[di], k.ln_t1], writes=[k.h[di]], tag=f"fadd{tch}{di}")
    for ci in range(4):
        a = ci * 512
        emit_ln(k, f"ln5c{ci}",
                lambda c, aa, w, a=a: k.h[c][:, a + aa: a + aa + w], lambda c: k.h[c],
                [(lambda c, aa, w, a=a: k.h[c][:, a + aa: a + aa + w], lambda c: k.h[c])],
                None, 1e-5, width=512)


def build_program(tap=None, debug=False, upto=None):
    nc = bass.Bass("TRN2", target_bir_lowering=False)
    k = Kern(nc, tap)
    emit_setup(k)
    emit_input(k)
    if tap:
        tap("input", k)
    stages = [("mamba", emit_mamba), ("tc", emit_tc), ("wlam", emit_wlam),
              ("moe", emit_moe), ("outer", emit_outer), ("ffn", emit_ffn)]
    for name, fn in stages:
        if upto is not None and name not in upto:
            continue
        fn(k)
        if tap:
            tap(name, k)
    emit_output_stub(k)
    k.p.finalize(debug=debug)
    return nc, k


def emit_output_stub(k):
    p = k.p
    otm = [k.sub(f"otm{i}", i, 2048, 1024, F32) for i in range(2)]
    for i in range(16):
        ps = (k.psA, k.psB)[i % 2]
        ob = otm[i % 2]
        for c in range(8):
            p.tr(ps[:, 128 * c:128 * (c + 1)], k.h[c][:, 128 * i:128 * (i + 1)], k.ident.ap,
                 reads=[k.h[c], k.ident], writes=[ps], tag=f"otr{i}_{c}")
        p.copy("dve", ob.ap, ps[:, 0:1024], reads=[ps], writes=[ob], tag=f"oev{i}")
        p.dma("sp", lambda e, ob=ob, i=i: e.dma_start(out=k.out.ap[128 * i:128 * (i + 1), :], in_=ob.ap),
              reads=[ob], writes=[k.out], tag=f"ost{i}")


def pack_cols(I):
    cols = np.zeros((128, COLS_W), np.float32)
    for i in range(4):
        for kk in range(8):
            cols[:, LNG + 8 * i + kk] = I["L0_attn_ln_g"][i][128 * kk:128 * (kk + 1)]
            cols[:, LNB + 8 * i + kk] = I["L0_attn_ln_b"][i][128 * kk:128 * (kk + 1)]
    for kk in range(16):
        cols[:, ALOGC + kk] = I["L0_attn_m_alog"][128 * kk:128 * (kk + 1)]
        cols[:, WDTC + kk] = I["L0_attn_m_wdt"][128 * kk:128 * (kk + 1)]
        cols[:, DTBC + kk] = I["L0_attn_m_dtb"][128 * kk:128 * (kk + 1)]
        cols[:, DSKC + kk] = I["L0_attn_m_dskip"][128 * kk:128 * (kk + 1)]
        for i in range(4):
            cols[:, CONVC + i * 16 + kk] = I["L0_attn_m_conv"][i][128 * kk:128 * (kk + 1)]
    for kk in range(8):
        cols[:, TCBC + kk] = I["L0_attn_tc_b"][128 * kk:128 * (kk + 1)]
        cols[:, MURC + kk] = I["L0_ffn_mu_r"][128 * kk:128 * (kk + 1)]
        cols[:, MUKC + kk] = I["L0_ffn_mu_k"][128 * kk:128 * (kk + 1)]
        for i in range(4):
            cols[:, WCVC + i * 8 + kk] = I["L0_attn_wlam_conv"][i][128 * kk:128 * (kk + 1)]
    cols[0:4, RBC] = I["L0_attn_router_b"]
    cols[:, THC] = float(I["L0_attn_tc_theta"])
    for e in range(8):
        for kf in range(32):
            cols[:, MB1C + e * 32 + kf] = I["L0_attn_moe_b1"][e][128 * kf:128 * (kf + 1)]
        for kk in range(8):
            cols[:, MB2C + e * 8 + kk] = I["L0_attn_moe_b2"][e][128 * kk:128 * (kk + 1)]
    return cols


def make_inmaps(inputs):
    I = {kk: np.asarray(v) for kk, v in inputs.items()}
    x = np.ascontiguousarray(I["L0_attn_hidden_states"][0])
    cols_base = pack_cols(I)
    tri = np.asarray(np.triu(np.ones((CK, CK))), np.float32)
    ident = np.eye(128, dtype=np.float32)
    shared = {
        "m_win": I["L0_attn_m_win"], "m_wbc": I["L0_attn_m_wbc"], "m_wout": I["L0_attn_m_wout"],
        "tc_w": I["L0_attn_tc_w"], "w_q": I["L0_attn_w_q"], "w_k": I["L0_attn_w_k"],
        "w_v": I["L0_attn_w_v"], "w_o": I["L0_attn_w_o"], "moe_wg": I["L0_attn_moe_wg"],
        "moe_w1": I["L0_attn_moe_w1"].reshape(E * D, FF),
        "moe_w2": I["L0_attn_moe_w2"].reshape(E * FF, D),
        "ffn_wr": I["L0_ffn_W_r"], "ffn_wk": I["L0_ffn_W_k"], "ffn_wv": I["L0_ffn_W_v"],
        "router_w": I["L0_attn_router_w"], "identity": ident, "tri": tri,
        "moe_b2r": I["L0_attn_moe_b2"],
        "ohot": np.repeat(np.eye(8, dtype=np.float32), 128, axis=1).reshape(8, 8, 128).transpose(0, 1, 2).reshape(8, 1024) if False else np.kron(np.eye(8, dtype=np.float32), np.ones((1, 128), np.float32)),
    }
    shared = {kk: np.ascontiguousarray(np.asarray(v, np.float32)) for kk, v in shared.items()}
    in_maps = []
    for r in range(NCORES):
        t0 = r * T
        xp = np.zeros((T + 3, D), np.float32)
        xp[3:] = x[t0:t0 + T]
        if r > 0:
            xp[0:3] = x[t0 - 3:t0]
        halo = np.zeros((D, 4), np.float32)
        halo[:, 0:3] = xp[0:3].T
        cols = cols_base.copy()
        for j in range(8):
            cols[:, SELR + j] = 1.0 if j == r else 0.0
            cols[:, SELP + j] = 1.0 if j == r - 1 else 0.0
            cols[:, SELN + j] = 1.0 if j == r + 1 else 0.0
        m = dict(shared)
        m["x_pad"] = xp
        m["x_halo_fm"] = np.ascontiguousarray(halo)
        m["cols"] = np.ascontiguousarray(cols)
        in_maps.append(m)
    return in_maps


def kernel(**inputs):
    nc, _ = build_program()
    in_maps = make_inmaps(inputs)
    res = run_bass_kernel_spmd(nc, in_maps, core_ids=list(range(NCORES)))
    return np.concatenate([r["y"] for r in res.results], axis=0)[None]
